# version 15
# baseline (speedup 1.0000x reference)
import numpy as np
from contextlib import ExitStack

import concourse.bass as bass
import concourse.mybir as mybir
from concourse.bass_utils import run_bass_kernel_spmd

F32 = mybir.dt.float32
BF16 = mybir.dt.bfloat16
AF = mybir.ActivationFunctionType
ALU = mybir.AluOpType
AX = mybir.AxisListType

N_CORES = 8
SEQ = 4096
D = 1024
H = 16
KD = 64
E = 8
DE = 2816
NCH = 8
CH = 512
NF = 4
FCH = 1024
NG = 8
NJ = 22
NM = SEQ // 128
SKEW = 1
HSLABS = [(0, 6), (6, 5)]
EPS_LN = 1e-5
EPS_RMS = 1e-6

ENGINES = ("sync", "gpsimd", "tensor", "scalar", "vector")


class Stream:
    __slots__ = ("sem", "n", "step", "name")

    def __init__(self, sem, step=1, name=""):
        self.sem = sem
        self.n = 0
        self.step = step
        self.name = name


class Buf:
    __slots__ = ("r", "w")

    def __init__(self):
        self.r = {}
        self.w = {}

    def rd(self):
        return list(self.w.items())

    def wr(self):
        d = dict(self.r)
        for s, t in self.w.items():
            d[s] = max(d.get(s, 0), t)
        return list(d.items())

    def note_r(self, *streams):
        for s in streams:
            self.r[s] = s.n

    def note_w(self, *streams):
        self.w = {s: s.n for s in streams}
        self.r = {}


class Sched:
    def __init__(self):
        self.steps = {e: [] for e in ENGINES}
        self.waited = {}

    def do(self, eng, fn=None, wait=(), inc=()):
        wl = []
        for w in wait:
            s, v = w if isinstance(w, tuple) else (w, w.n)
            assert v <= s.n, f"wait on future token {s.name}: {v} > {s.n}"
            if v <= 0:
                continue
            k = (eng, id(s))
            if self.waited.get(k, 0) >= v:
                continue
            self.waited[k] = v
            wl.append((s, v))
        il = []
        for s in inc:
            s.n += s.step
            il.append((s, s.step))
        assert len(il) <= 1
        self.steps[eng].append((wl, fn, il))

    def emit_into(self, block):
        def mk(name):
            steps = self.steps[name]

            def run(eng):
                for wl, fn, il in steps:
                    for s, v in wl:
                        eng.wait_ge(s.sem, v)
                    ins = fn(eng) if fn is not None else None
                    for s, k in il:
                        if ins is None:
                            ins = eng.nop()
                        ins.then_inc(s.sem, k)
            return run
        block.sync(mk("sync"))
        block.gpsimd(mk("gpsimd"))
        block.tensor(mk("tensor"))
        block.scalar(mk("scalar"))
        block.vector(mk("vector"))


def build_nc(dbg=(), phases=4, p1s=('ln','tr','q','k','v'),
             p3s=('heads','proj','projmm','projcp','projsq','projst',
                  'res','ln1f','rms','gate'), p3n=NCH):
    nc = bass.Bass("TRN2", target_bir_lowering=False)

    x_in = nc.dram_tensor("x", [SEQ, D], F32, kind="ExternalInput").ap()
    wq_in = nc.dram_tensor("wq", [128, NG * 1024], F32, kind="ExternalInput").ap()
    wk_in = nc.dram_tensor("wk", [128, NG * KD], F32, kind="ExternalInput").ap()
    wv_in = nc.dram_tensor("wv", [128, NG * KD], F32, kind="ExternalInput").ap()
    wo_in = nc.dram_tensor("wo", [128, NG * 1024], F32, kind="ExternalInput").ap()
    gw_in = nc.dram_tensor("gw", [128, NG * E], F32, kind="ExternalInput").ap()
    rms_in = nc.dram_tensor("rms", [128, NG], F32, kind="ExternalInput").ap()
    w1_in = nc.dram_tensor("w1t", [E, 128, NG * DE], F32,
                           kind="ExternalInput").ap()
    w2_in = nc.dram_tensor("w2t", [E, 128, NG * DE], F32,
                           kind="ExternalInput").ap()
    w3_in = nc.dram_tensor("w3t", [E, 128, 22 * 1024], F32,
                           kind="ExternalInput").ap()
    out = nc.dram_tensor("out", [SEQ, D], F32, kind="ExternalOutput").ap()

    x2s = nc.dram_tensor("x2s", [NG, 128, SEQ], F32, kind="Internal").ap()
    zs = nc.dram_tensor("zs", [NG, 128, SEQ], F32, kind="Internal").ap()
    xts = nc.dram_tensor("xts", [NG, 128, SEQ], F32, kind="Internal").ap()
    ys = nc.dram_tensor("ys", [NG, 128, SEQ], BF16, kind="Internal").ap()
    wTe_s = nc.dram_tensor("wte_s", [E, SEQ], F32, kind="Internal").ap()


    dbg_aps = {}

    def dbg_tensor(name, shape, dtype=F32):
        dbg_aps[name] = nc.dram_tensor("dbg_" + name, shape, dtype,
                                       kind="ExternalOutput").ap()
        return dbg_aps[name]

    import ml_dtypes
    identB_h = nc.inline_tensor(np.eye(128, dtype=ml_dtypes.bfloat16),
                                name="identB_c").ap()
    identF_h = nc.inline_tensor(np.eye(128, dtype=np.float32), name="identF_c").ap()

    ctx = ExitStack()
    with ctx:
        EC = ctx.enter_context

        T = Stream(EC(nc.semaphore("sT")), name="T")
        A = Stream(EC(nc.semaphore("sA")), name="A")
        V = Stream(EC(nc.semaphore("sV")), name="V")
        G = Stream(EC(nc.semaphore("sG")), name="G")
        DS = Stream(EC(nc.semaphore("sDS")), 16, name="DS")
        DG = Stream(EC(nc.semaphore("sDG")), 16, name="DG")
        STO = Stream(EC(nc.semaphore("sSTO")), 16, name="STO")

        def dstream(nm):
            return Stream(EC(nc.semaphore("s" + nm)), 16, name=nm)

        CST = dstream("CST")
        WAT = dstream("WAT")
        WOT = dstream("WOT")
        INIT = dstream("INIT")
        XB = [dstream(f"XB{i}") for i in range(2)]
        XTS = dstream("XTS")
        XTB = [dstream(f"XTB{i}") for i in range(2)]
        YSS = dstream("YSS")
        YB = dstream("YB")
        ACC = dstream("ACC")
        XF = [dstream(f"XF{i}") for i in range(2)]
        WR = dstream("WR")
        WQ1 = [dstream(f"WQ1{i}") for i in range(2)]
        WQ2 = [dstream(f"WQ2{i}") for i in range(2)]
        W3H = dstream("W3H")
        OB = [dstream(f"OB{i}") for i in range(2)]
        OALLB = [dstream(f"OALLB{i}") for i in range(2)]
        STG = dstream("STG")
        ALLS = (T, A, V, G, DS, DG, CST, WAT, WOT, INIT, XB[0], XB[1], XTS,
                XTB[0], XTB[1], YSS, YB, ACC, XF[0], XF[1], WR, WQ1[0],
                WQ1[1], WQ2[0], WQ2[1], W3H, OB[0], OB[1], OALLB[0],
                OALLB[1], STG)

        S = Sched()

        def barrier(extra=()):
            toks = [(s, s.n) for s in ALLS] + list(extra)
            for e in ENGINES:
                S.do(e, wait=toks)

        identB = EC(nc.sbuf_tensor("identB", [128, 128], BF16))
        identF = EC(nc.sbuf_tensor("identF", [128, 128], F32))
        ones_p1 = EC(nc.sbuf_tensor("ones_p1", [128, 1], F32))
        ones_1p = EC(nc.sbuf_tensor("ones_1p", [1, 128], F32))
        onesB = EC(nc.sbuf_tensor("onesB", [128, 64], F32))
        cbias = EC(nc.sbuf_tensor("cbias", [128, 1], F32))
        rms_sb = EC(nc.sbuf_tensor("rms_sb", [128, NG], F32))
        gw2 = EC(nc.sbuf_tensor("gw2", [128, NG * E], F32))
        srow = EC(nc.sbuf_tensor("srow", [1, 16], F32))
        cst = EC(nc.sbuf_tensor("cst", [128, 2], F32))
        stage8 = EC(nc.sbuf_tensor("stage8", [1, E * 128], F32))
        gwT = EC(nc.sbuf_tensor("gwT", [128, NG * E], F32))

        S.do("gpsimd", lambda g: g.memset(cst[:, 0:1], EPS_LN), inc=[G])
        S.do("gpsimd", lambda g: g.memset(cst[:, 1:2], EPS_RMS), inc=[G])
        S.do("gpsimd", lambda g: g.memset(ones_p1[:, :], 1.0), inc=[G])
        S.do("gpsimd", lambda g: g.memset(ones_1p[:, :], 1.0), inc=[G])
        S.do("gpsimd", lambda g: g.memset(onesB[:, :], 1.0), inc=[G])
        tok_ones = G.n
        S.do("sync", lambda s: s.dma_start(out=identB[:, :], in_=identB_h), inc=[INIT])
        S.do("sync", lambda s: s.dma_start(out=identF[:, :], in_=identF_h), inc=[INIT])
        S.do("sync", lambda s: s.dma_start(out=rms_sb[:, :], in_=rms_in), inc=[INIT])
        S.do("sync", lambda s: s.dma_start(out=gwT[:, :], in_=gw_in), inc=[INIT])
        tok_small = INIT.n
        for g in range(NG):
            S.do("vector", lambda v, g=g: v.tensor_scalar_mul(
                gw2[:, g * E:(g + 1) * E], gwT[:, g * E:(g + 1) * E],
                rms_sb[:, g:g + 1]),
                wait=[(INIT, tok_small)], inc=[V])
        tok_gw2 = V.n
        with ExitStack() as pctx:
            PE = pctx.enter_context
            wo_sb = PE(nc.sbuf_tensor("wo_sb", [128, NG * 1024], BF16))
            QT = PE(nc.sbuf_tensor("QT", [128, NG * SEQ], BF16))
            KT = PE(nc.sbuf_tensor("KT", [128, SEQ], BF16))
            vaug = PE(nc.sbuf_tensor("vaug", [128, NM * 65], BF16))

            S.do("gpsimd", lambda g: g.memset(vaug[:, :], 1.0), inc=[G])
            tok_vinit = G.n

            with ExitStack() as p1c:
                P1 = p1c.enter_context
                wq_sb = P1(nc.sbuf_tensor("wq_sb", [128, NG * 1024], BF16))
                wk_sb = P1(nc.sbuf_tensor("wk_sb", [128, NG * KD], BF16))
                wv_sb = P1(nc.sbuf_tensor("wv_sb", [128, NG * KD], BF16))
                S.do("gpsimd", lambda g: g.dma_start(out=wq_sb[:, :], in_=wq_in),
                     inc=[WAT])
                S.do("gpsimd", lambda g: g.dma_start(out=wk_sb[:, :], in_=wk_in),
                     inc=[WAT])
                S.do("gpsimd", lambda g: g.dma_start(out=wv_sb[:, :], in_=wv_in),
                     inc=[WAT])
                tok_attnw = WAT.n
                S.do("gpsimd", lambda g: g.dma_start(out=wo_sb[:, :], in_=wo_in),
                     inc=[WOT])
                tok_wo = WOT.n
                xbuf = [P1(nc.sbuf_tensor(f"xbuf{i}", [128, 4 * 1024], F32))
                        for i in range(2)]
                ln1 = [P1(nc.sbuf_tensor(f"ln1_{i}", [128, 1024], BF16))
                       for i in range(2)]
                ln1T = [P1(nc.sbuf_tensor(f"ln1T{i}", [128, NG * 512], BF16))
                        for i in range(2)]
                mv = P1(nc.sbuf_tensor("mv", [128, 8], F32))
                sqx = P1(nc.sbuf_tensor("sqx", [128, 1024], F32))
                xtbuf = P1(nc.sbuf_tensor("xtbuf", [128, 512], F32))
                sqb = P1(nc.sbuf_tensor("sqb", [128, 512], F32))
                ksq = P1(nc.sbuf_tensor("ksq", [64, 512], F32))

                pq = [P1(nc.psum_tensor(f"pq{i}", [128, 512], F32))
                      for i in range(2)]
                pk = P1(nc.psum_tensor("pk", [128, 512], F32))
                pv = P1(nc.psum_tensor("pv", [128, 64], F32))
                ptr = [P1(nc.psum_tensor(f"ptr{i}", [128, 128], BF16))
                      for i in range(2)]
                pst = P1(nc.psum_tensor("pst", [1, 512], F32))
                pst2 = P1(nc.psum_tensor("pst2", [1, 512], F32))

                b_ln1 = [Buf(), Buf()]
                b_xbuf = [Buf(), Buf()]
                b_ln1T = [Buf(), Buf()]
                b_srow = Buf()
                b_ptr = [Buf() for _ in range(2)]
                b_pq = [Buf(), Buf()]
                b_sq = Buf()
                b_pst = Buf()
                b_pst2 = Buf()
                b_pk = Buf()
                b_ksq = Buf()
                b_pv = Buf()

                for c in range(NCH if phases >= 1 else 0):
                    cb = c % 2
                    pre_ln1T = b_ln1T[cb].wr()
                    S.do("sync", lambda s, c=c, cb=cb: s.dma_start(
                        out=xbuf[cb].ap().rearrange("p (t d) -> p t d", t=4),
                        in_=x_in[c * CH:(c + 1) * CH, :]
                        .rearrange("(t p) d -> p t d", p=128)),
                        wait=[(INIT, tok_small)] + b_xbuf[cb].wr(),
                        inc=[XB[cb]])
                    b_xbuf[cb].note_w(XB[cb])
                    tok_x = XB[cb].n
                    for t in range(4):
                        lb = t % 2
                        S.do("vector", lambda v, t=t, cb=cb: v.reduce_sum(
                            mv[:, 6:7], xbuf[cb][:, t * 1024:(t + 1) * 1024],
                            axis=AX.X),
                            wait=[(XB[cb], tok_x), (V, V.n)], inc=[V])
                        S.do("vector", lambda v, t=t, cb=cb: v.scalar_tensor_tensor(
                            sqx[:, :], xbuf[cb][:, t * 1024:(t + 1) * 1024], 1.0,
                            xbuf[cb][:, t * 1024:(t + 1) * 1024], ALU.mult,
                            ALU.mult, accum_out=mv[:, 7:8]),
                            wait=[(V, V.n)], inc=[V])
                        S.do("vector", lambda v: v.tensor_scalar_mul(
                            mv[:, 0:1], mv[:, 6:7], 1.0 / D),
                            wait=[(V, V.n)], inc=[V])
                        S.do("vector", lambda v: v.tensor_scalar_mul(
                            mv[:, 5:6], mv[:, 7:8], 1.0 / D),
                            wait=[(V, V.n)], inc=[V])
                        S.do("vector", lambda v: v.scalar_tensor_tensor(
                            mv[:, 4:5], mv[:, 0:1], 1.0, mv[:, 0:1],
                            ALU.mult, ALU.mult),
                            wait=[(V, V.n)], inc=[V])
                        S.do("vector", lambda v: v.tensor_sub(
                            mv[:, 1:2], mv[:, 5:6], mv[:, 4:5]),
                            wait=[(V, V.n)], inc=[V])
                        S.do("scalar", lambda s: s.activation(
                            mv[:, 4:5], mv[:, 1:2], AF.Ln, bias=cst[:, 0:1]),
                            wait=[(V, V.n), (A, A.n), (G, tok_ones)], inc=[A])
                        S.do("scalar", lambda s: s.activation(
                            mv[:, 2:3], mv[:, 4:5], AF.Exp, scale=-0.5),
                            wait=[(A, A.n)], inc=[A])
                        S.do("vector", lambda v: v.scalar_tensor_tensor(
                            mv[:, 3:4], mv[:, 0:1], -1.0, mv[:, 2:3],
                            ALU.mult, ALU.mult),
                            wait=[(A, A.n), (V, V.n)], inc=[V])
                        S.do("vector", lambda v, t=t, lb=lb, cb=cb: v.tensor_scalar(
                            ln1[lb][:, :], xbuf[cb][:, t * 1024:(t + 1) * 1024],
                            mv[:, 2:3], mv[:, 3:4], ALU.mult, ALU.add),
                            wait=[(V, V.n)] + b_ln1[lb].wr(), inc=[V])
                        b_ln1[lb].note_w(V)
                        b_xbuf[cb].note_r(V)
                        for gg in range(NG if 'tr' in p1s else 0):
                            pb = gg % 2
                            S.do("tensor", lambda t_, t=t, gg=gg, lb=lb, pb=pb:
                                 t_.transpose(
                                     ptr[pb][:, :],
                                     ln1[lb][:, gg * 128:(gg + 1) * 128],
                                     identB[:, :]),
                                 wait=b_ln1[lb].rd() + b_ptr[pb].wr(), inc=[T])
                            b_ptr[pb].note_w(T)
                            b_ln1[lb].note_r(T)
                            S.do("vector", lambda v, t=t, gg=gg, pb=pb: v.tensor_copy(
                                out=ln1T[cb][:, gg * 512 + t * 128:
                                             gg * 512 + (t + 1) * 128],
                                in_=ptr[pb][:, :]),
                                wait=b_ptr[pb].rd() + pre_ln1T, inc=[V])
                            b_ptr[pb].note_r(V)
                            b_ln1T[cb].note_w(V)
                    tok_ln1T = V.n

                    for hp in range(NG if 'q' in p1s else 0):
                        pqb = hp % 2
                        for gg in range(NG):
                            S.do("tensor", lambda t_, hp=hp, gg=gg, pqb=pqb: t_.matmul(
                                pq[pqb][:, :],
                                wq_sb[:, gg * 1024 + hp * 128:
                                      gg * 1024 + (hp + 1) * 128],
                                ln1T[cb][:, gg * 512:(gg + 1) * 512],
                                start=(gg == 0), stop=(gg == NG - 1)),
                                wait=[(V, tok_ln1T), (WAT, tok_attnw)]
                                + (b_pq[pqb].wr() if gg == 0 else []),
                                inc=[T] if gg == NG - 1 else [])
                        b_pq[pqb].note_w(T)
                        b_ln1T[cb].note_r(T)
                        S.do("scalar", lambda s, hp=hp, c=c, pqb=pqb: s.copy(
                            out=QT[:, hp * SEQ + c * CH: hp * SEQ + (c + 1) * CH],
                            in_=pq[pqb][:, :]),
                            wait=b_pq[pqb].rd(), inc=[A])
                        S.do("scalar", lambda s, pqb=pqb: s.square(
                            sqb[:, :], pq[pqb][:, :]),
                            wait=b_sq.wr(), inc=[A])
                        b_pq[pqb].note_r(A)
                        b_sq.note_w(A)
                        S.do("tensor", lambda t_, hp=hp: t_.matmul(
                            pst[:, :], ones_p1[:, :], sqb[:, :],
                            start=(hp == 0), stop=(hp == NG - 1)),
                            wait=b_sq.rd() + (b_pst.wr() if hp == 0 else []),
                            inc=[T])
                        b_sq.note_r(T)
                        b_pst.note_w(T)
                    if 'q' not in p1s:
                        continue
                    S.do("vector", lambda v: v.reduce_max(
                        srow[0:1, 6:7], pst[:, :], axis=AX.X),
                        wait=b_pst.rd() + [(V, V.n)], inc=[V])
                    b_pst.note_r(V)
                    if c == 0:
                        S.do("vector", lambda v: v.tensor_copy(
                            out=srow[0:1, 0:1], in_=srow[0:1, 6:7]),
                            wait=[(V, V.n)], inc=[V])
                    else:
                        S.do("vector", lambda v: v.tensor_max(
                            srow[0:1, 0:1], srow[0:1, 0:1], srow[0:1, 6:7]),
                            wait=[(V, V.n)], inc=[V])

                    for gg in range(NG if 'k' in p1s else 0):
                        S.do("tensor", lambda t_, gg=gg: t_.matmul(
                            pk[0:64, :], wk_sb[:, gg * KD:(gg + 1) * KD],
                            ln1T[cb][:, gg * 512:(gg + 1) * 512],
                            start=(gg == 0), stop=(gg == NG - 1),
                            tile_position=(0, 0)),
                            wait=[(V, tok_ln1T)] + (b_pk.wr() if gg == 0 else []),
                            inc=[T] if gg == NG - 1 else [])
                    for gg in range(NG if 'k' in p1s else 0):
                        S.do("tensor", lambda t_, gg=gg: t_.matmul(
                            pk[64:128, :], wk_sb[:, gg * KD:(gg + 1) * KD],
                            ln1T[cb][:, gg * 512:(gg + 1) * 512],
                            start=(gg == 0), stop=(gg == NG - 1),
                            tile_position=(0, 64)),
                            inc=[T] if gg == NG - 1 else [])
                    b_pk.note_w(T)
                    b_ln1T[cb].note_r(T)
                    if 'k' not in p1s:
                        continue
                    S.do("scalar", lambda s, c=c: s.copy(
                        out=KT[:, c * CH:(c + 1) * CH], in_=pk[:, :]),
                        wait=b_pk.rd(), inc=[A])
                    S.do("scalar", lambda s: s.square(ksq[:, :], pk[0:64, :]),
                         wait=b_ksq.wr() + [(A, A.n)], inc=[A])
                    b_pk.note_r(A)
                    b_ksq.note_w(A)
                    S.do("tensor", lambda t_: t_.matmul(
                        pst2[:, :], ones_p1[0:64, :], ksq[:, :],
                        start=True, stop=True),
                        wait=b_ksq.rd() + b_pst2.wr(), inc=[T])
                    b_ksq.note_r(T)
                    b_pst2.note_w(T)
                    S.do("vector", lambda v: v.reduce_max(
                        srow[0:1, 7:8], pst2[:, :], axis=AX.X),
                        wait=b_pst2.rd() + [(V, V.n)], inc=[V])
                    b_pst2.note_r(V)
                    if c == 0:
                        S.do("vector", lambda v: v.tensor_copy(
                            out=srow[0:1, 1:2], in_=srow[0:1, 7:8]),
                            wait=[(V, V.n)], inc=[V])
                    else:
                        S.do("vector", lambda v: v.tensor_max(
                            srow[0:1, 1:2], srow[0:1, 1:2], srow[0:1, 7:8]),
                            wait=[(V, V.n)], inc=[V])

                    b_xtb = Buf()
                    for gg in range(NG):
                        pb = gg % 2
                        for t in range(4):
                            S.do("tensor", lambda t_, gg=gg, t=t, pb=pb, cb=cb:
                                 t_.transpose(
                                     pq[pb][:, t * 128:(t + 1) * 128],
                                     xbuf[cb][:, t * 1024 + gg * 128:
                                              t * 1024 + (gg + 1) * 128],
                                     identF[:, :]),
                                 wait=[(XB[cb], tok_x)]
                                 + (b_pq[pb].wr() if t == 0 else []),
                                 inc=[T] if t == 3 else [])
                        b_pq[pb].note_w(T)
                        b_xbuf[cb].note_r(T)
                        S.do("vector", lambda v, gg=gg, pb=pb: v.tensor_copy(
                            out=xtbuf[:, :], in_=pq[pb][:, :]),
                            wait=b_pq[pb].rd() + b_xtb.wr(), inc=[V])
                        b_pq[pb].note_r(V)
                        b_xtb.note_w(V)
                        S.do("sync", lambda s, gg=gg, c=c: s.dma_start(
                            out=xts[gg][:, c * CH:(c + 1) * CH],
                            in_=xtbuf[:, :]),
                            wait=b_xtb.rd() + [(V, V.n)], inc=[XTS])
                        b_xtb.note_r(XTS)

                    for mt in range(4 if 'v' in p1s else 0):
                        m_abs = c * 4 + mt
                        for gg in range(NG):
                            S.do("tensor", lambda t_, gg=gg, mt=mt: t_.matmul(
                                pv[:, :],
                                ln1T[cb][:, gg * 512 + mt * 128:
                                         gg * 512 + (mt + 1) * 128],
                                wv_sb[:, gg * KD:(gg + 1) * KD],
                                start=(gg == 0), stop=(gg == NG - 1)),
                                wait=[(V, tok_ln1T), (G, tok_vinit)]
                                + (b_pv.wr() if gg == 0 else []),
                                inc=[T] if gg == NG - 1 else [])
                        b_pv.note_w(T)
                        b_ln1T[cb].note_r(T)
                        S.do("scalar", lambda s, m_abs=m_abs: s.copy(
                            out=vaug[:, m_abs * 65: m_abs * 65 + 64], in_=pv[:, :]),
                            wait=b_pv.rd(), inc=[A])
                        b_pv.note_r(A)

                tok_cbias = A.n
                if phases >= 2:
                    barrier()
                    S.do("vector", lambda v: v.tensor_mul(
                        srow[0:1, 2:3], srow[0:1, 0:1], srow[0:1, 1:2]),
                        wait=[(V, V.n)], inc=[V])
                    S.do("scalar", lambda s: s.activation(
                        srow[0:1, 3:4], srow[0:1, 2:3], AF.Ln),
                        wait=[(V, V.n)], inc=[A])
                    S.do("scalar", lambda s: s.activation(
                        srow[0:1, 4:5], srow[0:1, 3:4], AF.Exp, scale=0.5),
                        wait=[(A, A.n)], inc=[A])
                    S.do("vector", lambda v: v.tensor_scalar_mul(
                        srow[0:1, 5:6], srow[0:1, 4:5], -0.125),
                        wait=[(A, A.n)], inc=[V])
                    S.do("tensor", lambda t_: t_.matmul(
                        pv[:, 0:1], ones_1p[:, :], srow[0:1, 5:6],
                        start=True, stop=True),
                        wait=[(V, V.n)] + b_pv.wr(), inc=[T])
                    b_pv.note_w(T)
                    S.do("scalar", lambda s: s.copy(out=cbias[:, :], in_=pv[:, 0:1]),
                         wait=b_pv.rd(), inc=[A])
                    b_pv.note_r(A)
                    tok_cbias = A.n

            if "qt" in dbg:
                ap = dbg_tensor("qt", [128, NG * SEQ], BF16)
                S.do("sync", lambda s, ap=ap: s.dma_start(out=ap, in_=QT[:, :]),
                     wait=[(A, A.n), (T, T.n), (V, V.n)], inc=[STO])
            if "kt" in dbg:
                ap = dbg_tensor("kt", [128, SEQ], BF16)
                S.do("sync", lambda s, ap=ap: s.dma_start(out=ap, in_=KT[:, :]),
                     wait=[(A, A.n), (T, T.n), (V, V.n)], inc=[STO])
            if "vaug" in dbg:
                ap = dbg_tensor("vaug", [128, NM * 65], BF16)
                S.do("sync", lambda s, ap=ap: s.dma_start(out=ap, in_=vaug[:, :]),
                     wait=[(A, A.n), (T, T.n), (V, V.n)], inc=[STO])
            if "cbias" in dbg:
                ap = dbg_tensor("cbias", [128, 1])
                S.do("sync", lambda s, ap=ap: s.dma_start(out=ap, in_=cbias[:, :]),
                     wait=[(A, A.n)], inc=[STO])

            with ExitStack() as p3c:
                P3 = p3c.enter_context
                Epair = [P3(nc.sbuf_tensor(f"Ep{i}", [128, 1024], BF16))
                         for i in range(3)]
                o_all = [P3(nc.sbuf_tensor(f"o_all{i}", [128, NG * 512], BF16))
                         for i in range(2)]
                tmpo = P3(nc.sbuf_tensor("tmpo", [64, 512], F32))
                tmpoB = P3(nc.sbuf_tensor("tmpoB", [64, 512], F32))
                otmpB = P3(nc.sbuf_tensor("otmpB", [64, 512], BF16))
                recr = P3(nc.sbuf_tensor("recr", [128, 512], F32))
                xtb = [P3(nc.sbuf_tensor(f"xtb{i}", [128, 512], F32))
                       for i in range(2)]
                attnT = P3(nc.sbuf_tensor("attnT", [128, NG * 512], F32))
                x2T = P3(nc.sbuf_tensor("x2T", [128, NG * 512], F32))
                zT = P3(nc.sbuf_tensor("zT", [128, NG * 512], F32))
                yTt = [P3(nc.sbuf_tensor(f"yTt{i}", [128, 512], BF16))
                       for i in range(2)]
                sAB = P3(nc.sbuf_tensor("sAB", [128, 1024], F32))
                rowA = P3(nc.sbuf_tensor("rowA", [1, 512], F32))
                rowB = P3(nc.sbuf_tensor("rowB", [1, 512], F32))
                rowC = P3(nc.sbuf_tensor("rowC", [1, 512], F32))
                sq3 = P3(nc.sbuf_tensor("sq3", [128, 512], F32))
                tv1 = P3(nc.sbuf_tensor("tv1", [128, 512], F32))
                lg8 = P3(nc.sbuf_tensor("lg8", [8, 512], F32))
                lgt = P3(nc.sbuf_tensor("lgt", [128, 4 * 8], F32))
                Ct = P3(nc.sbuf_tensor("Ct", [128, 4], F32))
                gt = P3(nc.sbuf_tensor("gt", [128, 10 * 8], F32))

                sp = [P3(nc.psum_tensor(f"sp{i}", [128, 1024], F32))
                      for i in range(2)]
                poA = P3(nc.psum_tensor("poA", [65, 512], F32))
                poB = P3(nc.psum_tensor("poB", [65, 512], F32))
                pr = P3(nc.psum_tensor("pr", [64, 512], F32))
                ptl = P3(nc.psum_tensor("ptl", [128, 512], F32))

                b_sp = [Buf(), Buf()]
                b_Ep = [Buf(), Buf(), Buf()]
                b_po = Buf()
                b_pr = Buf()
                b_ptl = Buf()
                b_tmpo = Buf()
                b_tmpoB = Buf()
                b_otmpB = Buf()
                b_oall = [Buf(), Buf()]
                b_oall_d = [0, 0]
                b_attnT = Buf()
                b_x2T = Buf()
                b_zT = Buf()
                b_sq3 = Buf()
                b_rowA = Buf()
                b_rowB = Buf()
                b_rowC = Buf()
                b_sAB = Buf()
                b_tv1 = Buf()
                b_xtb3 = [Buf(), Buf()]
                b_recr = Buf()
                b_lg8 = Buf()
                b_lgt = Buf()
                b_Ct = Buf()
                b_yTt = [Buf(), Buf()]
                b_gt = Buf()
                b_stage_dma = [0]

                def sched_sim(c, hp, m):
                    par = m % 2
                    ep = (hp * NM + m) % 3
                    S.do("tensor", lambda t_, m=m, hp=hp, c=c, par=par:
                         t_.matmul(
                             sp[par][:, 0:512],
                             KT[0:64, m * 128:(m + 1) * 128],
                             QT[0:64, hp * SEQ + c * CH:
                                hp * SEQ + (c + 1) * CH],
                             start=True, stop=True, tile_position=(0, 0)),
                         wait=b_sp[par].wr())
                    S.do("tensor", lambda t_, m=m, hp=hp, c=c, par=par:
                         t_.matmul(
                             sp[par][:, 512:1024],
                             KT[64:128, m * 128:(m + 1) * 128],
                             QT[64:128, hp * SEQ + c * CH:
                                hp * SEQ + (c + 1) * CH],
                             start=True, stop=True, tile_position=(64, 0)),
                         inc=[T])
                    b_sp[par].note_w(T)
                    S.do("scalar", lambda s, par=par, ep=ep: s.activation(
                        Epair[ep][:, 0:512], sp[par][:, 0:512], AF.Exp,
                        bias=cbias[:, 0:1], scale=0.125),
                        wait=b_sp[par].rd() + b_Ep[ep].wr(), inc=[A])
                    b_Ep[ep].note_w(A)
                    S.do("scalar", lambda s, par=par, ep=ep: s.activation(
                        Epair[ep][:, 512:1024], sp[par][:, 512:1024], AF.Exp,
                        bias=cbias[:, 0:1], scale=0.125), inc=[A])
                    b_sp[par].note_r(A)
                    b_Ep[ep].note_w(A)

                def sched_o(c, hp, m):
                    ep = (hp * NM + m) % 3
                    wa = b_Ep[ep].rd()
                    wa = [(s, v - 1) if s is A else (s, v) for s, v in wa]
                    S.do("tensor", lambda t_, m=m, ep=ep: t_.matmul(
                        poA[:, :], vaug[:, m * 65:(m + 1) * 65],
                        Epair[ep][:, 0:512],
                        start=(m == 0), stop=(m == NM - 1)),
                        wait=wa + (b_po.wr() if m == 0 else []))
                    S.do("tensor", lambda t_: t_.nop(),
                         wait=b_Ep[ep].rd())
                    S.do("tensor", lambda t_, m=m, ep=ep: t_.matmul(
                        poB[:, :], vaug[:, m * 65:(m + 1) * 65],
                        Epair[ep][:, 512:1024],
                        start=(m == 0), stop=(m == NM - 1)),
                        inc=[T])
                    b_Ep[ep].note_r(T)
                    b_po.note_w(T)

                def sched_norm(c, hp):
                    ob = c % 2
                    S.do("vector", lambda v: v.reciprocal(
                        recr[64:65, :], poA[64:65, :]),
                        wait=b_po.rd() + b_recr.wr(), inc=[V])
                    b_recr.note_w(V)
                    S.do("scalar", lambda s: s.copy(
                        out=tmpo[:, :], in_=poA[0:64, :]),
                        wait=b_po.rd() + b_tmpo.wr() + [(V, V.n)], inc=[A])
                    S.do("scalar", lambda s: s.copy(
                        out=tmpoB[:, :], in_=poB[0:64, :]),
                        wait=b_tmpoB.wr(), inc=[A])
                    b_po.note_w({}) if False else None
                    b_tmpo.note_w(A)
                    b_tmpoB.note_w(A)
                    S.do("tensor", lambda t_: t_.matmul(
                        pr[:, :], onesB[64:65, :], recr[64:65, :],
                        start=True, stop=True, tile_position=(64, 0)),
                        wait=b_recr.rd() + b_pr.wr() + [(G, tok_ones)],
                        inc=[T])
                    b_pr.note_w(T)
                    b_recr.note_r(T)
                    S.do("vector", lambda v, hp=hp, ob=ob: v.tensor_mul(
                        o_all[ob][0:64, hp * 512:(hp + 1) * 512],
                        tmpo[:, :], pr[:, :]),
                        wait=b_pr.rd() + b_tmpo.rd() + pre_oall[ob],
                        inc=[V])
                    b_pr.note_r(V)
                    b_tmpo.note_r(V)
                    b_oall[ob].note_w(V)
                    S.do("vector", lambda v: v.reciprocal(
                        recr[64:65, :], poB[64:65, :]),
                        wait=b_recr.wr() + [(V, V.n)], inc=[V])
                    b_recr.note_w(V)
                    b_po.note_r(A, V)
                    S.do("tensor", lambda t_: t_.matmul(
                        pr[:, :], onesB[64:65, :], recr[64:65, :],
                        start=True, stop=True, tile_position=(64, 0)),
                        wait=b_recr.rd() + b_pr.wr(), inc=[T])
                    b_recr.note_r(T)
                    b_pr.note_w(T)
                    S.do("vector", lambda v: v.tensor_mul(
                        otmpB[:, :], tmpoB[:, :], pr[:, :]),
                        wait=b_pr.rd() + b_tmpoB.rd() + b_otmpB.wr(),
                        inc=[V])
                    b_pr.note_r(V)
                    b_tmpoB.note_r(V)
                    b_otmpB.note_w(V)
                    S.do("sync", lambda s, hp=hp, ob=ob: s.dma_start(
                        out=o_all[ob][64:128, hp * 512:(hp + 1) * 512],
                        in_=otmpB[:, :]),
                        wait=b_otmpB.rd() + [(V, V.n)] + pre_oall[ob],
                        inc=[OALLB[ob]])
                    b_otmpB.note_r(OALLB[ob])
                    b_oall[ob].note_w(V, OALLB[ob])
                    b_oall_d[ob] = OALLB[ob].n

                def ln_rows(eps):
                    S.do("vector", lambda v: v.tensor_scalar_mul(
                        rowA[0:1, :], rowA[0:1, :], 1.0 / D),
                        wait=b_rowA.rd() + [(V, V.n)], inc=[V])
                    b_rowA.note_w(V)
                    S.do("vector", lambda v: v.tensor_scalar_mul(
                        rowB[0:1, :], rowB[0:1, :], 1.0 / D),
                        wait=b_rowB.rd() + [(V, V.n)], inc=[V])
                    b_rowB.note_w(V)
                    S.do("vector", lambda v: v.scalar_tensor_tensor(
                        rowC[0:1, :], rowA[0:1, :], 1.0, rowA[0:1, :],
                        ALU.mult, ALU.mult),
                        wait=b_rowA.rd() + b_rowC.wr(), inc=[V])
                    b_rowC.note_w(V)
                    S.do("vector", lambda v: v.tensor_sub(
                        rowB[0:1, :], rowB[0:1, :], rowC[0:1, :]),
                        wait=b_rowC.rd() + [(V, V.n)], inc=[V])
                    b_rowC.note_r(V)
                    b_rowB.note_w(V)
                    S.do("scalar", lambda s, eps=eps: s.activation(
                        rowC[0:1, :], rowB[0:1, :], AF.Ln, bias=eps),
                        wait=b_rowB.rd() + b_rowC.wr(), inc=[A])
                    b_rowB.note_r(A)
                    b_rowC.note_w(A)
                    S.do("scalar", lambda s: s.activation(
                        rowB[0:1, :], rowC[0:1, :], AF.Exp, scale=-0.5),
                        wait=b_rowC.rd() + b_rowB.wr() + [(A, A.n)], inc=[A])
                    b_rowC.note_r(A)
                    b_rowB.note_w(A)
                    S.do("vector", lambda v: v.scalar_tensor_tensor(
                        rowC[0:1, :], rowA[0:1, :], -1.0, rowB[0:1, :],
                        ALU.mult, ALU.mult),
                        wait=b_rowB.rd() + b_rowA.rd() + b_rowC.wr(), inc=[V])
                    b_rowA.note_r(V)
                    b_rowB.note_r(V)
                    b_rowC.note_w(V)

                def bcast_ab():
                    S.do("tensor", lambda t_: t_.matmul(
                        ptl[:, :], ones_1p[:, :], rowB[0:1, :],
                        start=True, stop=True),
                        wait=b_rowB.rd() + b_ptl.wr(), inc=[T])
                    b_rowB.note_r(T)
                    b_ptl.note_w(T)
                    S.do("scalar", lambda s: s.copy(out=sAB[:, 0:512],
                                                    in_=ptl[:, :]),
                         wait=b_ptl.rd() + b_sAB.wr(), inc=[A])
                    b_ptl.note_r(A)
                    b_sAB.note_w(A)
                    S.do("tensor", lambda t_: t_.matmul(
                        ptl[:, :], ones_1p[:, :], rowC[0:1, :],
                        start=True, stop=True),
                        wait=b_rowC.rd() + b_ptl.wr(), inc=[T])
                    b_rowC.note_r(T)
                    b_ptl.note_w(T)
                    S.do("scalar", lambda s: s.copy(out=sAB[:, 512:1024],
                                                    in_=ptl[:, :]),
                         wait=b_ptl.rd() + [(A, A.n)], inc=[A])
                    b_ptl.note_r(A)
                    b_sAB.note_w(A)

                def stats_pass(src_buf, b_src, dst_row, b_dst):
                    for i in range(NG):
                        S.do("tensor", lambda t_, i=i: t_.matmul(
                            ptl[0:1, :], ones_p1[:, :],
                            src_buf[:, i * 512:(i + 1) * 512],
                            start=(i == 0), stop=(i == NG - 1)),
                            wait=b_src.rd()
                            + (b_ptl.wr() if i == 0 else []),
                            inc=[T] if i == NG - 1 else [])
                    b_ptl.note_w(T)
                    b_src.note_r(T)
                    S.do("vector", lambda v: v.tensor_copy(
                        out=dst_row[0:1, :], in_=ptl[0:1, :]),
                        wait=b_ptl.rd() + b_dst.wr(), inc=[V])
                    b_ptl.note_r(V)
                    b_dst.note_w(V)

                def sq_stats_pass(src_buf, b_src, dst_row, b_dst):
                    for i in range(NG):
                        S.do("scalar", lambda s, i=i: s.square(
                            sq3[:, :], src_buf[:, i * 512:(i + 1) * 512]),
                            wait=b_src.rd() + b_sq3.wr(), inc=[A])
                        b_sq3.note_w(A)
                        b_src.note_r(A)
                        S.do("tensor", lambda t_, i=i: t_.matmul(
                            ptl[0:1, :], ones_p1[:, :], sq3[:, :],
                            start=(i == 0), stop=(i == NG - 1)),
                            wait=b_sq3.rd()
                            + (b_ptl.wr() if i == 0 else []),
                            inc=[T])
                        b_sq3.note_r(T)
                    b_ptl.note_w(T)
                    S.do("vector", lambda v: v.tensor_copy(
                        out=dst_row[0:1, :], in_=ptl[0:1, :]),
                        wait=b_ptl.rd() + b_dst.wr(), inc=[V])
                    b_ptl.note_r(V)
                    b_dst.note_w(V)

                def tail_gen(c):
                    ob = c % 2

                    tok_xts = (c + 1) * NG * 16

                    def xt_load(i, c=c, tok_xts=0):
                        ib = i % 2
                        S.do("sync", lambda s, i=i, ib=ib: s.dma_start(
                            out=xtb[ib][:, :],
                            in_=xts[i][:, c * CH:(c + 1) * CH]),
                            wait=b_xtb3[ib].wr() + [(XTS, min(tok_xts, XTS.n))],
                            inc=[XTB[ib]])
                        b_xtb3[ib].note_w(XTB[ib])
                    tok_oall = b_oall[ob].w.get(V, 0)
                    tok_oall_d = b_oall_d[ob]
                    pre_attnT = b_attnT.wr()
                    for i in range(NG):
                        for g2 in range(NG):
                            S.do("tensor", lambda t_, i=i, g2=g2, ob=ob:
                                 t_.matmul(
                                     ptl[:, :],
                                     wo_sb[:, g2 * 1024 + i * 128:
                                           g2 * 1024 + (i + 1) * 128],
                                     o_all[ob][:, g2 * 512:(g2 + 1) * 512],
                                     start=(g2 == 0), stop=(g2 == NG - 1)),
                                 wait=[(V, tok_oall), (OALLB[ob], tok_oall_d),
                                       (WOT, tok_wo)]
                                 + (b_ptl.wr() if g2 == 0 else []),
                                 inc=[T] if g2 == NG - 1 else [])
                        b_ptl.note_w(T)
                        b_oall[ob].note_r(T)
                        S.do("vector", lambda v, i=i: v.tensor_copy(
                            out=attnT[:, i * 512:(i + 1) * 512],
                            in_=ptl[:, :]),
                            wait=b_ptl.rd() + pre_attnT, inc=[V])
                        b_ptl.note_r(V)
                        b_attnT.note_w(V)
                        yield
                    stats_pass(attnT, b_attnT, rowA, b_rowA)
                    yield
                    sq_stats_pass(attnT, b_attnT, rowB, b_rowB)
                    yield
                    ln_rows(cst[0:1, 0:1])
                    yield
                    bcast_ab()
                    yield
                    pre_x2T = b_x2T.wr()
                    xt_load(0, tok_xts=tok_xts)
                    xt_load(1, tok_xts=tok_xts)
                    for i in range(NG):
                        ib = i % 2
                        S.do("vector", lambda v, i=i: v.tensor_mul(
                            tv1[:, :], attnT[:, i * 512:(i + 1) * 512],
                            sAB[:, 0:512]),
                            wait=b_sAB.rd() + b_attnT.rd() + b_tv1.wr(),
                            inc=[V])
                        b_attnT.note_r(V)
                        b_sAB.note_r(V)
                        b_tv1.note_w(V)
                        S.do("vector", lambda v: v.tensor_add(
                            tv1[:, :], tv1[:, :], sAB[:, 512:1024]),
                            wait=b_tv1.rd() + [(V, V.n)], inc=[V])
                        b_sAB.note_r(V)
                        b_tv1.note_w(V)
                        S.do("vector", lambda v, i=i, ib=ib: v.tensor_add(
                            x2T[:, i * 512:(i + 1) * 512], tv1[:, :],
                            xtb[ib][:, :]),
                            wait=b_tv1.rd() + b_xtb3[ib].rd() + pre_x2T,
                            inc=[V])
                        b_tv1.note_r(V)
                        b_xtb3[ib].note_r(V)
                        b_x2T.note_w(V)
                        if i + 2 < NG:
                            xt_load(i + 2, tok_xts=tok_xts)
                        S.do("sync", lambda s, i=i, c=c: s.dma_start(
                            out=x2s[i][:, c * CH:(c + 1) * CH],
                            in_=x2T[:, i * 512:(i + 1) * 512]),
                            wait=[(V, V.n)], inc=[DS])
                        b_x2T.note_r(DS)
                        yield
                    stats_pass(x2T, b_x2T, rowA, b_rowA)
                    yield
                    sq_stats_pass(x2T, b_x2T, rowB, b_rowB)
                    yield
                    ln_rows(cst[0:1, 0:1])
                    yield
                    bcast_ab()
                    yield
                    pre_zT = b_zT.wr()
                    for i in range(NG):
                        S.do("vector", lambda v, i=i: v.tensor_mul(
                            tv1[:, :], x2T[:, i * 512:(i + 1) * 512],
                            sAB[:, 0:512]),
                            wait=b_sAB.rd() + b_x2T.rd() + b_tv1.wr(),
                            inc=[V])
                        b_x2T.note_r(V)
                        b_sAB.note_r(V)
                        b_tv1.note_w(V)
                        S.do("vector", lambda v, i=i: v.tensor_add(
                            zT[:, i * 512:(i + 1) * 512], tv1[:, :],
                            sAB[:, 512:1024]),
                            wait=b_tv1.rd() + pre_zT, inc=[V])
                        b_tv1.note_r(V)
                        b_zT.note_w(V)
                        S.do("sync", lambda s, i=i, c=c: s.dma_start(
                            out=zs[i][:, c * CH:(c + 1) * CH],
                            in_=zT[:, i * 512:(i + 1) * 512]),
                            wait=[(V, V.n)], inc=[DS])
                        b_zT.note_r(DS)
                        yield
                    sq_stats_pass(zT, b_zT, rowB, b_rowB)
                    yield
                    S.do("vector", lambda v: v.tensor_scalar_mul(
                        rowB[0:1, :], rowB[0:1, :], 1.0 / D),
                        wait=b_rowB.rd() + [(V, V.n)], inc=[V])
                    b_rowB.note_w(V)
                    S.do("scalar", lambda s: s.activation(
                        rowC[0:1, :], rowB[0:1, :], AF.Ln, bias=cst[0:1, 1:2]),
                        wait=b_rowB.rd() + b_rowC.wr(), inc=[A])
                    b_rowB.note_r(A)
                    b_rowC.note_w(A)
                    S.do("scalar", lambda s: s.activation(
                        rowB[0:1, :], rowC[0:1, :], AF.Exp, scale=-0.5),
                        wait=b_rowC.rd() + b_rowB.wr() + [(A, A.n)], inc=[A])
                    b_rowC.note_r(A)
                    b_rowB.note_w(A)
                    yield
                    S.do("tensor", lambda t_: t_.matmul(
                        ptl[:, :], ones_1p[:, :], rowB[0:1, :],
                        start=True, stop=True),
                        wait=b_rowB.rd() + b_ptl.wr(), inc=[T])
                    b_ptl.note_w(T)
                    S.do("scalar", lambda s: s.copy(
                        out=sAB[:, 0:512], in_=ptl[:, :]),
                        wait=b_ptl.rd() + b_sAB.wr(), inc=[A])
                    b_ptl.note_r(A)
                    b_sAB.note_w(A)
                    yield
                    for i in range(NG):
                        ib = i % 2
                        S.do("vector", lambda v, i=i: v.tensor_mul(
                            tv1[:, :], zT[:, i * 512:(i + 1) * 512],
                            sAB[:, 0:512]),
                            wait=b_sAB.rd() + b_zT.rd() + b_tv1.wr(), inc=[V])
                        b_zT.note_r(V)
                        b_sAB.note_r(V)
                        b_tv1.note_w(V)
                        S.do("vector", lambda v, i=i, ib=ib: v.tensor_scalar_mul(
                            yTt[ib][:, :], tv1[:, :], rms_sb[:, i:i + 1]),
                            wait=b_tv1.rd() + b_yTt[ib].wr(), inc=[V])
                        b_tv1.note_r(V)
                        b_yTt[ib].note_w(V)
                        S.do("gpsimd", lambda g, i=i, c=c, ib=ib: g.dma_start(
                            out=ys[i][:, c * CH:(c + 1) * CH],
                            in_=yTt[ib][:, :]),
                            wait=b_yTt[ib].rd(), inc=[YSS])
                        b_yTt[ib].note_r(YSS)
                        yield
                    for gg in range(NG):
                        S.do("tensor", lambda t_, gg=gg: t_.matmul(
                            ptl[0:8, :], gw2[:, gg * E:(gg + 1) * E],
                            zT[:, gg * 512:(gg + 1) * 512],
                            start=(gg == 0), stop=(gg == NG - 1)),
                            wait=b_zT.rd() + [(V, tok_gw2)]
                            + (b_ptl.wr() if gg == 0 else []),
                            inc=[T] if gg == NG - 1 else [])
                        b_zT.note_r(T)
                        b_ptl.note_w(T)
                    S.do("vector", lambda v: v.tensor_copy(
                        out=lg8[:, :], in_=ptl[0:8, :]),
                        wait=b_ptl.rd() + b_lg8.wr(), inc=[V])
                    b_ptl.note_r(V)
                    b_lg8.note_w(V)
                    yield
                    for tb in range(4):
                        S.do("tensor", lambda t_, tb=tb: t_.transpose(
                            ptl[:, tb:tb + 1],
                            rowB[0:1, tb * 128:(tb + 1) * 128],
                            identF[0:1, 0:1]),
                            wait=b_rowB.rd() + (b_ptl.wr() if tb == 0 else []),
                            inc=[T] if tb == 3 else [])
                    b_rowB.note_r(T)
                    b_ptl.note_w(T)
                    S.do("vector", lambda v: v.tensor_copy(
                        out=Ct[:, 0:4], in_=ptl[:, 0:4]),
                        wait=b_ptl.rd() + b_Ct.wr(), inc=[V])
                    b_ptl.note_r(V)
                    b_Ct.note_w(V)
                    yield
                    for tb in range(4):
                        S.do("tensor", lambda t_, tb=tb: t_.transpose(
                            ptl[:, 256 + tb * 8: 256 + (tb + 1) * 8],
                            lg8[:, tb * 128:(tb + 1) * 128],
                            identF[0:8, 0:8]),
                            wait=b_lg8.rd() + (b_ptl.wr() if tb == 0 else []),
                            inc=[T] if tb == 3 else [])
                    b_lg8.note_r(T)
                    b_ptl.note_w(T)
                    S.do("vector", lambda v: v.tensor_copy(
                        out=lgt[:, :], in_=ptl[:, 256:288]),
                        wait=b_ptl.rd() + b_lgt.wr(), inc=[V])
                    b_ptl.note_r(V)
                    b_lgt.note_w(V)
                    yield
                    for tb in range(4):
                        Lg = lgt[:, tb * 8:(tb + 1) * 8]
                        g0 = gt[:, 0:8]
                        top2 = gt[:, 8:16]
                        eq1 = gt[:, 16:24]
                        p1_ = gt[:, 24:32]
                        p2_ = gt[:, 32:40]
                        p3_ = gt[:, 40:48]
                        ffo1 = gt[:, 48:56]
                        eq2 = gt[:, 56:64]
                        ffo2 = gt[:, 64:72]
                        w12 = gt[:, 72:80]

                        def VD(fn, extra=()):
                            S.do("vector", fn,
                                 wait=[(V, V.n), (G, G.n)] + list(extra),
                                 inc=[V])

                        def GM(ap):
                            S.do("gpsimd", lambda g, ap=ap: g.memset(ap, 0.0),
                                 wait=[(V, V.n)], inc=[G])

                        VD(lambda v, tb=tb: v.tensor_scalar_mul(
                            g0, lgt[:, tb * 8:(tb + 1) * 8], Ct[:, tb:tb + 1]),
                           extra=b_Ct.rd() + b_lgt.rd() + b_gt.wr())
                        b_gt.note_w(V)
                        b_Ct.note_r(V)
                        VD(lambda v: v.max(top2, g0))
                        VD(lambda v: v.tensor_single_scalar(
                            eq1, g0, top2[:, 0:1], ALU.is_equal))
                        GM(p1_)
                        VD(lambda v: v.tensor_copy(out=p1_[:, 1:8],
                                                   in_=eq1[:, 0:7]))
                        GM(p2_)
                        VD(lambda v: v.tensor_copy(out=p2_[:, 0:2],
                                                   in_=p1_[:, 0:2]))
                        VD(lambda v: v.tensor_add(p2_[:, 2:8], p1_[:, 2:8],
                                                  p1_[:, 0:6]))
                        GM(p3_)
                        VD(lambda v: v.tensor_copy(out=p3_[:, 0:4],
                                                   in_=p2_[:, 0:4]))
                        VD(lambda v: v.tensor_add(p3_[:, 4:8], p2_[:, 4:8],
                                                  p2_[:, 0:4]))
                        VD(lambda v: v.tensor_single_scalar(
                            ffo1, p3_, 0.0, ALU.is_equal))
                        VD(lambda v: v.tensor_mul(ffo1, eq1, ffo1))
                        VD(lambda v: v.tensor_single_scalar(
                            eq2, g0, top2[:, 1:2], ALU.is_equal))
                        VD(lambda v: v.scalar_tensor_tensor(
                            eq2, ffo1, -1.0, eq2, ALU.mult, ALU.add))
                        VD(lambda v: v.tensor_scalar_max(eq2, eq2, 0.0))
                        GM(p1_)
                        VD(lambda v: v.tensor_copy(out=p1_[:, 1:8],
                                                   in_=eq2[:, 0:7]))
                        GM(p2_)
                        VD(lambda v: v.tensor_copy(out=p2_[:, 0:2],
                                                   in_=p1_[:, 0:2]))
                        VD(lambda v: v.tensor_add(p2_[:, 2:8], p1_[:, 2:8],
                                                  p1_[:, 0:6]))
                        GM(p3_)
                        VD(lambda v: v.tensor_copy(out=p3_[:, 0:4],
                                                   in_=p2_[:, 0:4]))
                        VD(lambda v: v.tensor_add(p3_[:, 4:8], p2_[:, 4:8],
                                                  p2_[:, 0:4]))
                        VD(lambda v: v.tensor_single_scalar(
                            ffo2, p3_, 0.0, ALU.is_equal))
                        VD(lambda v: v.tensor_mul(ffo2, eq2, ffo2))
                        VD(lambda v: v.tensor_sub(w12[:, 2:3], top2[:, 1:2],
                                                  top2[:, 0:1]))
                        S.do("scalar", lambda s: s.activation(
                            w12[:, 3:4], w12[:, 2:3], AF.Exp),
                            wait=[(V, V.n), (A, A.n)], inc=[A])
                        S.do("vector", lambda v: v.tensor_scalar_add(
                            w12[:, 4:5], w12[:, 3:4], 1.0),
                            wait=[(A, A.n)], inc=[V])
                        VD(lambda v: v.reciprocal(w12[:, 5:6], w12[:, 4:5]))
                        VD(lambda v: v.tensor_mul(w12[:, 6:7], w12[:, 3:4],
                                                  w12[:, 5:6]))
                        VD(lambda v: v.tensor_scalar_mul(ffo1, ffo1,
                                                         w12[:, 5:6]))
                        VD(lambda v: v.tensor_scalar_mul(ffo2, ffo2,
                                                         w12[:, 6:7]))
                        VD(lambda v: v.tensor_add(g0, ffo1, ffo2))
                        b_gt.note_w(V)
                        yield
                        for e in range(E):
                            S.do("tensor", lambda t_, e=e: t_.transpose(
                                ptl[0:1, 0:128], g0[:, e:e + 1],
                                identF[:, :]),
                                wait=b_gt.rd() + b_ptl.wr(), inc=[T])
                            b_gt.note_r(T)
                            b_ptl.note_w(T)
                            S.do("vector", lambda v, e=e: v.tensor_copy(
                                out=stage8[0:1, e * 128:(e + 1) * 128],
                                in_=ptl[0:1, 0:128]),
                                wait=b_ptl.rd() + [(STG, b_stage_dma[0])],
                                inc=[V])
                            b_ptl.note_r(V)
                        S.do("gpsimd", lambda g, tb=tb, c=c: g.dma_start(
                            out=wTe_s[:, c * CH + tb * 128:
                                      c * CH + (tb + 1) * 128],
                            in_=stage8[0:1, :].rearrange(
                                "p (e k) -> p e k", e=E)),
                            wait=[(V, V.n)], inc=[STG])
                        b_stage_dma[0] = STG.n
                        yield

                tail = None
                pre_oall = [[], []]
                for c in range(NCH if phases >= 3 else 0):
                    if c == 0:
                        barrier(extra=[(A, tok_cbias)])
                    pre_oall[c % 2] = b_oall[c % 2].wr()
                    for hp in range(NG if 'heads' in p3s else 0):
                        for m in range(NM):
                            sched_sim(c, hp, m)
                            if tail is not None and (hp * NM + m) % 4 == 0:
                                next(tail, None)
                            k = hp * NM + m - SKEW
                            if k >= 0:
                                pm = (k // NM, k % NM)
                                sched_o(c, *pm)
                                if pm[1] == NM - 1:
                                    sched_norm(c, pm[0])
                    for k in range(NG * NM - SKEW, NG * NM):
                        pm = (k // NM, k % NM)
                        sched_o(c, *pm)
                        if pm[1] == NM - 1:
                            sched_norm(c, pm[0])
                    if tail is not None:
                        for _ in tail:
                            pass
                    tail = tail_gen(c)
                if tail is not None:
                    for _ in tail:
                        pass

                if "x2t" in dbg:
                    ap = dbg_tensor("x2t", [NG, 128, SEQ])
                    S.do("sync", lambda s, ap=ap: s.dma_start(out=ap, in_=x2s),
                         wait=[(DS, DS.n), (V, V.n)], inc=[STO])
                if "zt" in dbg:
                    ap = dbg_tensor("zt", [NG, 128, SEQ])
                    S.do("sync", lambda s, ap=ap: s.dma_start(out=ap, in_=zs),
                         wait=[(DS, DS.n), (V, V.n)], inc=[STO])
                if "yt" in dbg:
                    ap = dbg_tensor("yt", [NG, 128, SEQ], BF16)
                    S.do("sync", lambda s, ap=ap: s.dma_start(out=ap, in_=ys),
                         wait=[(DG, DG.n), (V, V.n)], inc=[STO])
                if "wte" in dbg:
                    ap = dbg_tensor("wte", [E, SEQ])
                    S.do("sync", lambda s, ap=ap: s.dma_start(out=ap, in_=wTe_s),
                         wait=[(DG, DG.n)], inc=[STO])

        barrier()
        with ExitStack() as p4c:
            P4 = p4c.enter_context
            ybuf = P4(nc.sbuf_tensor("ybuf", [128, NG * FCH], BF16))
            acc = P4(nc.sbuf_tensor("acc", [128, NG * FCH], F32))
            hbuf = P4(nc.sbuf_tensor("hbuf", [128, 11 * FCH], BF16))
            w1q = [P4(nc.sbuf_tensor(f"w1q{i}", [128, NG * 768], BF16))
                   for i in range(2)]
            w2q = [P4(nc.sbuf_tensor(f"w2q{i}", [128, NG * 768], BF16))
                   for i in range(2)]
            w3h = P4(nc.sbuf_tensor("w3h", [128, 11 * 1024], BF16))
            sil = [P4(nc.sbuf_tensor(f"sil{i}", [128, FCH], F32))
                   for i in range(2)]
            wbb = P4(nc.sbuf_tensor("wbb", [128, FCH], F32))
            tmpc = [P4(nc.sbuf_tensor(f"tmpc{i}", [128, 512], F32))
                    for i in range(2)]
            x2f = [P4(nc.sbuf_tensor(f"x2f{i}", [128, 512], F32))
                   for i in range(2)]
            outb = [P4(nc.sbuf_tensor(f"outb{i}", [128, 1024], F32))
                    for i in range(2)]
            sAB4 = P4(nc.sbuf_tensor("sAB4", [128, 1024], F32))
            r4a = P4(nc.sbuf_tensor("r4a", [1, FCH], F32))
            r4b = P4(nc.sbuf_tensor("r4b", [1, FCH], F32))
            r4c = P4(nc.sbuf_tensor("r4c", [1, FCH], F32))
            sq4 = P4(nc.sbuf_tensor("sq4", [128, FCH], F32))
            wrow = P4(nc.sbuf_tensor("wrow", [1, FCH], F32))

            pup = P4(nc.psum_tensor("pup", [128, 1024], F32))
            pgt = P4(nc.psum_tensor("pgt", [128, 1024], F32))
            pdn = [P4(nc.psum_tensor(f"pdn{i}", [128, 512], F32))
                   for i in range(2)]
            pms = P4(nc.psum_tensor("pms", [128, 1024], F32))

            b_y = Buf()
            b_acc = Buf()
            b_h = Buf()
            b_w1q = [Buf(), Buf()]
            b_w2q = [Buf(), Buf()]
            b_w3h = Buf()
            b_sil = [Buf(), Buf()]
            b_wb = Buf()
            b_tmpc = [Buf(), Buf()]
            b_x2f = [Buf(), Buf()]
            b_outb = [Buf(), Buf()]
            b_sab4 = Buf()
            b_r4a = Buf()
            b_r4b = Buf()
            b_r4c = Buf()
            b_sq4 = Buf()
            b_pup = Buf()
            b_pgt = Buf()
            b_pdn = [Buf(), Buf()]
            b_pms = Buf()
            b_wrow = Buf()

            def q_load(eng, dstbuf, dst_ap, src_ap, waits, bufobj, stream):
                S.do(eng, lambda s, d=dst_ap, i_=src_ap: s.dma_start(out=d, in_=i_),
                     wait=waits + bufobj.wr(), inc=[stream])
                bufobj.note_w(stream)

            for F in range(NF if phases >= 4 else 0):
                S.do("gpsimd", lambda g, F=F: g.dma_start(
                    out=ybuf.ap().rearrange("p (g c) -> p g c", g=NG),
                    in_=ys.rearrange("g p c -> p g c")
                    [:, :, F * FCH:(F + 1) * FCH]),
                    wait=b_y.wr(), inc=[YB])
                b_y.note_w(YB)
                S.do("gpsimd", lambda g, F=F: g.dma_start(
                    out=acc.ap().rearrange("p (g c) -> p g c", g=NG),
                    in_=zs.rearrange("g p c -> p g c")
                    [:, :, F * FCH:(F + 1) * FCH]),
                    wait=b_acc.wr(), inc=[ACC])
                b_acc.note_w(ACC)

                for e in range(E):
                    S.do("sync", lambda s, e=e, F=F: s.dma_start(
                        out=wrow[0:1, :], in_=wTe_s[e, F * FCH:(F + 1) * FCH]),
                        wait=b_wrow.wr(), inc=[WR])
                    b_wrow.note_w(WR)
                    for nsub in range(2):
                        S.do("tensor", lambda t_, nsub=nsub: t_.matmul(
                            pms[:, nsub * 512:(nsub + 1) * 512], ones_1p[:, :],
                            wrow[0:1, nsub * 512:(nsub + 1) * 512],
                            start=True, stop=True),
                            wait=b_wrow.rd()
                            + (b_pms.wr() if nsub == 0 else []) + [(G, tok_ones)],
                            inc=[T] if nsub == 1 else [])
                    b_wrow.note_r(T)
                    b_pms.note_w(T)
                    S.do("scalar", lambda s: s.copy(out=wbb[:, :], in_=pms[:, :]),
                         wait=b_pms.rd() + b_wb.wr(), inc=[A])
                    b_pms.note_r(A)
                    b_wb.note_w(A)

                    for half in range(2):
                        for si, (j0, nj) in enumerate(HSLABS):
                            qb = si
                            qi = half * 2 + si
                            col0 = sum(NG * n * 128 for _, n in
                                       (HSLABS * 2)[:qi])
                            w = NG * nj * 128
                            S.do("gpsimd", lambda g, e=e, col0=col0, w=w,
                                 qb=qb: g.dma_start(
                                     out=w1q[qb][:, 0:w],
                                     in_=w1_in[e][:, col0:col0 + w]),
                                 wait=b_w1q[qb].wr(), inc=[WQ1[qb]])
                            b_w1q[qb].note_w(WQ1[qb])
                            S.do("gpsimd", lambda g, e=e, col0=col0, w=w,
                                 qb=qb: g.dma_start(
                                     out=w2q[qb][:, 0:w],
                                     in_=w2_in[e][:, col0:col0 + w]),
                                 wait=b_w2q[qb].wr(), inc=[WQ2[qb]])
                            b_w2q[qb].note_w(WQ2[qb])
                        S.do("gpsimd", lambda g, e=e, half=half: g.dma_start(
                            out=w3h[:, :],
                            in_=w3_in[e][:, half * 11264:(half + 1) * 11264]),
                            wait=b_w3h.wr(), inc=[W3H])
                        b_w3h.note_w(W3H)

                        pre_h = b_h.wr()
                        for si, (j0, nj) in enumerate(HSLABS):
                            qb = si
                            for jl in range(nj):
                                j_loc = j0 + jl
                                sb = (half * 11 + j_loc) % 2
                                for nsub in range(2):
                                    for gg in range(NG):
                                        S.do("tensor", lambda t_, gg=gg, jl=jl,
                                             nsub=nsub, qb=qb, nj=nj: t_.matmul(
                                                 pup[:, nsub * 512:
                                                     (nsub + 1) * 512],
                                                 w1q[qb][:, (gg * nj + jl) * 128:
                                                         (gg * nj + jl + 1) * 128],
                                                 ybuf[:, gg * FCH + nsub * 512:
                                                      gg * FCH + (nsub + 1) * 512],
                                                 start=(gg == 0),
                                                 stop=(gg == NG - 1)),
                                             wait=b_w1q[qb].rd() + b_y.rd()
                                             + (b_pup.wr()
                                                if (gg == 0 and nsub == 0) else []),
                                             inc=[T] if (gg == NG - 1 and nsub == 1)
                                             else [])
                                b_pup.note_w(T)
                                b_y.note_r(T)
                                b_w1q[qb].note_r(T)
                                for nsub in range(2):
                                    for gg in range(NG):
                                        S.do("tensor", lambda t_, gg=gg, jl=jl,
                                             nsub=nsub, qb=qb, nj=nj: t_.matmul(
                                                 pgt[:, nsub * 512:
                                                     (nsub + 1) * 512],
                                                 w2q[qb][:, (gg * nj + jl) * 128:
                                                         (gg * nj + jl + 1) * 128],
                                                 ybuf[:, gg * FCH + nsub * 512:
                                                      gg * FCH + (nsub + 1) * 512],
                                                 start=(gg == 0),
                                                 stop=(gg == NG - 1)),
                                             wait=b_w2q[qb].rd()
                                             + (b_pgt.wr()
                                                if (gg == 0 and nsub == 0) else []),
                                             inc=[T] if (gg == NG - 1 and nsub == 1)
                                             else [])
                                b_pgt.note_w(T)
                                b_w2q[qb].note_r(T)
                                S.do("scalar", lambda s, sb=sb: s.activation(
                                    sil[sb][:, :], pup[:, :], AF.Silu),
                                    wait=b_pup.rd() + b_sil[sb].wr(), inc=[A])
                                b_pup.note_r(A)
                                b_sil[sb].note_w(A)
                                S.do("vector", lambda v, j_loc=j_loc, sb=sb:
                                     v.tensor_mul(
                                         hbuf[:, j_loc * FCH:(j_loc + 1) * FCH],
                                         sil[sb][:, :], pgt[:, :]),
                                     wait=b_sil[sb].rd() + b_pgt.rd() + pre_h,
                                     inc=[V])
                                b_sil[sb].note_r(V)
                                b_pgt.note_r(V)
                                b_h.note_w(V)
                        tok_h = V.n
                        for i in range(NG):
                            for nsub in range(2):
                                db = (i * 2 + nsub) % 2
                                for jl in range(11):
                                    S.do("tensor", lambda t_, i=i, jl=jl,
                                         nsub=nsub, db=db: t_.matmul(
                                             pdn[db][:, :],
                                             w3h[:, jl * 1024 + i * 128:
                                                 jl * 1024 + (i + 1) * 128],
                                             hbuf[:, jl * FCH + nsub * 512:
                                                  jl * FCH + (nsub + 1) * 512],
                                             start=(jl == 0), stop=(jl == 10)),
                                         wait=b_w3h.rd() + [(V, tok_h)]
                                         + (b_pdn[db].wr() if jl == 0 else []),
                                         inc=[T] if jl == 10 else [])
                                b_pdn[db].note_w(T)
                                b_h.note_r(T)
                                b_w3h.note_r(T)
                                tb_ = db
                                S.do("vector", lambda v, db=db, tb_=tb_, nsub=nsub:
                                     v.tensor_mul(
                                         tmpc[tb_][:, :], pdn[db][:, :],
                                         wbb[:, nsub * 512:(nsub + 1) * 512]),
                                     wait=b_pdn[db].rd() + b_wb.rd()
                                     + b_tmpc[tb_].wr(), inc=[V])
                                b_pdn[db].note_r(V)
                                b_wb.note_r(V)
                                b_tmpc[tb_].note_w(V)
                                S.do("vector", lambda v, i=i, nsub=nsub, tb_=tb_:
                                     v.tensor_add(
                                         acc[:, i * FCH + nsub * 512:
                                             i * FCH + (nsub + 1) * 512],
                                         acc[:, i * FCH + nsub * 512:
                                             i * FCH + (nsub + 1) * 512],
                                         tmpc[tb_][:, :]),
                                     wait=b_tmpc[tb_].rd() + b_acc.rd()
                                     + [(V, V.n)], inc=[V])
                                b_tmpc[tb_].note_r(V)
                                b_acc.note_w(V)

                for i in range(NG):
                    S.do("scalar", lambda s, i=i: s.square(
                        sq4[:, :], acc[:, i * FCH:(i + 1) * FCH]),
                        wait=b_acc.rd() + b_sq4.wr(), inc=[A])
                    b_sq4.note_w(A)
                    for nsub in range(2):
                        S.do("tensor", lambda t_, i=i, nsub=nsub: t_.matmul(
                            pms[0:1, nsub * 512:(nsub + 1) * 512], ones_p1[:, :],
                            acc[:, i * FCH + nsub * 512:
                                i * FCH + (nsub + 1) * 512],
                            start=(i == 0), stop=(i == NG - 1)),
                            wait=b_acc.rd()
                            + (b_pms.wr() if (i == 0 and nsub == 0) else []),
                            inc=[T] if nsub == 1 else [])
                    b_pms.note_w(T)
                    b_acc.note_r(T, A)
                    for nsub in range(2):
                        S.do("tensor", lambda t_, i=i, nsub=nsub: t_.matmul(
                            pdn[nsub][0:1, :], ones_p1[:, :],
                            sq4[:, nsub * 512:(nsub + 1) * 512],
                            start=(i == 0), stop=(i == NG - 1)),
                            wait=b_sq4.rd()
                            + (b_pdn[nsub].wr() if i == 0 else []),
                            inc=[T] if nsub == 1 else [])
                        b_pdn[nsub].note_w(T)
                    b_sq4.note_r(T)
                S.do("vector", lambda v: v.tensor_scalar_mul(
                    r4a[0:1, :], pms[0:1, :], 1.0 / D),
                    wait=b_pms.rd() + b_r4a.wr(), inc=[V])
                b_pms.note_r(V)
                b_r4a.note_w(V)
                S.do("vector", lambda v: v.tensor_scalar_mul(
                    r4b[0:1, 0:512], pdn[0][0:1, :], 1.0 / D),
                    wait=b_pdn[0].rd() + b_r4b.wr(), inc=[V])
                S.do("vector", lambda v: v.tensor_scalar_mul(
                    r4b[0:1, 512:1024], pdn[1][0:1, :], 1.0 / D),
                    wait=b_pdn[1].rd() + [(V, V.n)], inc=[V])
                b_pdn[0].note_r(V)
                b_pdn[1].note_r(V)
                b_r4b.note_w(V)
                S.do("vector", lambda v: v.scalar_tensor_tensor(
                    r4c[0:1, :], r4a[0:1, :], 1.0, r4a[0:1, :],
                    ALU.mult, ALU.mult),
                    wait=b_r4a.rd() + b_r4c.wr() + [(V, V.n)], inc=[V])
                b_r4c.note_w(V)
                S.do("vector", lambda v: v.tensor_sub(
                    r4b[0:1, :], r4b[0:1, :], r4c[0:1, :]),
                    wait=b_r4c.rd() + [(V, V.n)], inc=[V])
                b_r4c.note_r(V)
                b_r4b.note_w(V)
                S.do("scalar", lambda s: s.activation(
                    r4c[0:1, :], r4b[0:1, :], AF.Ln, bias=cst[0:1, 0:1]),
                    wait=b_r4b.rd() + b_r4c.wr(), inc=[A])
                b_r4b.note_r(A)
                b_r4c.note_w(A)
                S.do("scalar", lambda s: s.activation(
                    r4b[0:1, :], r4c[0:1, :], AF.Exp, scale=-0.5),
                    wait=b_r4c.rd() + b_r4b.wr() + [(A, A.n)], inc=[A])
                b_r4c.note_r(A)
                b_r4b.note_w(A)
                S.do("vector", lambda v: v.scalar_tensor_tensor(
                    r4c[0:1, :], r4a[0:1, :], -1.0, r4b[0:1, :],
                    ALU.mult, ALU.mult),
                    wait=b_r4b.rd() + b_r4a.rd() + b_r4c.wr(), inc=[V])
                b_r4a.note_r(V)
                b_r4b.note_r(V)
                b_r4c.note_w(V)

                for nsub in range(2):
                    S.do("tensor", lambda t_, nsub=nsub: t_.matmul(
                        pms[:, 0:512], ones_1p[:, :],
                        r4b[0:1, nsub * 512:(nsub + 1) * 512],
                        start=True, stop=True),
                        wait=b_r4b.rd() + b_r4c.rd() + b_pms.wr())
                    S.do("tensor", lambda t_, nsub=nsub: t_.matmul(
                        pms[:, 512:1024], ones_1p[:, :],
                        r4c[0:1, nsub * 512:(nsub + 1) * 512],
                        start=True, stop=True), inc=[T])
                    b_r4b.note_r(T)
                    b_r4c.note_r(T)
                    b_pms.note_w(T)
                    S.do("scalar", lambda s: s.copy(out=sAB4[:, :], in_=pms[:, :]),
                         wait=b_pms.rd() + b_sab4.wr(), inc=[A])
                    b_pms.note_r(A)
                    b_sab4.note_w(A)
                    x2tok = [0] * NG

                    def x2_load(i, F=F, nsub=nsub):
                        xb = i % 2
                        S.do("gpsimd", lambda g, i=i, xb=xb: g.dma_start(
                            out=x2f[xb][:, :],
                            in_=x2s[i][:, F * FCH + nsub * 512:
                                       F * FCH + (nsub + 1) * 512]),
                            wait=b_x2f[xb].wr(), inc=[XF[xb]])
                        b_x2f[xb].note_w(XF[xb])
                        x2tok[i] = (XF[xb], XF[xb].n)

                    x2_load(0)
                    x2_load(1)
                    for i in range(NG):
                        xb = i % 2
                        S.do("vector", lambda v, i=i, nsub=nsub: v.tensor_mul(
                            tmpc[0][:, :],
                            acc[:, i * FCH + nsub * 512:
                                i * FCH + (nsub + 1) * 512],
                            sAB4[:, 0:512]),
                            wait=b_sab4.rd() + b_tmpc[0].wr() + b_acc.rd(),
                            inc=[V])
                        b_sab4.note_r(V)
                        b_tmpc[0].note_w(V)
                        S.do("vector", lambda v: v.tensor_add(
                            tmpc[1][:, :], tmpc[0][:, :], sAB4[:, 512:1024]),
                            wait=b_tmpc[0].rd() + b_tmpc[1].wr(), inc=[V])
                        b_tmpc[0].note_r(V)
                        b_tmpc[1].note_w(V)
                        S.do("vector", lambda v, i=i, nsub=nsub, xb=xb: v.tensor_add(
                            acc[:, i * FCH + nsub * 512:
                                i * FCH + (nsub + 1) * 512],
                            tmpc[1][:, :], x2f[xb][:, :]),
                            wait=b_tmpc[1].rd() + [x2tok[i], (V, V.n)],
                            inc=[V])
                        b_tmpc[1].note_r(V)
                        b_x2f[xb].note_r(V)
                        b_acc.note_w(V)
                        if i + 2 < NG:
                            x2_load(i + 2)
                    for tt in range(4):
                        ob = tt % 2
                        t_abs = F * FCH + nsub * 512 + tt * 128
                        for i in range(NG):
                            S.do("tensor", lambda t_, i=i, tt=tt, nsub=nsub:
                                 t_.transpose(
                                     pms[:, i * 128:(i + 1) * 128],
                                     acc[:, i * FCH + nsub * 512 + tt * 128:
                                         i * FCH + nsub * 512 + (tt + 1) * 128],
                                     identF[:, :]),
                                 wait=b_acc.rd()
                                 + (b_pms.wr() if i == 0 else []),
                                 inc=[T] if i == NG - 1 else [])
                        b_pms.note_w(T)
                        b_acc.note_r(T)
                        S.do("scalar", lambda s, ob=ob: s.copy(
                            out=outb[ob][:, :], in_=pms[:, :]),
                            wait=b_pms.rd() + b_outb[ob].wr(), inc=[A])
                        b_pms.note_r(A)
                        b_outb[ob].note_w(A)
                        S.do("gpsimd", lambda g, t_abs=t_abs, ob=ob: g.dma_start(
                            out=out[t_abs:t_abs + 128, :], in_=outb[ob][:, :]),
                            wait=b_outb[ob].rd(), inc=[OB[ob]])
                        b_outb[ob].note_r(OB[ob])

        S.do("sync", wait=[(s, s.n) for s in ALLS] + [(STO, STO.n)])

        with nc.Block() as blk:
            S.emit_into(blk)

    return nc, dbg_aps


_NC_CACHE = {}


def _get_nc(dbg=()):
    key = tuple(sorted(dbg))
    if key not in _NC_CACHE:
        _NC_CACHE[key] = build_nc(dbg)
    return _NC_CACHE[key]


def prep_inputs(L0_attn_x, L0_attn_query_proj, L0_attn_key_proj,
                L0_attn_value_proj, L0_attn_output_proj, L0_ffn_gate_w,
                L0_ffn_rms_w, L0_ffn_W1, L0_ffn_W2, L0_ffn_W3):
    x = np.ascontiguousarray(np.asarray(L0_attn_x, dtype=np.float32))
    def sbuf_layout(a):
        c = a.shape[1]
        return np.ascontiguousarray(
            a.reshape(NG, 128, c).transpose(1, 0, 2).reshape(128, NG * c))

    wq = sbuf_layout(np.asarray(L0_attn_query_proj, np.float32)
                     .transpose(2, 0, 1).reshape(D, H * KD))
    wk = sbuf_layout(np.asarray(L0_attn_key_proj, np.float32))
    wv = sbuf_layout(np.asarray(L0_attn_value_proj, np.float32))
    wo = sbuf_layout(np.asarray(L0_attn_output_proj, np.float32)
                     .transpose(1, 2, 0).reshape(H * KD, D))
    gw = np.ascontiguousarray(
        np.asarray(L0_ffn_gate_w, np.float32).T.reshape(NG, 128, E)
        .transpose(1, 0, 2).reshape(128, NG * E))
    rms = np.ascontiguousarray(
        np.asarray(L0_ffn_rms_w, np.float32).reshape(NG, 128).T)
    qrs = []
    for half in range(2):
        for j0, nj in HSLABS:
            qrs.append((half * 1408 + j0 * 128, half * 1408 + (j0 + nj) * 128))

    def slab12(W):
        out = np.empty((E, 128, NG * DE), np.float32)
        for e in range(E):
            v = np.asarray(W[e], np.float32).T.reshape(NG, 128, DE) \
                .transpose(1, 0, 2)
            out[e] = np.concatenate(
                [v[:, :, a:b].reshape(128, NG * (b - a)) for a, b in qrs],
                axis=1)
        return out

    def slab3(W):
        out = np.empty((E, 128, 22 * 1024), np.float32)
        for e in range(E):
            out[e] = np.asarray(W[e], np.float32).T.reshape(22, 128, 1024) \
                .transpose(1, 0, 2).reshape(128, 22 * 1024)
        return out

    w1t = slab12(L0_ffn_W1)
    w2t = slab12(L0_ffn_W2)
    w3t = slab3(L0_ffn_W3)
    in_maps = []
    for c in range(N_CORES):
        in_maps.append(dict(x=np.ascontiguousarray(x[c]), wq=wq, wk=wk, wv=wv,
                            wo=wo, gw=gw, rms=rms, w1t=w1t, w2t=w2t, w3t=w3t))
    return in_maps


def kernel(**inputs):
    in_maps = prep_inputs(**inputs)
    nc, _ = _get_nc()
    res = run_bass_kernel_spmd(nc, in_maps, core_ids=list(range(N_CORES)))
    return np.stack([res.results[c]["out"] for c in range(N_CORES)], axis=0)


# version 16 (final)
# speedup vs baseline: 1.0083x; 1.0083x over previous
import numpy as np
from contextlib import ExitStack

import concourse.bass as bass
import concourse.mybir as mybir
from concourse.bass_utils import run_bass_kernel_spmd

F32 = mybir.dt.float32
BF16 = mybir.dt.bfloat16
AF = mybir.ActivationFunctionType
ALU = mybir.AluOpType
AX = mybir.AxisListType

N_CORES = 8
SEQ = 4096
D = 1024
H = 16
KD = 64
E = 8
DE = 2816
NCH = 8
CH = 512
NF = 4
FCH = 1024
NG = 8
NJ = 22
NM = SEQ // 128
SKEW = 1
HSLABS = [(0, 6), (6, 5)]
EPS_LN = 1e-5
EPS_RMS = 1e-6

ENGINES = ("sync", "gpsimd", "tensor", "scalar", "vector")


class Stream:
    __slots__ = ("sem", "n", "step", "name")

    def __init__(self, sem, step=1, name=""):
        self.sem = sem
        self.n = 0
        self.step = step
        self.name = name


class Buf:
    __slots__ = ("r", "w")

    def __init__(self):
        self.r = {}
        self.w = {}

    def rd(self):
        return list(self.w.items())

    def wr(self):
        d = dict(self.r)
        for s, t in self.w.items():
            d[s] = max(d.get(s, 0), t)
        return list(d.items())

    def note_r(self, *streams):
        for s in streams:
            self.r[s] = s.n

    def note_w(self, *streams):
        self.w = {s: s.n for s in streams}
        self.r = {}


class Sched:
    def __init__(self):
        self.steps = {e: [] for e in ENGINES}
        self.waited = {}

    def do(self, eng, fn=None, wait=(), inc=()):
        wl = []
        for w in wait:
            s, v = w if isinstance(w, tuple) else (w, w.n)
            assert v <= s.n, f"wait on future token {s.name}: {v} > {s.n}"
            if v <= 0:
                continue
            k = (eng, id(s))
            if self.waited.get(k, 0) >= v:
                continue
            self.waited[k] = v
            wl.append((s, v))
        il = []
        for s in inc:
            s.n += s.step
            il.append((s, s.step))
        assert len(il) <= 1
        self.steps[eng].append((wl, fn, il))

    def emit_into(self, block):
        def mk(name):
            steps = self.steps[name]

            def run(eng):
                for wl, fn, il in steps:
                    for s, v in wl:
                        eng.wait_ge(s.sem, v)
                    ins = fn(eng) if fn is not None else None
                    for s, k in il:
                        if ins is None:
                            ins = eng.nop()
                        ins.then_inc(s.sem, k)
            return run
        block.sync(mk("sync"))
        block.gpsimd(mk("gpsimd"))
        block.tensor(mk("tensor"))
        block.scalar(mk("scalar"))
        block.vector(mk("vector"))


def build_nc(dbg=(), phases=4, p1s=('ln','tr','q','k','v'),
             p3s=('heads','proj','projmm','projcp','projsq','projst',
                  'res','ln1f','rms','gate'), p3n=NCH):
    nc = bass.Bass("TRN2", target_bir_lowering=False)

    x_in = nc.dram_tensor("x", [SEQ, D], F32, kind="ExternalInput").ap()
    wq_in = nc.dram_tensor("wq", [128, NG * 1024], F32, kind="ExternalInput").ap()
    wk_in = nc.dram_tensor("wk", [128, NG * KD], F32, kind="ExternalInput").ap()
    wv_in = nc.dram_tensor("wv", [128, NG * KD], F32, kind="ExternalInput").ap()
    wo_in = nc.dram_tensor("wo", [128, NG * 1024], F32, kind="ExternalInput").ap()
    gw_in = nc.dram_tensor("gw", [128, NG * E], F32, kind="ExternalInput").ap()
    rms_in = nc.dram_tensor("rms", [128, NG], F32, kind="ExternalInput").ap()
    w1_in = nc.dram_tensor("w1t", [E, 128, NG * DE], F32,
                           kind="ExternalInput").ap()
    w2_in = nc.dram_tensor("w2t", [E, 128, NG * DE], F32,
                           kind="ExternalInput").ap()
    w3_in = nc.dram_tensor("w3t", [E, 128, 22 * 1024], F32,
                           kind="ExternalInput").ap()
    out = nc.dram_tensor("out", [SEQ, D], F32, kind="ExternalOutput").ap()

    x2s = nc.dram_tensor("x2s", [NG, 128, SEQ], F32, kind="Internal").ap()
    zs = nc.dram_tensor("zs", [NG, 128, SEQ], F32, kind="Internal").ap()
    xts = nc.dram_tensor("xts", [NG, 128, SEQ], F32, kind="Internal").ap()
    ys = nc.dram_tensor("ys", [NG, 128, SEQ], BF16, kind="Internal").ap()
    wTe_s = nc.dram_tensor("wte_s", [E, SEQ], F32, kind="Internal").ap()


    dbg_aps = {}

    def dbg_tensor(name, shape, dtype=F32):
        dbg_aps[name] = nc.dram_tensor("dbg_" + name, shape, dtype,
                                       kind="ExternalOutput").ap()
        return dbg_aps[name]

    import ml_dtypes
    identB_h = nc.inline_tensor(np.eye(128, dtype=ml_dtypes.bfloat16),
                                name="identB_c").ap()
    identF_h = nc.inline_tensor(np.eye(128, dtype=np.float32), name="identF_c").ap()

    ctx = ExitStack()
    with ctx:
        EC = ctx.enter_context

        T = Stream(EC(nc.semaphore("sT")), name="T")
        A = Stream(EC(nc.semaphore("sA")), name="A")
        V = Stream(EC(nc.semaphore("sV")), name="V")
        G = Stream(EC(nc.semaphore("sG")), name="G")
        DS = Stream(EC(nc.semaphore("sDS")), 16, name="DS")
        DG = Stream(EC(nc.semaphore("sDG")), 16, name="DG")
        STO = Stream(EC(nc.semaphore("sSTO")), 16, name="STO")

        def dstream(nm):
            return Stream(EC(nc.semaphore("s" + nm)), 16, name=nm)

        CST = dstream("CST")
        WAT = dstream("WAT")
        WOT = dstream("WOT")
        INIT = dstream("INIT")
        XB = [dstream(f"XB{i}") for i in range(2)]
        XTS = dstream("XTS")
        XTB = [dstream(f"XTB{i}") for i in range(2)]
        YSS = dstream("YSS")
        YB = dstream("YB")
        ACC = dstream("ACC")
        XF = [dstream(f"XF{i}") for i in range(2)]
        WR = dstream("WR")
        WQ1 = [dstream(f"WQ1{i}") for i in range(2)]
        WQ2 = [dstream(f"WQ2{i}") for i in range(2)]
        W3H = dstream("W3H")
        OB = [dstream(f"OB{i}") for i in range(2)]
        OALLB = [dstream(f"OALLB{i}") for i in range(2)]
        STG = dstream("STG")
        ALLS = (T, A, V, G, DS, DG, CST, WAT, WOT, INIT, XB[0], XB[1], XTS,
                XTB[0], XTB[1], YSS, YB, ACC, XF[0], XF[1], WR, WQ1[0],
                WQ1[1], WQ2[0], WQ2[1], W3H, OB[0], OB[1], OALLB[0],
                OALLB[1], STG)

        S = Sched()

        def barrier(extra=()):
            toks = [(s, s.n) for s in ALLS] + list(extra)
            for e in ENGINES:
                S.do(e, wait=toks)

        identB = EC(nc.sbuf_tensor("identB", [128, 128], BF16))
        identF = EC(nc.sbuf_tensor("identF", [128, 128], F32))
        ones_p1 = EC(nc.sbuf_tensor("ones_p1", [128, 1], F32))
        ones_1p = EC(nc.sbuf_tensor("ones_1p", [1, 128], F32))
        onesB = EC(nc.sbuf_tensor("onesB", [128, 64], F32))
        cbias = EC(nc.sbuf_tensor("cbias", [128, 1], F32))
        rms_sb = EC(nc.sbuf_tensor("rms_sb", [128, NG], F32))
        gw2 = EC(nc.sbuf_tensor("gw2", [128, NG * E], F32))
        srow = EC(nc.sbuf_tensor("srow", [1, 16], F32))
        cst = EC(nc.sbuf_tensor("cst", [128, 2], F32))
        stage8 = EC(nc.sbuf_tensor("stage8", [1, E * 128], F32))
        gwT = EC(nc.sbuf_tensor("gwT", [128, NG * E], F32))

        S.do("gpsimd", lambda g: g.memset(cst[:, 0:1], EPS_LN), inc=[G])
        S.do("gpsimd", lambda g: g.memset(cst[:, 1:2], EPS_RMS), inc=[G])
        S.do("gpsimd", lambda g: g.memset(ones_p1[:, :], 1.0), inc=[G])
        S.do("gpsimd", lambda g: g.memset(ones_1p[:, :], 1.0), inc=[G])
        S.do("gpsimd", lambda g: g.memset(onesB[:, :], 1.0), inc=[G])
        tok_ones = G.n
        S.do("sync", lambda s: s.dma_start(out=identB[:, :], in_=identB_h), inc=[INIT])
        S.do("sync", lambda s: s.dma_start(out=identF[:, :], in_=identF_h), inc=[INIT])
        S.do("sync", lambda s: s.dma_start(out=rms_sb[:, :], in_=rms_in), inc=[INIT])
        S.do("sync", lambda s: s.dma_start(out=gwT[:, :], in_=gw_in), inc=[INIT])
        tok_small = INIT.n
        for g in range(NG):
            S.do("vector", lambda v, g=g: v.tensor_scalar_mul(
                gw2[:, g * E:(g + 1) * E], gwT[:, g * E:(g + 1) * E],
                rms_sb[:, g:g + 1]),
                wait=[(INIT, tok_small)], inc=[V])
        tok_gw2 = V.n
        with ExitStack() as pctx:
            PE = pctx.enter_context
            wo_sb = PE(nc.sbuf_tensor("wo_sb", [128, NG * 1024], BF16))
            QT = PE(nc.sbuf_tensor("QT", [128, NG * SEQ], BF16))
            KT = PE(nc.sbuf_tensor("KT", [128, SEQ], BF16))
            vaug = PE(nc.sbuf_tensor("vaug", [128, NM * 65], BF16))

            S.do("gpsimd", lambda g: g.memset(vaug[:, :], 1.0), inc=[G])
            tok_vinit = G.n

            with ExitStack() as p1c:
                P1 = p1c.enter_context
                wq_sb = P1(nc.sbuf_tensor("wq_sb", [128, NG * 1024], BF16))
                wk_sb = P1(nc.sbuf_tensor("wk_sb", [128, NG * KD], BF16))
                wv_sb = P1(nc.sbuf_tensor("wv_sb", [128, NG * KD], BF16))
                S.do("gpsimd", lambda g: g.dma_start(out=wq_sb[:, :], in_=wq_in),
                     inc=[WAT])
                S.do("gpsimd", lambda g: g.dma_start(out=wk_sb[:, :], in_=wk_in),
                     inc=[WAT])
                S.do("gpsimd", lambda g: g.dma_start(out=wv_sb[:, :], in_=wv_in),
                     inc=[WAT])
                tok_attnw = WAT.n
                S.do("gpsimd", lambda g: g.dma_start(out=wo_sb[:, :], in_=wo_in),
                     inc=[WOT])
                tok_wo = WOT.n
                xbuf = [P1(nc.sbuf_tensor(f"xbuf{i}", [128, 4 * 1024], F32))
                        for i in range(2)]
                ln1 = [P1(nc.sbuf_tensor(f"ln1_{i}", [128, 1024], BF16))
                       for i in range(2)]
                ln1T = [P1(nc.sbuf_tensor(f"ln1T{i}", [128, NG * 512], BF16))
                        for i in range(2)]
                mv = P1(nc.sbuf_tensor("mv", [128, 8], F32))
                sqx = P1(nc.sbuf_tensor("sqx", [128, 1024], F32))
                xtbuf = P1(nc.sbuf_tensor("xtbuf", [128, 512], F32))
                sqb = P1(nc.sbuf_tensor("sqb", [128, 512], F32))
                ksq = P1(nc.sbuf_tensor("ksq", [64, 512], F32))

                pq = [P1(nc.psum_tensor(f"pq{i}", [128, 512], F32))
                      for i in range(2)]
                pk = P1(nc.psum_tensor("pk", [128, 512], F32))
                pv = P1(nc.psum_tensor("pv", [128, 64], F32))
                ptr = [P1(nc.psum_tensor(f"ptr{i}", [128, 128], BF16))
                      for i in range(2)]
                pst = P1(nc.psum_tensor("pst", [1, 512], F32))
                pst2 = P1(nc.psum_tensor("pst2", [1, 512], F32))

                b_ln1 = [Buf(), Buf()]
                b_xbuf = [Buf(), Buf()]
                b_ln1T = [Buf(), Buf()]
                b_srow = Buf()
                b_ptr = [Buf() for _ in range(2)]
                b_pq = [Buf(), Buf()]
                b_sq = Buf()
                b_pst = Buf()
                b_pst2 = Buf()
                b_pk = Buf()
                b_ksq = Buf()
                b_pv = Buf()

                for c in range(NCH if phases >= 1 else 0):
                    cb = c % 2
                    pre_ln1T = b_ln1T[cb].wr()
                    S.do("sync", lambda s, c=c, cb=cb: s.dma_start(
                        out=xbuf[cb].ap().rearrange("p (t d) -> p t d", t=4),
                        in_=x_in[c * CH:(c + 1) * CH, :]
                        .rearrange("(t p) d -> p t d", p=128)),
                        wait=[(INIT, tok_small)] + b_xbuf[cb].wr(),
                        inc=[XB[cb]])
                    b_xbuf[cb].note_w(XB[cb])
                    tok_x = XB[cb].n
                    for t in range(4):
                        lb = t % 2
                        S.do("vector", lambda v, t=t, cb=cb: v.reduce_sum(
                            mv[:, 6:7], xbuf[cb][:, t * 1024:(t + 1) * 1024],
                            axis=AX.X),
                            wait=[(XB[cb], tok_x), (V, V.n)], inc=[V])
                        S.do("vector", lambda v, t=t, cb=cb: v.scalar_tensor_tensor(
                            sqx[:, :], xbuf[cb][:, t * 1024:(t + 1) * 1024], 1.0,
                            xbuf[cb][:, t * 1024:(t + 1) * 1024], ALU.mult,
                            ALU.mult, accum_out=mv[:, 7:8]),
                            wait=[(V, V.n)], inc=[V])
                        S.do("vector", lambda v: v.tensor_scalar_mul(
                            mv[:, 0:1], mv[:, 6:7], 1.0 / D),
                            wait=[(V, V.n)], inc=[V])
                        S.do("vector", lambda v: v.tensor_scalar_mul(
                            mv[:, 5:6], mv[:, 7:8], 1.0 / D),
                            wait=[(V, V.n)], inc=[V])
                        S.do("vector", lambda v: v.scalar_tensor_tensor(
                            mv[:, 4:5], mv[:, 0:1], 1.0, mv[:, 0:1],
                            ALU.mult, ALU.mult),
                            wait=[(V, V.n)], inc=[V])
                        S.do("vector", lambda v: v.tensor_sub(
                            mv[:, 1:2], mv[:, 5:6], mv[:, 4:5]),
                            wait=[(V, V.n)], inc=[V])
                        S.do("scalar", lambda s: s.activation(
                            mv[:, 4:5], mv[:, 1:2], AF.Ln, bias=cst[:, 0:1]),
                            wait=[(V, V.n), (A, A.n), (G, tok_ones)], inc=[A])
                        S.do("scalar", lambda s: s.activation(
                            mv[:, 2:3], mv[:, 4:5], AF.Exp, scale=-0.5),
                            wait=[(A, A.n)], inc=[A])
                        S.do("vector", lambda v: v.scalar_tensor_tensor(
                            mv[:, 3:4], mv[:, 0:1], -1.0, mv[:, 2:3],
                            ALU.mult, ALU.mult),
                            wait=[(A, A.n), (V, V.n)], inc=[V])
                        S.do("vector", lambda v, t=t, lb=lb, cb=cb: v.tensor_scalar(
                            ln1[lb][:, :], xbuf[cb][:, t * 1024:(t + 1) * 1024],
                            mv[:, 2:3], mv[:, 3:4], ALU.mult, ALU.add),
                            wait=[(V, V.n)] + b_ln1[lb].wr(), inc=[V])
                        b_ln1[lb].note_w(V)
                        b_xbuf[cb].note_r(V)
                        for gg in range(NG if 'tr' in p1s else 0):
                            pb = gg % 2
                            S.do("tensor", lambda t_, t=t, gg=gg, lb=lb, pb=pb:
                                 t_.transpose(
                                     ptr[pb][:, :],
                                     ln1[lb][:, gg * 128:(gg + 1) * 128],
                                     identB[:, :]),
                                 wait=b_ln1[lb].rd() + b_ptr[pb].wr(), inc=[T])
                            b_ptr[pb].note_w(T)
                            b_ln1[lb].note_r(T)
                            S.do("vector", lambda v, t=t, gg=gg, pb=pb: v.tensor_copy(
                                out=ln1T[cb][:, gg * 512 + t * 128:
                                             gg * 512 + (t + 1) * 128],
                                in_=ptr[pb][:, :]),
                                wait=b_ptr[pb].rd() + pre_ln1T, inc=[V])
                            b_ptr[pb].note_r(V)
                            b_ln1T[cb].note_w(V)
                    tok_ln1T = V.n

                    for hp in range(NG if 'q' in p1s else 0):
                        pqb = hp % 2
                        for gg in range(NG):
                            S.do("tensor", lambda t_, hp=hp, gg=gg, pqb=pqb: t_.matmul(
                                pq[pqb][:, :],
                                wq_sb[:, gg * 1024 + hp * 128:
                                      gg * 1024 + (hp + 1) * 128],
                                ln1T[cb][:, gg * 512:(gg + 1) * 512],
                                start=(gg == 0), stop=(gg == NG - 1)),
                                wait=[(V, tok_ln1T), (WAT, tok_attnw)]
                                + (b_pq[pqb].wr() if gg == 0 else []),
                                inc=[T] if gg == NG - 1 else [])
                        b_pq[pqb].note_w(T)
                        b_ln1T[cb].note_r(T)
                        S.do("scalar", lambda s, hp=hp, c=c, pqb=pqb: s.copy(
                            out=QT[:, hp * SEQ + c * CH: hp * SEQ + (c + 1) * CH],
                            in_=pq[pqb][:, :]),
                            wait=b_pq[pqb].rd(), inc=[A])
                        S.do("scalar", lambda s, pqb=pqb: s.square(
                            sqb[:, :], pq[pqb][:, :]),
                            wait=b_sq.wr(), inc=[A])
                        b_pq[pqb].note_r(A)
                        b_sq.note_w(A)
                        S.do("tensor", lambda t_, hp=hp: t_.matmul(
                            pst[:, :], ones_p1[:, :], sqb[:, :],
                            start=(hp == 0), stop=(hp == NG - 1)),
                            wait=b_sq.rd() + (b_pst.wr() if hp == 0 else []),
                            inc=[T])
                        b_sq.note_r(T)
                        b_pst.note_w(T)
                    if 'q' not in p1s:
                        continue
                    S.do("vector", lambda v: v.reduce_max(
                        srow[0:1, 6:7], pst[:, :], axis=AX.X),
                        wait=b_pst.rd() + [(V, V.n)], inc=[V])
                    b_pst.note_r(V)
                    if c == 0:
                        S.do("vector", lambda v: v.tensor_copy(
                            out=srow[0:1, 0:1], in_=srow[0:1, 6:7]),
                            wait=[(V, V.n)], inc=[V])
                    else:
                        S.do("vector", lambda v: v.tensor_max(
                            srow[0:1, 0:1], srow[0:1, 0:1], srow[0:1, 6:7]),
                            wait=[(V, V.n)], inc=[V])

                    for gg in range(NG if 'k' in p1s else 0):
                        S.do("tensor", lambda t_, gg=gg: t_.matmul(
                            pk[0:64, :], wk_sb[:, gg * KD:(gg + 1) * KD],
                            ln1T[cb][:, gg * 512:(gg + 1) * 512],
                            start=(gg == 0), stop=(gg == NG - 1),
                            tile_position=(0, 0)),
                            wait=[(V, tok_ln1T)] + (b_pk.wr() if gg == 0 else []),
                            inc=[T] if gg == NG - 1 else [])
                    for gg in range(NG if 'k' in p1s else 0):
                        S.do("tensor", lambda t_, gg=gg: t_.matmul(
                            pk[64:128, :], wk_sb[:, gg * KD:(gg + 1) * KD],
                            ln1T[cb][:, gg * 512:(gg + 1) * 512],
                            start=(gg == 0), stop=(gg == NG - 1),
                            tile_position=(0, 64)),
                            inc=[T] if gg == NG - 1 else [])
                    b_pk.note_w(T)
                    b_ln1T[cb].note_r(T)
                    if 'k' not in p1s:
                        continue
                    S.do("scalar", lambda s, c=c: s.copy(
                        out=KT[:, c * CH:(c + 1) * CH], in_=pk[:, :]),
                        wait=b_pk.rd(), inc=[A])
                    S.do("scalar", lambda s: s.square(ksq[:, :], pk[0:64, :]),
                         wait=b_ksq.wr() + [(A, A.n)], inc=[A])
                    b_pk.note_r(A)
                    b_ksq.note_w(A)
                    S.do("tensor", lambda t_: t_.matmul(
                        pst2[:, :], ones_p1[0:64, :], ksq[:, :],
                        start=True, stop=True),
                        wait=b_ksq.rd() + b_pst2.wr(), inc=[T])
                    b_ksq.note_r(T)
                    b_pst2.note_w(T)
                    S.do("vector", lambda v: v.reduce_max(
                        srow[0:1, 7:8], pst2[:, :], axis=AX.X),
                        wait=b_pst2.rd() + [(V, V.n)], inc=[V])
                    b_pst2.note_r(V)
                    if c == 0:
                        S.do("vector", lambda v: v.tensor_copy(
                            out=srow[0:1, 1:2], in_=srow[0:1, 7:8]),
                            wait=[(V, V.n)], inc=[V])
                    else:
                        S.do("vector", lambda v: v.tensor_max(
                            srow[0:1, 1:2], srow[0:1, 1:2], srow[0:1, 7:8]),
                            wait=[(V, V.n)], inc=[V])

                    b_xtb = Buf()
                    for gg in range(NG):
                        pb = gg % 2
                        for t in range(4):
                            S.do("tensor", lambda t_, gg=gg, t=t, pb=pb, cb=cb:
                                 t_.transpose(
                                     pq[pb][:, t * 128:(t + 1) * 128],
                                     xbuf[cb][:, t * 1024 + gg * 128:
                                              t * 1024 + (gg + 1) * 128],
                                     identF[:, :]),
                                 wait=[(XB[cb], tok_x)]
                                 + (b_pq[pb].wr() if t == 0 else []),
                                 inc=[T] if t == 3 else [])
                        b_pq[pb].note_w(T)
                        b_xbuf[cb].note_r(T)
                        S.do("vector", lambda v, gg=gg, pb=pb: v.tensor_copy(
                            out=xtbuf[:, :], in_=pq[pb][:, :]),
                            wait=b_pq[pb].rd() + b_xtb.wr(), inc=[V])
                        b_pq[pb].note_r(V)
                        b_xtb.note_w(V)
                        S.do("sync", lambda s, gg=gg, c=c: s.dma_start(
                            out=xts[gg][:, c * CH:(c + 1) * CH],
                            in_=xtbuf[:, :]),
                            wait=b_xtb.rd() + [(V, V.n)], inc=[XTS])
                        b_xtb.note_r(XTS)

                    for mt in range(4 if 'v' in p1s else 0):
                        m_abs = c * 4 + mt
                        for gg in range(NG):
                            S.do("tensor", lambda t_, gg=gg, mt=mt: t_.matmul(
                                pv[:, :],
                                ln1T[cb][:, gg * 512 + mt * 128:
                                         gg * 512 + (mt + 1) * 128],
                                wv_sb[:, gg * KD:(gg + 1) * KD],
                                start=(gg == 0), stop=(gg == NG - 1)),
                                wait=[(V, tok_ln1T), (G, tok_vinit)]
                                + (b_pv.wr() if gg == 0 else []),
                                inc=[T] if gg == NG - 1 else [])
                        b_pv.note_w(T)
                        b_ln1T[cb].note_r(T)
                        S.do("scalar", lambda s, m_abs=m_abs: s.copy(
                            out=vaug[:, m_abs * 65: m_abs * 65 + 64], in_=pv[:, :]),
                            wait=b_pv.rd(), inc=[A])
                        b_pv.note_r(A)

                tok_cbias = A.n
                if phases >= 2:
                    barrier()
                    S.do("vector", lambda v: v.tensor_mul(
                        srow[0:1, 2:3], srow[0:1, 0:1], srow[0:1, 1:2]),
                        wait=[(V, V.n)], inc=[V])
                    S.do("scalar", lambda s: s.activation(
                        srow[0:1, 3:4], srow[0:1, 2:3], AF.Ln),
                        wait=[(V, V.n)], inc=[A])
                    S.do("scalar", lambda s: s.activation(
                        srow[0:1, 4:5], srow[0:1, 3:4], AF.Exp, scale=0.5),
                        wait=[(A, A.n)], inc=[A])
                    S.do("vector", lambda v: v.tensor_scalar_mul(
                        srow[0:1, 5:6], srow[0:1, 4:5], -0.125),
                        wait=[(A, A.n)], inc=[V])
                    S.do("tensor", lambda t_: t_.matmul(
                        pv[:, 0:1], ones_1p[:, :], srow[0:1, 5:6],
                        start=True, stop=True),
                        wait=[(V, V.n)] + b_pv.wr(), inc=[T])
                    b_pv.note_w(T)
                    S.do("scalar", lambda s: s.copy(out=cbias[:, :], in_=pv[:, 0:1]),
                         wait=b_pv.rd(), inc=[A])
                    b_pv.note_r(A)
                    tok_cbias = A.n

            if "qt" in dbg:
                ap = dbg_tensor("qt", [128, NG * SEQ], BF16)
                S.do("sync", lambda s, ap=ap: s.dma_start(out=ap, in_=QT[:, :]),
                     wait=[(A, A.n), (T, T.n), (V, V.n)], inc=[STO])
            if "kt" in dbg:
                ap = dbg_tensor("kt", [128, SEQ], BF16)
                S.do("sync", lambda s, ap=ap: s.dma_start(out=ap, in_=KT[:, :]),
                     wait=[(A, A.n), (T, T.n), (V, V.n)], inc=[STO])
            if "vaug" in dbg:
                ap = dbg_tensor("vaug", [128, NM * 65], BF16)
                S.do("sync", lambda s, ap=ap: s.dma_start(out=ap, in_=vaug[:, :]),
                     wait=[(A, A.n), (T, T.n), (V, V.n)], inc=[STO])
            if "cbias" in dbg:
                ap = dbg_tensor("cbias", [128, 1])
                S.do("sync", lambda s, ap=ap: s.dma_start(out=ap, in_=cbias[:, :]),
                     wait=[(A, A.n)], inc=[STO])

            with ExitStack() as p3c:
                P3 = p3c.enter_context
                Epair = [P3(nc.sbuf_tensor(f"Ep{i}", [128, 1024], BF16))
                         for i in range(3)]
                o_all = [P3(nc.sbuf_tensor(f"o_all{i}", [128, NG * 512], BF16))
                         for i in range(2)]
                tmpo = P3(nc.sbuf_tensor("tmpo", [64, 512], F32))
                tmpoB = P3(nc.sbuf_tensor("tmpoB", [64, 512], F32))
                otmpB = P3(nc.sbuf_tensor("otmpB", [64, 512], BF16))
                recr = P3(nc.sbuf_tensor("recr", [128, 512], F32))
                xtb = [P3(nc.sbuf_tensor(f"xtb{i}", [128, 512], F32))
                       for i in range(2)]
                attnT = P3(nc.sbuf_tensor("attnT", [128, NG * 512], F32))
                x2T = P3(nc.sbuf_tensor("x2T", [128, NG * 512], F32))
                zT = P3(nc.sbuf_tensor("zT", [128, NG * 512], F32))
                yTt = [P3(nc.sbuf_tensor(f"yTt{i}", [128, 512], BF16))
                       for i in range(2)]
                sAB = P3(nc.sbuf_tensor("sAB", [128, 1024], F32))
                rowA = P3(nc.sbuf_tensor("rowA", [1, 512], F32))
                rowB = P3(nc.sbuf_tensor("rowB", [1, 512], F32))
                rowC = P3(nc.sbuf_tensor("rowC", [1, 512], F32))
                sq3 = [P3(nc.sbuf_tensor(f"sq3{i}", [128, 512], F32))
                       for i in range(2)]
                tv1 = P3(nc.sbuf_tensor("tv1", [128, 512], F32))
                lg8 = P3(nc.sbuf_tensor("lg8", [8, 512], F32))
                lgt = P3(nc.sbuf_tensor("lgt", [128, 4 * 8], F32))
                Ct = P3(nc.sbuf_tensor("Ct", [128, 4], F32))
                gt = P3(nc.sbuf_tensor("gt", [128, 10 * 8], F32))

                sp = [P3(nc.psum_tensor(f"sp{i}", [128, 1024], F32))
                      for i in range(2)]
                poA = P3(nc.psum_tensor("poA", [65, 512], F32))
                poB = P3(nc.psum_tensor("poB", [65, 512], F32))
                pr = P3(nc.psum_tensor("pr", [64, 512], F32))
                ptl = P3(nc.psum_tensor("ptl", [128, 512], F32))

                b_sp = [Buf(), Buf()]
                b_Ep = [Buf(), Buf(), Buf()]
                b_po = Buf()
                b_pr = Buf()
                b_ptl = Buf()
                b_tmpo = Buf()
                b_tmpoB = Buf()
                b_otmpB = Buf()
                b_oall = [Buf(), Buf()]
                b_oall_d = [0, 0]
                b_attnT = Buf()
                b_x2T = Buf()
                b_zT = Buf()
                b_sq3 = [Buf(), Buf()]
                b_rowA = Buf()
                b_rowB = Buf()
                b_rowC = Buf()
                b_sAB = Buf()
                b_tv1 = Buf()
                b_xtb3 = [Buf(), Buf()]
                b_recr = Buf()
                b_lg8 = Buf()
                b_lgt = Buf()
                b_Ct = Buf()
                b_yTt = [Buf(), Buf()]
                b_gt = Buf()
                b_stage_dma = [0]

                def sched_sim(c, hp, m):
                    par = m % 2
                    ep = (hp * NM + m) % 3
                    S.do("tensor", lambda t_, m=m, hp=hp, c=c, par=par:
                         t_.matmul(
                             sp[par][:, 0:512],
                             KT[0:64, m * 128:(m + 1) * 128],
                             QT[0:64, hp * SEQ + c * CH:
                                hp * SEQ + (c + 1) * CH],
                             start=True, stop=True, tile_position=(0, 0)),
                         wait=b_sp[par].wr())
                    S.do("tensor", lambda t_, m=m, hp=hp, c=c, par=par:
                         t_.matmul(
                             sp[par][:, 512:1024],
                             KT[64:128, m * 128:(m + 1) * 128],
                             QT[64:128, hp * SEQ + c * CH:
                                hp * SEQ + (c + 1) * CH],
                             start=True, stop=True, tile_position=(64, 0)),
                         inc=[T])
                    b_sp[par].note_w(T)
                    S.do("scalar", lambda s, par=par, ep=ep: s.activation(
                        Epair[ep][:, 0:512], sp[par][:, 0:512], AF.Exp,
                        bias=cbias[:, 0:1], scale=0.125),
                        wait=b_sp[par].rd() + b_Ep[ep].wr(), inc=[A])
                    b_Ep[ep].note_w(A)
                    S.do("scalar", lambda s, par=par, ep=ep: s.activation(
                        Epair[ep][:, 512:1024], sp[par][:, 512:1024], AF.Exp,
                        bias=cbias[:, 0:1], scale=0.125), inc=[A])
                    b_sp[par].note_r(A)
                    b_Ep[ep].note_w(A)

                def sched_o(c, hp, m):
                    ep = (hp * NM + m) % 3
                    wa = b_Ep[ep].rd()
                    wa = [(s, v - 1) if s is A else (s, v) for s, v in wa]
                    S.do("tensor", lambda t_, m=m, ep=ep: t_.matmul(
                        poA[:, :], vaug[:, m * 65:(m + 1) * 65],
                        Epair[ep][:, 0:512],
                        start=(m == 0), stop=(m == NM - 1)),
                        wait=wa + (b_po.wr() if m == 0 else []))
                    S.do("tensor", lambda t_: t_.nop(),
                         wait=b_Ep[ep].rd())
                    S.do("tensor", lambda t_, m=m, ep=ep: t_.matmul(
                        poB[:, :], vaug[:, m * 65:(m + 1) * 65],
                        Epair[ep][:, 512:1024],
                        start=(m == 0), stop=(m == NM - 1)),
                        inc=[T])
                    b_Ep[ep].note_r(T)
                    b_po.note_w(T)

                def sched_norm(c, hp):
                    ob = c % 2
                    S.do("vector", lambda v: v.reciprocal(
                        recr[64:65, :], poA[64:65, :]),
                        wait=b_po.rd() + b_recr.wr(), inc=[V])
                    b_recr.note_w(V)
                    S.do("scalar", lambda s: s.copy(
                        out=tmpo[:, :], in_=poA[0:64, :]),
                        wait=b_po.rd() + b_tmpo.wr() + [(V, V.n)], inc=[A])
                    S.do("scalar", lambda s: s.copy(
                        out=tmpoB[:, :], in_=poB[0:64, :]),
                        wait=b_tmpoB.wr(), inc=[A])
                    b_po.note_w({}) if False else None
                    b_tmpo.note_w(A)
                    b_tmpoB.note_w(A)
                    S.do("tensor", lambda t_: t_.matmul(
                        pr[:, :], onesB[64:65, :], recr[64:65, :],
                        start=True, stop=True, tile_position=(64, 0)),
                        wait=b_recr.rd() + b_pr.wr() + [(G, tok_ones)],
                        inc=[T])
                    b_pr.note_w(T)
                    b_recr.note_r(T)
                    S.do("vector", lambda v, hp=hp, ob=ob: v.tensor_mul(
                        o_all[ob][0:64, hp * 512:(hp + 1) * 512],
                        tmpo[:, :], pr[:, :]),
                        wait=b_pr.rd() + b_tmpo.rd() + pre_oall[ob],
                        inc=[V])
                    b_pr.note_r(V)
                    b_tmpo.note_r(V)
                    b_oall[ob].note_w(V)
                    S.do("vector", lambda v: v.reciprocal(
                        recr[64:65, :], poB[64:65, :]),
                        wait=b_recr.wr() + [(V, V.n)], inc=[V])
                    b_recr.note_w(V)
                    b_po.note_r(A, V)
                    S.do("tensor", lambda t_: t_.matmul(
                        pr[:, :], onesB[64:65, :], recr[64:65, :],
                        start=True, stop=True, tile_position=(64, 0)),
                        wait=b_recr.rd() + b_pr.wr(), inc=[T])
                    b_recr.note_r(T)
                    b_pr.note_w(T)
                    S.do("vector", lambda v: v.tensor_mul(
                        otmpB[:, :], tmpoB[:, :], pr[:, :]),
                        wait=b_pr.rd() + b_tmpoB.rd() + b_otmpB.wr(),
                        inc=[V])
                    b_pr.note_r(V)
                    b_tmpoB.note_r(V)
                    b_otmpB.note_w(V)
                    S.do("sync", lambda s, hp=hp, ob=ob: s.dma_start(
                        out=o_all[ob][64:128, hp * 512:(hp + 1) * 512],
                        in_=otmpB[:, :]),
                        wait=b_otmpB.rd() + [(V, V.n)] + pre_oall[ob],
                        inc=[OALLB[ob]])
                    b_otmpB.note_r(OALLB[ob])
                    b_oall[ob].note_w(V, OALLB[ob])
                    b_oall_d[ob] = OALLB[ob].n

                def ln_rows(eps):
                    S.do("vector", lambda v: v.tensor_scalar_mul(
                        rowA[0:1, :], rowA[0:1, :], 1.0 / D),
                        wait=b_rowA.rd() + [(V, V.n)], inc=[V])
                    b_rowA.note_w(V)
                    S.do("vector", lambda v: v.tensor_scalar_mul(
                        rowB[0:1, :], rowB[0:1, :], 1.0 / D),
                        wait=b_rowB.rd() + [(V, V.n)], inc=[V])
                    b_rowB.note_w(V)
                    S.do("vector", lambda v: v.scalar_tensor_tensor(
                        rowC[0:1, :], rowA[0:1, :], 1.0, rowA[0:1, :],
                        ALU.mult, ALU.mult),
                        wait=b_rowA.rd() + b_rowC.wr(), inc=[V])
                    b_rowC.note_w(V)
                    S.do("vector", lambda v: v.tensor_sub(
                        rowB[0:1, :], rowB[0:1, :], rowC[0:1, :]),
                        wait=b_rowC.rd() + [(V, V.n)], inc=[V])
                    b_rowC.note_r(V)
                    b_rowB.note_w(V)
                    S.do("scalar", lambda s, eps=eps: s.activation(
                        rowC[0:1, :], rowB[0:1, :], AF.Ln, bias=eps),
                        wait=b_rowB.rd() + b_rowC.wr(), inc=[A])
                    b_rowB.note_r(A)
                    b_rowC.note_w(A)
                    S.do("scalar", lambda s: s.activation(
                        rowB[0:1, :], rowC[0:1, :], AF.Exp, scale=-0.5),
                        wait=b_rowC.rd() + b_rowB.wr() + [(A, A.n)], inc=[A])
                    b_rowC.note_r(A)
                    b_rowB.note_w(A)
                    S.do("vector", lambda v: v.scalar_tensor_tensor(
                        rowC[0:1, :], rowA[0:1, :], -1.0, rowB[0:1, :],
                        ALU.mult, ALU.mult),
                        wait=b_rowB.rd() + b_rowA.rd() + b_rowC.wr(), inc=[V])
                    b_rowA.note_r(V)
                    b_rowB.note_r(V)
                    b_rowC.note_w(V)

                def bcast_ab():
                    S.do("tensor", lambda t_: t_.matmul(
                        ptl[:, :], ones_1p[:, :], rowB[0:1, :],
                        start=True, stop=True),
                        wait=b_rowB.rd() + b_ptl.wr(), inc=[T])
                    b_rowB.note_r(T)
                    b_ptl.note_w(T)
                    S.do("scalar", lambda s: s.copy(out=sAB[:, 0:512],
                                                    in_=ptl[:, :]),
                         wait=b_ptl.rd() + b_sAB.wr(), inc=[A])
                    b_ptl.note_r(A)
                    b_sAB.note_w(A)
                    S.do("tensor", lambda t_: t_.matmul(
                        ptl[:, :], ones_1p[:, :], rowC[0:1, :],
                        start=True, stop=True),
                        wait=b_rowC.rd() + b_ptl.wr(), inc=[T])
                    b_rowC.note_r(T)
                    b_ptl.note_w(T)
                    S.do("scalar", lambda s: s.copy(out=sAB[:, 512:1024],
                                                    in_=ptl[:, :]),
                         wait=b_ptl.rd() + [(A, A.n)], inc=[A])
                    b_ptl.note_r(A)
                    b_sAB.note_w(A)

                def stats_pass(src_buf, b_src, dst_row, b_dst):
                    for i in range(NG):
                        S.do("tensor", lambda t_, i=i: t_.matmul(
                            ptl[0:1, :], ones_p1[:, :],
                            src_buf[:, i * 512:(i + 1) * 512],
                            start=(i == 0), stop=(i == NG - 1)),
                            wait=b_src.rd()
                            + (b_ptl.wr() if i == 0 else []),
                            inc=[T] if i == NG - 1 else [])
                    b_ptl.note_w(T)
                    b_src.note_r(T)
                    S.do("vector", lambda v: v.tensor_copy(
                        out=dst_row[0:1, :], in_=ptl[0:1, :]),
                        wait=b_ptl.rd() + b_dst.wr(), inc=[V])
                    b_ptl.note_r(V)
                    b_dst.note_w(V)

                def sq_stats_pass(src_buf, b_src, dst_row, b_dst):
                    for i in range(NG):
                        qb = i % 2
                        S.do("scalar", lambda s, i=i, qb=qb: s.square(
                            sq3[qb][:, :], src_buf[:, i * 512:(i + 1) * 512]),
                            wait=b_src.rd() + b_sq3[qb].wr(), inc=[A])
                        b_sq3[qb].note_w(A)
                        b_src.note_r(A)
                        S.do("tensor", lambda t_, i=i, qb=qb: t_.matmul(
                            ptl[0:1, :], ones_p1[:, :], sq3[qb][:, :],
                            start=(i == 0), stop=(i == NG - 1)),
                            wait=b_sq3[qb].rd()
                            + (b_ptl.wr() if i == 0 else []),
                            inc=[T])
                        b_sq3[qb].note_r(T)
                    b_ptl.note_w(T)
                    S.do("vector", lambda v: v.tensor_copy(
                        out=dst_row[0:1, :], in_=ptl[0:1, :]),
                        wait=b_ptl.rd() + b_dst.wr(), inc=[V])
                    b_ptl.note_r(V)
                    b_dst.note_w(V)

                def tail_gen(c):
                    ob = c % 2

                    tok_xts = (c + 1) * NG * 16

                    def xt_load(i, c=c, tok_xts=0):
                        ib = i % 2
                        S.do("sync", lambda s, i=i, ib=ib: s.dma_start(
                            out=xtb[ib][:, :],
                            in_=xts[i][:, c * CH:(c + 1) * CH]),
                            wait=b_xtb3[ib].wr() + [(XTS, min(tok_xts, XTS.n))],
                            inc=[XTB[ib]])
                        b_xtb3[ib].note_w(XTB[ib])
                    tok_oall = b_oall[ob].w.get(V, 0)
                    tok_oall_d = b_oall_d[ob]
                    pre_attnT = b_attnT.wr()
                    for i in range(NG):
                        for g2 in range(NG):
                            S.do("tensor", lambda t_, i=i, g2=g2, ob=ob:
                                 t_.matmul(
                                     ptl[:, :],
                                     wo_sb[:, g2 * 1024 + i * 128:
                                           g2 * 1024 + (i + 1) * 128],
                                     o_all[ob][:, g2 * 512:(g2 + 1) * 512],
                                     start=(g2 == 0), stop=(g2 == NG - 1)),
                                 wait=[(V, tok_oall), (OALLB[ob], tok_oall_d),
                                       (WOT, tok_wo)]
                                 + (b_ptl.wr() if g2 == 0 else []),
                                 inc=[T] if g2 == NG - 1 else [])
                        b_ptl.note_w(T)
                        b_oall[ob].note_r(T)
                        S.do("vector", lambda v, i=i: v.tensor_copy(
                            out=attnT[:, i * 512:(i + 1) * 512],
                            in_=ptl[:, :]),
                            wait=b_ptl.rd() + pre_attnT, inc=[V])
                        b_ptl.note_r(V)
                        b_attnT.note_w(V)
                        yield
                    stats_pass(attnT, b_attnT, rowA, b_rowA)
                    yield
                    sq_stats_pass(attnT, b_attnT, rowB, b_rowB)
                    yield
                    ln_rows(cst[0:1, 0:1])
                    yield
                    bcast_ab()
                    yield
                    pre_x2T = b_x2T.wr()
                    xt_load(0, tok_xts=tok_xts)
                    xt_load(1, tok_xts=tok_xts)
                    for i in range(NG):
                        ib = i % 2
                        S.do("vector", lambda v, i=i: v.tensor_mul(
                            tv1[:, :], attnT[:, i * 512:(i + 1) * 512],
                            sAB[:, 0:512]),
                            wait=b_sAB.rd() + b_attnT.rd() + b_tv1.wr(),
                            inc=[V])
                        b_attnT.note_r(V)
                        b_sAB.note_r(V)
                        b_tv1.note_w(V)
                        S.do("vector", lambda v: v.tensor_add(
                            tv1[:, :], tv1[:, :], sAB[:, 512:1024]),
                            wait=b_tv1.rd() + [(V, V.n)], inc=[V])
                        b_sAB.note_r(V)
                        b_tv1.note_w(V)
                        S.do("vector", lambda v, i=i, ib=ib: v.tensor_add(
                            x2T[:, i * 512:(i + 1) * 512], tv1[:, :],
                            xtb[ib][:, :]),
                            wait=b_tv1.rd() + b_xtb3[ib].rd() + pre_x2T,
                            inc=[V])
                        b_tv1.note_r(V)
                        b_xtb3[ib].note_r(V)
                        b_x2T.note_w(V)
                        if i + 2 < NG:
                            xt_load(i + 2, tok_xts=tok_xts)
                        S.do("sync", lambda s, i=i, c=c: s.dma_start(
                            out=x2s[i][:, c * CH:(c + 1) * CH],
                            in_=x2T[:, i * 512:(i + 1) * 512]),
                            wait=[(V, V.n)], inc=[DS])
                        b_x2T.note_r(DS)
                        yield
                    stats_pass(x2T, b_x2T, rowA, b_rowA)
                    yield
                    sq_stats_pass(x2T, b_x2T, rowB, b_rowB)
                    yield
                    ln_rows(cst[0:1, 0:1])
                    yield
                    bcast_ab()
                    yield
                    pre_zT = b_zT.wr()
                    for i in range(NG):
                        S.do("vector", lambda v, i=i: v.tensor_mul(
                            tv1[:, :], x2T[:, i * 512:(i + 1) * 512],
                            sAB[:, 0:512]),
                            wait=b_sAB.rd() + b_x2T.rd() + b_tv1.wr(),
                            inc=[V])
                        b_x2T.note_r(V)
                        b_sAB.note_r(V)
                        b_tv1.note_w(V)
                        S.do("vector", lambda v, i=i: v.tensor_add(
                            zT[:, i * 512:(i + 1) * 512], tv1[:, :],
                            sAB[:, 512:1024]),
                            wait=b_tv1.rd() + pre_zT, inc=[V])
                        b_tv1.note_r(V)
                        b_zT.note_w(V)
                        S.do("sync", lambda s, i=i, c=c: s.dma_start(
                            out=zs[i][:, c * CH:(c + 1) * CH],
                            in_=zT[:, i * 512:(i + 1) * 512]),
                            wait=[(V, V.n)], inc=[DS])
                        b_zT.note_r(DS)
                        yield
                    sq_stats_pass(zT, b_zT, rowB, b_rowB)
                    yield
                    S.do("vector", lambda v: v.tensor_scalar_mul(
                        rowB[0:1, :], rowB[0:1, :], 1.0 / D),
                        wait=b_rowB.rd() + [(V, V.n)], inc=[V])
                    b_rowB.note_w(V)
                    S.do("scalar", lambda s: s.activation(
                        rowC[0:1, :], rowB[0:1, :], AF.Ln, bias=cst[0:1, 1:2]),
                        wait=b_rowB.rd() + b_rowC.wr(), inc=[A])
                    b_rowB.note_r(A)
                    b_rowC.note_w(A)
                    S.do("scalar", lambda s: s.activation(
                        rowB[0:1, :], rowC[0:1, :], AF.Exp, scale=-0.5),
                        wait=b_rowC.rd() + b_rowB.wr() + [(A, A.n)], inc=[A])
                    b_rowC.note_r(A)
                    b_rowB.note_w(A)
                    yield
                    S.do("tensor", lambda t_: t_.matmul(
                        ptl[:, :], ones_1p[:, :], rowB[0:1, :],
                        start=True, stop=True),
                        wait=b_rowB.rd() + b_ptl.wr(), inc=[T])
                    b_ptl.note_w(T)
                    S.do("scalar", lambda s: s.copy(
                        out=sAB[:, 0:512], in_=ptl[:, :]),
                        wait=b_ptl.rd() + b_sAB.wr(), inc=[A])
                    b_ptl.note_r(A)
                    b_sAB.note_w(A)
                    yield
                    for i in range(NG):
                        ib = i % 2
                        S.do("vector", lambda v, i=i: v.tensor_mul(
                            tv1[:, :], zT[:, i * 512:(i + 1) * 512],
                            sAB[:, 0:512]),
                            wait=b_sAB.rd() + b_zT.rd() + b_tv1.wr(), inc=[V])
                        b_zT.note_r(V)
                        b_sAB.note_r(V)
                        b_tv1.note_w(V)
                        S.do("vector", lambda v, i=i, ib=ib: v.tensor_scalar_mul(
                            yTt[ib][:, :], tv1[:, :], rms_sb[:, i:i + 1]),
                            wait=b_tv1.rd() + b_yTt[ib].wr(), inc=[V])
                        b_tv1.note_r(V)
                        b_yTt[ib].note_w(V)
                        S.do("gpsimd", lambda g, i=i, c=c, ib=ib: g.dma_start(
                            out=ys[i][:, c * CH:(c + 1) * CH],
                            in_=yTt[ib][:, :]),
                            wait=b_yTt[ib].rd(), inc=[YSS])
                        b_yTt[ib].note_r(YSS)
                        yield
                    for gg in range(NG):
                        S.do("tensor", lambda t_, gg=gg: t_.matmul(
                            ptl[0:8, :], gw2[:, gg * E:(gg + 1) * E],
                            zT[:, gg * 512:(gg + 1) * 512],
                            start=(gg == 0), stop=(gg == NG - 1)),
                            wait=b_zT.rd() + [(V, tok_gw2)]
                            + (b_ptl.wr() if gg == 0 else []),
                            inc=[T] if gg == NG - 1 else [])
                        b_zT.note_r(T)
                        b_ptl.note_w(T)
                    S.do("vector", lambda v: v.tensor_copy(
                        out=lg8[:, :], in_=ptl[0:8, :]),
                        wait=b_ptl.rd() + b_lg8.wr(), inc=[V])
                    b_ptl.note_r(V)
                    b_lg8.note_w(V)
                    yield
                    for tb in range(4):
                        S.do("tensor", lambda t_, tb=tb: t_.transpose(
                            ptl[:, tb:tb + 1],
                            rowB[0:1, tb * 128:(tb + 1) * 128],
                            identF[0:1, 0:1]),
                            wait=b_rowB.rd() + (b_ptl.wr() if tb == 0 else []),
                            inc=[T] if tb == 3 else [])
                    b_rowB.note_r(T)
                    b_ptl.note_w(T)
                    S.do("vector", lambda v: v.tensor_copy(
                        out=Ct[:, 0:4], in_=ptl[:, 0:4]),
                        wait=b_ptl.rd() + b_Ct.wr(), inc=[V])
                    b_ptl.note_r(V)
                    b_Ct.note_w(V)
                    yield
                    for tb in range(4):
                        S.do("tensor", lambda t_, tb=tb: t_.transpose(
                            ptl[:, 256 + tb * 8: 256 + (tb + 1) * 8],
                            lg8[:, tb * 128:(tb + 1) * 128],
                            identF[0:8, 0:8]),
                            wait=b_lg8.rd() + (b_ptl.wr() if tb == 0 else []),
                            inc=[T] if tb == 3 else [])
                    b_lg8.note_r(T)
                    b_ptl.note_w(T)
                    S.do("vector", lambda v: v.tensor_copy(
                        out=lgt[:, :], in_=ptl[:, 256:288]),
                        wait=b_ptl.rd() + b_lgt.wr(), inc=[V])
                    b_ptl.note_r(V)
                    b_lgt.note_w(V)
                    yield
                    for tb in range(4):
                        Lg = lgt[:, tb * 8:(tb + 1) * 8]
                        g0 = gt[:, 0:8]
                        top2 = gt[:, 8:16]
                        eq1 = gt[:, 16:24]
                        p1_ = gt[:, 24:32]
                        p2_ = gt[:, 32:40]
                        p3_ = gt[:, 40:48]
                        ffo1 = gt[:, 48:56]
                        eq2 = gt[:, 56:64]
                        ffo2 = gt[:, 64:72]
                        w12 = gt[:, 72:80]

                        def VD(fn, extra=()):
                            S.do("vector", fn,
                                 wait=[(V, V.n), (G, G.n)] + list(extra),
                                 inc=[V])

                        def GM(ap):
                            S.do("gpsimd", lambda g, ap=ap: g.memset(ap, 0.0),
                                 wait=[(V, V.n)], inc=[G])

                        VD(lambda v, tb=tb: v.tensor_scalar_mul(
                            g0, lgt[:, tb * 8:(tb + 1) * 8], Ct[:, tb:tb + 1]),
                           extra=b_Ct.rd() + b_lgt.rd() + b_gt.wr())
                        b_gt.note_w(V)
                        b_Ct.note_r(V)
                        VD(lambda v: v.max(top2, g0))
                        VD(lambda v: v.tensor_single_scalar(
                            eq1, g0, top2[:, 0:1], ALU.is_equal))
                        GM(p1_)
                        VD(lambda v: v.tensor_copy(out=p1_[:, 1:8],
                                                   in_=eq1[:, 0:7]))
                        GM(p2_)
                        VD(lambda v: v.tensor_copy(out=p2_[:, 0:2],
                                                   in_=p1_[:, 0:2]))
                        VD(lambda v: v.tensor_add(p2_[:, 2:8], p1_[:, 2:8],
                                                  p1_[:, 0:6]))
                        GM(p3_)
                        VD(lambda v: v.tensor_copy(out=p3_[:, 0:4],
                                                   in_=p2_[:, 0:4]))
                        VD(lambda v: v.tensor_add(p3_[:, 4:8], p2_[:, 4:8],
                                                  p2_[:, 0:4]))
                        VD(lambda v: v.tensor_single_scalar(
                            ffo1, p3_, 0.0, ALU.is_equal))
                        VD(lambda v: v.tensor_mul(ffo1, eq1, ffo1))
                        VD(lambda v: v.tensor_single_scalar(
                            eq2, g0, top2[:, 1:2], ALU.is_equal))
                        VD(lambda v: v.scalar_tensor_tensor(
                            eq2, ffo1, -1.0, eq2, ALU.mult, ALU.add))
                        VD(lambda v: v.tensor_scalar_max(eq2, eq2, 0.0))
                        GM(p1_)
                        VD(lambda v: v.tensor_copy(out=p1_[:, 1:8],
                                                   in_=eq2[:, 0:7]))
                        GM(p2_)
                        VD(lambda v: v.tensor_copy(out=p2_[:, 0:2],
                                                   in_=p1_[:, 0:2]))
                        VD(lambda v: v.tensor_add(p2_[:, 2:8], p1_[:, 2:8],
                                                  p1_[:, 0:6]))
                        GM(p3_)
                        VD(lambda v: v.tensor_copy(out=p3_[:, 0:4],
                                                   in_=p2_[:, 0:4]))
                        VD(lambda v: v.tensor_add(p3_[:, 4:8], p2_[:, 4:8],
                                                  p2_[:, 0:4]))
                        VD(lambda v: v.tensor_single_scalar(
                            ffo2, p3_, 0.0, ALU.is_equal))
                        VD(lambda v: v.tensor_mul(ffo2, eq2, ffo2))
                        VD(lambda v: v.tensor_sub(w12[:, 2:3], top2[:, 1:2],
                                                  top2[:, 0:1]))
                        S.do("scalar", lambda s: s.activation(
                            w12[:, 3:4], w12[:, 2:3], AF.Exp),
                            wait=[(V, V.n), (A, A.n)], inc=[A])
                        S.do("vector", lambda v: v.tensor_scalar_add(
                            w12[:, 4:5], w12[:, 3:4], 1.0),
                            wait=[(A, A.n)], inc=[V])
                        VD(lambda v: v.reciprocal(w12[:, 5:6], w12[:, 4:5]))
                        VD(lambda v: v.tensor_mul(w12[:, 6:7], w12[:, 3:4],
                                                  w12[:, 5:6]))
                        VD(lambda v: v.tensor_scalar_mul(ffo1, ffo1,
                                                         w12[:, 5:6]))
                        VD(lambda v: v.tensor_scalar_mul(ffo2, ffo2,
                                                         w12[:, 6:7]))
                        VD(lambda v: v.tensor_add(g0, ffo1, ffo2))
                        b_gt.note_w(V)
                        yield
                        for e in range(E):
                            S.do("tensor", lambda t_, e=e: t_.transpose(
                                ptl[0:1, 0:128], g0[:, e:e + 1],
                                identF[:, :]),
                                wait=b_gt.rd() + b_ptl.wr(), inc=[T])
                            b_gt.note_r(T)
                            b_ptl.note_w(T)
                            S.do("vector", lambda v, e=e: v.tensor_copy(
                                out=stage8[0:1, e * 128:(e + 1) * 128],
                                in_=ptl[0:1, 0:128]),
                                wait=b_ptl.rd() + [(STG, b_stage_dma[0])],
                                inc=[V])
                            b_ptl.note_r(V)
                        S.do("gpsimd", lambda g, tb=tb, c=c: g.dma_start(
                            out=wTe_s[:, c * CH + tb * 128:
                                      c * CH + (tb + 1) * 128],
                            in_=stage8[0:1, :].rearrange(
                                "p (e k) -> p e k", e=E)),
                            wait=[(V, V.n)], inc=[STG])
                        b_stage_dma[0] = STG.n
                        yield

                tail = None
                pre_oall = [[], []]
                for c in range(NCH if phases >= 3 else 0):
                    if c == 0:
                        barrier(extra=[(A, tok_cbias)])
                    pre_oall[c % 2] = b_oall[c % 2].wr()
                    for hp in range(NG if 'heads' in p3s else 0):
                        for m in range(NM):
                            sched_sim(c, hp, m)
                            if tail is not None and (hp * NM + m) % 4 == 0:
                                next(tail, None)
                            k = hp * NM + m - SKEW
                            if k >= 0:
                                pm = (k // NM, k % NM)
                                sched_o(c, *pm)
                                if pm[1] == NM - 1:
                                    sched_norm(c, pm[0])
                    for k in range(NG * NM - SKEW, NG * NM):
                        pm = (k // NM, k % NM)
                        sched_o(c, *pm)
                        if pm[1] == NM - 1:
                            sched_norm(c, pm[0])
                    if tail is not None:
                        for _ in tail:
                            pass
                    tail = tail_gen(c)
                if tail is not None:
                    for _ in tail:
                        pass

                if "x2t" in dbg:
                    ap = dbg_tensor("x2t", [NG, 128, SEQ])
                    S.do("sync", lambda s, ap=ap: s.dma_start(out=ap, in_=x2s),
                         wait=[(DS, DS.n), (V, V.n)], inc=[STO])
                if "zt" in dbg:
                    ap = dbg_tensor("zt", [NG, 128, SEQ])
                    S.do("sync", lambda s, ap=ap: s.dma_start(out=ap, in_=zs),
                         wait=[(DS, DS.n), (V, V.n)], inc=[STO])
                if "yt" in dbg:
                    ap = dbg_tensor("yt", [NG, 128, SEQ], BF16)
                    S.do("sync", lambda s, ap=ap: s.dma_start(out=ap, in_=ys),
                         wait=[(DG, DG.n), (V, V.n)], inc=[STO])
                if "wte" in dbg:
                    ap = dbg_tensor("wte", [E, SEQ])
                    S.do("sync", lambda s, ap=ap: s.dma_start(out=ap, in_=wTe_s),
                         wait=[(DG, DG.n)], inc=[STO])

        barrier()
        with ExitStack() as p4c:
            P4 = p4c.enter_context
            ybuf = P4(nc.sbuf_tensor("ybuf", [128, NG * FCH], BF16))
            acc = P4(nc.sbuf_tensor("acc", [128, NG * FCH], F32))
            hbuf = P4(nc.sbuf_tensor("hbuf", [128, 11 * FCH], BF16))
            w1q = [P4(nc.sbuf_tensor(f"w1q{i}", [128, NG * 768], BF16))
                   for i in range(2)]
            w2q = [P4(nc.sbuf_tensor(f"w2q{i}", [128, NG * 768], BF16))
                   for i in range(2)]
            w3h = P4(nc.sbuf_tensor("w3h", [128, 11 * 1024], BF16))
            sil = [P4(nc.sbuf_tensor(f"sil{i}", [128, FCH], F32))
                   for i in range(2)]
            wbb = P4(nc.sbuf_tensor("wbb", [128, FCH], F32))
            tmpc = [P4(nc.sbuf_tensor(f"tmpc{i}", [128, 512], F32))
                    for i in range(2)]
            x2f = [P4(nc.sbuf_tensor(f"x2f{i}", [128, 512], F32))
                   for i in range(2)]
            outb = [P4(nc.sbuf_tensor(f"outb{i}", [128, 1024], F32))
                    for i in range(2)]
            sAB4 = P4(nc.sbuf_tensor("sAB4", [128, 1024], F32))
            r4a = P4(nc.sbuf_tensor("r4a", [1, FCH], F32))
            r4b = P4(nc.sbuf_tensor("r4b", [1, FCH], F32))
            r4c = P4(nc.sbuf_tensor("r4c", [1, FCH], F32))
            sq4 = P4(nc.sbuf_tensor("sq4", [128, FCH], F32))
            wrow = P4(nc.sbuf_tensor("wrow", [1, FCH], F32))

            pup = P4(nc.psum_tensor("pup", [128, 1024], F32))
            pgt = P4(nc.psum_tensor("pgt", [128, 1024], F32))
            pdn = [P4(nc.psum_tensor(f"pdn{i}", [128, 512], F32))
                   for i in range(2)]
            pms = P4(nc.psum_tensor("pms", [128, 1024], F32))

            b_y = Buf()
            b_acc = Buf()
            b_h = Buf()
            b_w1q = [Buf(), Buf()]
            b_w2q = [Buf(), Buf()]
            b_w3h = Buf()
            b_sil = [Buf(), Buf()]
            b_wb = Buf()
            b_tmpc = [Buf(), Buf()]
            b_x2f = [Buf(), Buf()]
            b_outb = [Buf(), Buf()]
            b_sab4 = Buf()
            b_r4a = Buf()
            b_r4b = Buf()
            b_r4c = Buf()
            b_sq4 = Buf()
            b_pup = Buf()
            b_pgt = Buf()
            b_pdn = [Buf(), Buf()]
            b_pms = Buf()
            b_wrow = Buf()

            def q_load(eng, dstbuf, dst_ap, src_ap, waits, bufobj, stream):
                S.do(eng, lambda s, d=dst_ap, i_=src_ap: s.dma_start(out=d, in_=i_),
                     wait=waits + bufobj.wr(), inc=[stream])
                bufobj.note_w(stream)

            for F in range(NF if phases >= 4 else 0):
                S.do("gpsimd", lambda g, F=F: g.dma_start(
                    out=ybuf.ap().rearrange("p (g c) -> p g c", g=NG),
                    in_=ys.rearrange("g p c -> p g c")
                    [:, :, F * FCH:(F + 1) * FCH]),
                    wait=b_y.wr(), inc=[YB])
                b_y.note_w(YB)
                S.do("gpsimd", lambda g, F=F: g.dma_start(
                    out=acc.ap().rearrange("p (g c) -> p g c", g=NG),
                    in_=zs.rearrange("g p c -> p g c")
                    [:, :, F * FCH:(F + 1) * FCH]),
                    wait=b_acc.wr(), inc=[ACC])
                b_acc.note_w(ACC)

                for e in range(E):
                    S.do("sync", lambda s, e=e, F=F: s.dma_start(
                        out=wrow[0:1, :], in_=wTe_s[e, F * FCH:(F + 1) * FCH]),
                        wait=b_wrow.wr(), inc=[WR])
                    b_wrow.note_w(WR)
                    for nsub in range(2):
                        S.do("tensor", lambda t_, nsub=nsub: t_.matmul(
                            pms[:, nsub * 512:(nsub + 1) * 512], ones_1p[:, :],
                            wrow[0:1, nsub * 512:(nsub + 1) * 512],
                            start=True, stop=True),
                            wait=b_wrow.rd()
                            + (b_pms.wr() if nsub == 0 else []) + [(G, tok_ones)],
                            inc=[T] if nsub == 1 else [])
                    b_wrow.note_r(T)
                    b_pms.note_w(T)
                    S.do("scalar", lambda s: s.copy(out=wbb[:, :], in_=pms[:, :]),
                         wait=b_pms.rd() + b_wb.wr(), inc=[A])
                    b_pms.note_r(A)
                    b_wb.note_w(A)

                    for half in range(2):
                        for si, (j0, nj) in enumerate(HSLABS):
                            qb = si
                            qi = half * 2 + si
                            col0 = sum(NG * n * 128 for _, n in
                                       (HSLABS * 2)[:qi])
                            w = NG * nj * 128
                            S.do("gpsimd", lambda g, e=e, col0=col0, w=w,
                                 qb=qb: g.dma_start(
                                     out=w1q[qb][:, 0:w],
                                     in_=w1_in[e][:, col0:col0 + w]),
                                 wait=b_w1q[qb].wr(), inc=[WQ1[qb]])
                            b_w1q[qb].note_w(WQ1[qb])
                            S.do("gpsimd", lambda g, e=e, col0=col0, w=w,
                                 qb=qb: g.dma_start(
                                     out=w2q[qb][:, 0:w],
                                     in_=w2_in[e][:, col0:col0 + w]),
                                 wait=b_w2q[qb].wr(), inc=[WQ2[qb]])
                            b_w2q[qb].note_w(WQ2[qb])
                        S.do("gpsimd", lambda g, e=e, half=half: g.dma_start(
                            out=w3h[:, :],
                            in_=w3_in[e][:, half * 11264:(half + 1) * 11264]),
                            wait=b_w3h.wr(), inc=[W3H])
                        b_w3h.note_w(W3H)

                        pre_h = b_h.wr()
                        for si, (j0, nj) in enumerate(HSLABS):
                            qb = si
                            for jl in range(nj):
                                j_loc = j0 + jl
                                sb = (half * 11 + j_loc) % 2
                                for nsub in range(2):
                                    for gg in range(NG):
                                        S.do("tensor", lambda t_, gg=gg, jl=jl,
                                             nsub=nsub, qb=qb, nj=nj: t_.matmul(
                                                 pup[:, nsub * 512:
                                                     (nsub + 1) * 512],
                                                 w1q[qb][:, (gg * nj + jl) * 128:
                                                         (gg * nj + jl + 1) * 128],
                                                 ybuf[:, gg * FCH + nsub * 512:
                                                      gg * FCH + (nsub + 1) * 512],
                                                 start=(gg == 0),
                                                 stop=(gg == NG - 1)),
                                             wait=b_w1q[qb].rd() + b_y.rd()
                                             + (b_pup.wr()
                                                if (gg == 0 and nsub == 0) else []),
                                             inc=[T] if (gg == NG - 1 and nsub == 1)
                                             else [])
                                b_pup.note_w(T)
                                b_y.note_r(T)
                                b_w1q[qb].note_r(T)
                                for nsub in range(2):
                                    for gg in range(NG):
                                        S.do("tensor", lambda t_, gg=gg, jl=jl,
                                             nsub=nsub, qb=qb, nj=nj: t_.matmul(
                                                 pgt[:, nsub * 512:
                                                     (nsub + 1) * 512],
                                                 w2q[qb][:, (gg * nj + jl) * 128:
                                                         (gg * nj + jl + 1) * 128],
                                                 ybuf[:, gg * FCH + nsub * 512:
                                                      gg * FCH + (nsub + 1) * 512],
                                                 start=(gg == 0),
                                                 stop=(gg == NG - 1)),
                                             wait=b_w2q[qb].rd()
                                             + (b_pgt.wr()
                                                if (gg == 0 and nsub == 0) else []),
                                             inc=[T] if (gg == NG - 1 and nsub == 1)
                                             else [])
                                b_pgt.note_w(T)
                                b_w2q[qb].note_r(T)
                                S.do("scalar", lambda s, sb=sb: s.activation(
                                    sil[sb][:, :], pup[:, :], AF.Silu),
                                    wait=b_pup.rd() + b_sil[sb].wr(), inc=[A])
                                b_pup.note_r(A)
                                b_sil[sb].note_w(A)
                                S.do("vector", lambda v, j_loc=j_loc, sb=sb:
                                     v.tensor_mul(
                                         hbuf[:, j_loc * FCH:(j_loc + 1) * FCH],
                                         sil[sb][:, :], pgt[:, :]),
                                     wait=b_sil[sb].rd() + b_pgt.rd() + pre_h,
                                     inc=[V])
                                b_sil[sb].note_r(V)
                                b_pgt.note_r(V)
                                b_h.note_w(V)
                        tok_h = V.n
                        for i in range(NG):
                            for nsub in range(2):
                                db = (i * 2 + nsub) % 2
                                for jl in range(11):
                                    S.do("tensor", lambda t_, i=i, jl=jl,
                                         nsub=nsub, db=db: t_.matmul(
                                             pdn[db][:, :],
                                             w3h[:, jl * 1024 + i * 128:
                                                 jl * 1024 + (i + 1) * 128],
                                             hbuf[:, jl * FCH + nsub * 512:
                                                  jl * FCH + (nsub + 1) * 512],
                                             start=(jl == 0), stop=(jl == 10)),
                                         wait=b_w3h.rd() + [(V, tok_h)]
                                         + (b_pdn[db].wr() if jl == 0 else []),
                                         inc=[T] if jl == 10 else [])
                                b_pdn[db].note_w(T)
                                b_h.note_r(T)
                                b_w3h.note_r(T)
                                tb_ = db
                                S.do("vector", lambda v, db=db, tb_=tb_, nsub=nsub:
                                     v.tensor_mul(
                                         tmpc[tb_][:, :], pdn[db][:, :],
                                         wbb[:, nsub * 512:(nsub + 1) * 512]),
                                     wait=b_pdn[db].rd() + b_wb.rd()
                                     + b_tmpc[tb_].wr(), inc=[V])
                                b_pdn[db].note_r(V)
                                b_wb.note_r(V)
                                b_tmpc[tb_].note_w(V)
                                S.do("vector", lambda v, i=i, nsub=nsub, tb_=tb_:
                                     v.tensor_add(
                                         acc[:, i * FCH + nsub * 512:
                                             i * FCH + (nsub + 1) * 512],
                                         acc[:, i * FCH + nsub * 512:
                                             i * FCH + (nsub + 1) * 512],
                                         tmpc[tb_][:, :]),
                                     wait=b_tmpc[tb_].rd() + b_acc.rd()
                                     + [(V, V.n)], inc=[V])
                                b_tmpc[tb_].note_r(V)
                                b_acc.note_w(V)

                for i in range(NG):
                    S.do("scalar", lambda s, i=i: s.square(
                        sq4[:, :], acc[:, i * FCH:(i + 1) * FCH]),
                        wait=b_acc.rd() + b_sq4.wr(), inc=[A])
                    b_sq4.note_w(A)
                    for nsub in range(2):
                        S.do("tensor", lambda t_, i=i, nsub=nsub: t_.matmul(
                            pms[0:1, nsub * 512:(nsub + 1) * 512], ones_p1[:, :],
                            acc[:, i * FCH + nsub * 512:
                                i * FCH + (nsub + 1) * 512],
                            start=(i == 0), stop=(i == NG - 1)),
                            wait=b_acc.rd()
                            + (b_pms.wr() if (i == 0 and nsub == 0) else []),
                            inc=[T] if nsub == 1 else [])
                    b_pms.note_w(T)
                    b_acc.note_r(T, A)
                    for nsub in range(2):
                        S.do("tensor", lambda t_, i=i, nsub=nsub: t_.matmul(
                            pdn[nsub][0:1, :], ones_p1[:, :],
                            sq4[:, nsub * 512:(nsub + 1) * 512],
                            start=(i == 0), stop=(i == NG - 1)),
                            wait=b_sq4.rd()
                            + (b_pdn[nsub].wr() if i == 0 else []),
                            inc=[T] if nsub == 1 else [])
                        b_pdn[nsub].note_w(T)
                    b_sq4.note_r(T)
                S.do("vector", lambda v: v.tensor_scalar_mul(
                    r4a[0:1, :], pms[0:1, :], 1.0 / D),
                    wait=b_pms.rd() + b_r4a.wr(), inc=[V])
                b_pms.note_r(V)
                b_r4a.note_w(V)
                S.do("vector", lambda v: v.tensor_scalar_mul(
                    r4b[0:1, 0:512], pdn[0][0:1, :], 1.0 / D),
                    wait=b_pdn[0].rd() + b_r4b.wr(), inc=[V])
                S.do("vector", lambda v: v.tensor_scalar_mul(
                    r4b[0:1, 512:1024], pdn[1][0:1, :], 1.0 / D),
                    wait=b_pdn[1].rd() + [(V, V.n)], inc=[V])
                b_pdn[0].note_r(V)
                b_pdn[1].note_r(V)
                b_r4b.note_w(V)
                S.do("vector", lambda v: v.scalar_tensor_tensor(
                    r4c[0:1, :], r4a[0:1, :], 1.0, r4a[0:1, :],
                    ALU.mult, ALU.mult),
                    wait=b_r4a.rd() + b_r4c.wr() + [(V, V.n)], inc=[V])
                b_r4c.note_w(V)
                S.do("vector", lambda v: v.tensor_sub(
                    r4b[0:1, :], r4b[0:1, :], r4c[0:1, :]),
                    wait=b_r4c.rd() + [(V, V.n)], inc=[V])
                b_r4c.note_r(V)
                b_r4b.note_w(V)
                S.do("scalar", lambda s: s.activation(
                    r4c[0:1, :], r4b[0:1, :], AF.Ln, bias=cst[0:1, 0:1]),
                    wait=b_r4b.rd() + b_r4c.wr(), inc=[A])
                b_r4b.note_r(A)
                b_r4c.note_w(A)
                S.do("scalar", lambda s: s.activation(
                    r4b[0:1, :], r4c[0:1, :], AF.Exp, scale=-0.5),
                    wait=b_r4c.rd() + b_r4b.wr() + [(A, A.n)], inc=[A])
                b_r4c.note_r(A)
                b_r4b.note_w(A)
                S.do("vector", lambda v: v.scalar_tensor_tensor(
                    r4c[0:1, :], r4a[0:1, :], -1.0, r4b[0:1, :],
                    ALU.mult, ALU.mult),
                    wait=b_r4b.rd() + b_r4a.rd() + b_r4c.wr(), inc=[V])
                b_r4a.note_r(V)
                b_r4b.note_r(V)
                b_r4c.note_w(V)

                for nsub in range(2):
                    S.do("tensor", lambda t_, nsub=nsub: t_.matmul(
                        pms[:, 0:512], ones_1p[:, :],
                        r4b[0:1, nsub * 512:(nsub + 1) * 512],
                        start=True, stop=True),
                        wait=b_r4b.rd() + b_r4c.rd() + b_pms.wr())
                    S.do("tensor", lambda t_, nsub=nsub: t_.matmul(
                        pms[:, 512:1024], ones_1p[:, :],
                        r4c[0:1, nsub * 512:(nsub + 1) * 512],
                        start=True, stop=True), inc=[T])
                    b_r4b.note_r(T)
                    b_r4c.note_r(T)
                    b_pms.note_w(T)
                    S.do("scalar", lambda s: s.copy(out=sAB4[:, :], in_=pms[:, :]),
                         wait=b_pms.rd() + b_sab4.wr(), inc=[A])
                    b_pms.note_r(A)
                    b_sab4.note_w(A)
                    x2tok = [0] * NG

                    def x2_load(i, F=F, nsub=nsub):
                        xb = i % 2
                        S.do("gpsimd", lambda g, i=i, xb=xb: g.dma_start(
                            out=x2f[xb][:, :],
                            in_=x2s[i][:, F * FCH + nsub * 512:
                                       F * FCH + (nsub + 1) * 512]),
                            wait=b_x2f[xb].wr(), inc=[XF[xb]])
                        b_x2f[xb].note_w(XF[xb])
                        x2tok[i] = (XF[xb], XF[xb].n)

                    x2_load(0)
                    x2_load(1)
                    for i in range(NG):
                        xb = i % 2
                        S.do("vector", lambda v, i=i, nsub=nsub: v.tensor_mul(
                            tmpc[0][:, :],
                            acc[:, i * FCH + nsub * 512:
                                i * FCH + (nsub + 1) * 512],
                            sAB4[:, 0:512]),
                            wait=b_sab4.rd() + b_tmpc[0].wr() + b_acc.rd(),
                            inc=[V])
                        b_sab4.note_r(V)
                        b_tmpc[0].note_w(V)
                        S.do("vector", lambda v: v.tensor_add(
                            tmpc[1][:, :], tmpc[0][:, :], sAB4[:, 512:1024]),
                            wait=b_tmpc[0].rd() + b_tmpc[1].wr(), inc=[V])
                        b_tmpc[0].note_r(V)
                        b_tmpc[1].note_w(V)
                        S.do("vector", lambda v, i=i, nsub=nsub, xb=xb: v.tensor_add(
                            acc[:, i * FCH + nsub * 512:
                                i * FCH + (nsub + 1) * 512],
                            tmpc[1][:, :], x2f[xb][:, :]),
                            wait=b_tmpc[1].rd() + [x2tok[i], (V, V.n)],
                            inc=[V])
                        b_tmpc[1].note_r(V)
                        b_x2f[xb].note_r(V)
                        b_acc.note_w(V)
                        if i + 2 < NG:
                            x2_load(i + 2)
                    for tt in range(4):
                        ob = tt % 2
                        t_abs = F * FCH + nsub * 512 + tt * 128
                        for i in range(NG):
                            S.do("tensor", lambda t_, i=i, tt=tt, nsub=nsub:
                                 t_.transpose(
                                     pms[:, i * 128:(i + 1) * 128],
                                     acc[:, i * FCH + nsub * 512 + tt * 128:
                                         i * FCH + nsub * 512 + (tt + 1) * 128],
                                     identF[:, :]),
                                 wait=b_acc.rd()
                                 + (b_pms.wr() if i == 0 else []),
                                 inc=[T] if i == NG - 1 else [])
                        b_pms.note_w(T)
                        b_acc.note_r(T)
                        S.do("scalar", lambda s, ob=ob: s.copy(
                            out=outb[ob][:, :], in_=pms[:, :]),
                            wait=b_pms.rd() + b_outb[ob].wr(), inc=[A])
                        b_pms.note_r(A)
                        b_outb[ob].note_w(A)
                        S.do("gpsimd", lambda g, t_abs=t_abs, ob=ob: g.dma_start(
                            out=out[t_abs:t_abs + 128, :], in_=outb[ob][:, :]),
                            wait=b_outb[ob].rd(), inc=[OB[ob]])
                        b_outb[ob].note_r(OB[ob])

        S.do("sync", wait=[(s, s.n) for s in ALLS] + [(STO, STO.n)])

        with nc.Block() as blk:
            S.emit_into(blk)

    return nc, dbg_aps


_NC_CACHE = {}


def _get_nc(dbg=()):
    key = tuple(sorted(dbg))
    if key not in _NC_CACHE:
        _NC_CACHE[key] = build_nc(dbg)
    return _NC_CACHE[key]


def prep_inputs(L0_attn_x, L0_attn_query_proj, L0_attn_key_proj,
                L0_attn_value_proj, L0_attn_output_proj, L0_ffn_gate_w,
                L0_ffn_rms_w, L0_ffn_W1, L0_ffn_W2, L0_ffn_W3):
    x = np.ascontiguousarray(np.asarray(L0_attn_x, dtype=np.float32))
    def sbuf_layout(a):
        c = a.shape[1]
        return np.ascontiguousarray(
            a.reshape(NG, 128, c).transpose(1, 0, 2).reshape(128, NG * c))

    wq = sbuf_layout(np.asarray(L0_attn_query_proj, np.float32)
                     .transpose(2, 0, 1).reshape(D, H * KD))
    wk = sbuf_layout(np.asarray(L0_attn_key_proj, np.float32))
    wv = sbuf_layout(np.asarray(L0_attn_value_proj, np.float32))
    wo = sbuf_layout(np.asarray(L0_attn_output_proj, np.float32)
                     .transpose(1, 2, 0).reshape(H * KD, D))
    gw = np.ascontiguousarray(
        np.asarray(L0_ffn_gate_w, np.float32).T.reshape(NG, 128, E)
        .transpose(1, 0, 2).reshape(128, NG * E))
    rms = np.ascontiguousarray(
        np.asarray(L0_ffn_rms_w, np.float32).reshape(NG, 128).T)
    qrs = []
    for half in range(2):
        for j0, nj in HSLABS:
            qrs.append((half * 1408 + j0 * 128, half * 1408 + (j0 + nj) * 128))

    def slab12(W):
        out = np.empty((E, 128, NG * DE), np.float32)
        for e in range(E):
            v = np.asarray(W[e], np.float32).T.reshape(NG, 128, DE) \
                .transpose(1, 0, 2)
            out[e] = np.concatenate(
                [v[:, :, a:b].reshape(128, NG * (b - a)) for a, b in qrs],
                axis=1)
        return out

    def slab3(W):
        out = np.empty((E, 128, 22 * 1024), np.float32)
        for e in range(E):
            out[e] = np.asarray(W[e], np.float32).T.reshape(22, 128, 1024) \
                .transpose(1, 0, 2).reshape(128, 22 * 1024)
        return out

    w1t = slab12(L0_ffn_W1)
    w2t = slab12(L0_ffn_W2)
    w3t = slab3(L0_ffn_W3)
    in_maps = []
    for c in range(N_CORES):
        in_maps.append(dict(x=np.ascontiguousarray(x[c]), wq=wq, wk=wk, wv=wv,
                            wo=wo, gw=gw, rms=rms, w1t=w1t, w2t=w2t, w3t=w3t))
    return in_maps


def kernel(**inputs):
    in_maps = prep_inputs(**inputs)
    nc, _ = _get_nc()
    res = run_bass_kernel_spmd(nc, in_maps, core_ids=list(range(N_CORES)))
    return np.stack([res.results[c]["out"] for c in range(N_CORES)], axis=0)


# version 17
# speedup vs baseline: 1.0182x; 1.0098x over previous
import numpy as np
from contextlib import ExitStack

import concourse.bass as bass
import concourse.mybir as mybir
from concourse.bass_utils import run_bass_kernel_spmd

F32 = mybir.dt.float32
BF16 = mybir.dt.bfloat16
AF = mybir.ActivationFunctionType
ALU = mybir.AluOpType
AX = mybir.AxisListType

N_CORES = 8
SEQ = 4096
D = 1024
H = 16
KD = 64
E = 8
DE = 2816
NCH = 8
CH = 512
NF = 4
FCH = 1024
NG = 8
NJ = 22
NM = SEQ // 128
SKEW = 1
HSLABS = [(0, 6), (6, 5)]
EPS_LN = 1e-5
EPS_RMS = 1e-6

ENGINES = ("sync", "gpsimd", "tensor", "scalar", "vector")


class Stream:
    __slots__ = ("sem", "n", "step", "name")

    def __init__(self, sem, step=1, name=""):
        self.sem = sem
        self.n = 0
        self.step = step
        self.name = name


class Buf:
    __slots__ = ("r", "w")

    def __init__(self):
        self.r = {}
        self.w = {}

    def rd(self):
        return list(self.w.items())

    def wr(self):
        d = dict(self.r)
        for s, t in self.w.items():
            d[s] = max(d.get(s, 0), t)
        return list(d.items())

    def note_r(self, *streams):
        for s in streams:
            self.r[s] = s.n

    def note_w(self, *streams):
        self.w = {s: s.n for s in streams}
        self.r = {}


class Sched:
    def __init__(self):
        self.steps = {e: [] for e in ENGINES}
        self.waited = {}

    def do(self, eng, fn=None, wait=(), inc=()):
        wl = []
        for w in wait:
            s, v = w if isinstance(w, tuple) else (w, w.n)
            assert v <= s.n, f"wait on future token {s.name}: {v} > {s.n}"
            if v <= 0:
                continue
            k = (eng, id(s))
            if self.waited.get(k, 0) >= v:
                continue
            self.waited[k] = v
            wl.append((s, v))
        il = []
        for s in inc:
            s.n += s.step
            il.append((s, s.step))
        assert len(il) <= 1
        self.steps[eng].append((wl, fn, il))

    def emit_into(self, block):
        def mk(name):
            steps = self.steps[name]

            def run(eng):
                for wl, fn, il in steps:
                    for s, v in wl:
                        eng.wait_ge(s.sem, v)
                    ins = fn(eng) if fn is not None else None
                    for s, k in il:
                        if ins is None:
                            ins = eng.nop()
                        ins.then_inc(s.sem, k)
            return run
        block.sync(mk("sync"))
        block.gpsimd(mk("gpsimd"))
        block.tensor(mk("tensor"))
        block.scalar(mk("scalar"))
        block.vector(mk("vector"))


def build_nc(dbg=(), phases=4, p1s=('ln','tr','q','k','v'),
             p3s=('heads','proj','projmm','projcp','projsq','projst',
                  'res','ln1f','rms','gate'), p3n=NCH):
    nc = bass.Bass("TRN2", target_bir_lowering=False)

    x_in = nc.dram_tensor("x", [SEQ, D], F32, kind="ExternalInput").ap()
    wq_in = nc.dram_tensor("wq", [128, NG * 1024], F32, kind="ExternalInput").ap()
    wk_in = nc.dram_tensor("wk", [128, NG * KD], F32, kind="ExternalInput").ap()
    wv_in = nc.dram_tensor("wv", [128, NG * KD], F32, kind="ExternalInput").ap()
    wo_in = nc.dram_tensor("wo", [128, NG * 1024], F32, kind="ExternalInput").ap()
    gw_in = nc.dram_tensor("gw", [128, NG * E], F32, kind="ExternalInput").ap()
    rms_in = nc.dram_tensor("rms", [128, NG], F32, kind="ExternalInput").ap()
    w1_in = nc.dram_tensor("w1t", [E, 128, NG * DE], F32,
                           kind="ExternalInput").ap()
    w2_in = nc.dram_tensor("w2t", [E, 128, NG * DE], F32,
                           kind="ExternalInput").ap()
    w3_in = nc.dram_tensor("w3t", [E, 128, 22 * 1024], F32,
                           kind="ExternalInput").ap()
    out = nc.dram_tensor("out", [SEQ, D], F32, kind="ExternalOutput").ap()

    x2s = nc.dram_tensor("x2s", [NG, 128, SEQ], F32, kind="Internal").ap()
    zs = nc.dram_tensor("zs", [NG, 128, SEQ], F32, kind="Internal").ap()
    xts = nc.dram_tensor("xts", [NG, 128, SEQ], F32, kind="Internal").ap()
    ys = nc.dram_tensor("ys", [NG, 128, SEQ], BF16, kind="Internal").ap()
    wTe_s = nc.dram_tensor("wte_s", [E, SEQ], F32, kind="Internal").ap()


    dbg_aps = {}

    def dbg_tensor(name, shape, dtype=F32):
        dbg_aps[name] = nc.dram_tensor("dbg_" + name, shape, dtype,
                                       kind="ExternalOutput").ap()
        return dbg_aps[name]

    import ml_dtypes
    identB_h = nc.inline_tensor(np.eye(128, dtype=ml_dtypes.bfloat16),
                                name="identB_c").ap()
    identF_h = nc.inline_tensor(np.eye(128, dtype=np.float32), name="identF_c").ap()

    ctx = ExitStack()
    with ctx:
        EC = ctx.enter_context

        T = Stream(EC(nc.semaphore("sT")), name="T")
        A = Stream(EC(nc.semaphore("sA")), name="A")
        V = Stream(EC(nc.semaphore("sV")), name="V")
        G = Stream(EC(nc.semaphore("sG")), name="G")
        DS = Stream(EC(nc.semaphore("sDS")), 16, name="DS")
        DG = Stream(EC(nc.semaphore("sDG")), 16, name="DG")
        STO = Stream(EC(nc.semaphore("sSTO")), 16, name="STO")

        def dstream(nm):
            return Stream(EC(nc.semaphore("s" + nm)), 16, name=nm)

        CST = dstream("CST")
        WAT = dstream("WAT")
        WOT = dstream("WOT")
        INIT = dstream("INIT")
        XB = [dstream(f"XB{i}") for i in range(2)]
        XTS = dstream("XTS")
        XTB = [dstream(f"XTB{i}") for i in range(2)]
        YSS = dstream("YSS")
        YB = dstream("YB")
        ACC = dstream("ACC")
        XF = [dstream(f"XF{i}") for i in range(2)]
        WR = dstream("WR")
        WQ1 = [dstream(f"WQ1{i}") for i in range(2)]
        WQ2 = [dstream(f"WQ2{i}") for i in range(2)]
        W3H = dstream("W3H")
        OB = [dstream(f"OB{i}") for i in range(2)]
        OALLB = [dstream(f"OALLB{i}") for i in range(2)]
        STG = dstream("STG")
        ALLS = (T, A, V, G, DS, DG, CST, WAT, WOT, INIT, XB[0], XB[1], XTS,
                XTB[0], XTB[1], YSS, YB, ACC, XF[0], XF[1], WR, WQ1[0],
                WQ1[1], WQ2[0], WQ2[1], W3H, OB[0], OB[1], OALLB[0],
                OALLB[1], STG)

        S = Sched()

        def barrier(extra=()):
            toks = [(s, s.n) for s in ALLS] + list(extra)
            for e in ENGINES:
                S.do(e, wait=toks)

        identB = EC(nc.sbuf_tensor("identB", [128, 128], BF16))
        identF = EC(nc.sbuf_tensor("identF", [128, 128], F32))
        ones_p1 = EC(nc.sbuf_tensor("ones_p1", [128, 1], F32))
        ones_1p = EC(nc.sbuf_tensor("ones_1p", [1, 128], F32))
        onesB = EC(nc.sbuf_tensor("onesB", [128, 64], F32))
        cbias = EC(nc.sbuf_tensor("cbias", [128, 1], F32))
        rms_sb = EC(nc.sbuf_tensor("rms_sb", [128, NG], F32))
        gw2 = EC(nc.sbuf_tensor("gw2", [128, NG * E], F32))
        srow = EC(nc.sbuf_tensor("srow", [1, 16], F32))
        cst = EC(nc.sbuf_tensor("cst", [128, 2], F32))
        stage8 = EC(nc.sbuf_tensor("stage8", [1, E * 128], F32))
        gwT = EC(nc.sbuf_tensor("gwT", [128, NG * E], F32))

        S.do("gpsimd", lambda g: g.memset(cst[:, 0:1], EPS_LN), inc=[G])
        S.do("gpsimd", lambda g: g.memset(cst[:, 1:2], EPS_RMS), inc=[G])
        S.do("gpsimd", lambda g: g.memset(ones_p1[:, :], 1.0), inc=[G])
        S.do("gpsimd", lambda g: g.memset(ones_1p[:, :], 1.0), inc=[G])
        S.do("gpsimd", lambda g: g.memset(onesB[:, :], 1.0), inc=[G])
        tok_ones = G.n
        S.do("sync", lambda s: s.dma_start(out=identB[:, :], in_=identB_h), inc=[INIT])
        S.do("sync", lambda s: s.dma_start(out=identF[:, :], in_=identF_h), inc=[INIT])
        S.do("sync", lambda s: s.dma_start(out=rms_sb[:, :], in_=rms_in), inc=[INIT])
        S.do("sync", lambda s: s.dma_start(out=gwT[:, :], in_=gw_in), inc=[INIT])
        tok_small = INIT.n
        for g in range(NG):
            S.do("vector", lambda v, g=g: v.tensor_scalar_mul(
                gw2[:, g * E:(g + 1) * E], gwT[:, g * E:(g + 1) * E],
                rms_sb[:, g:g + 1]),
                wait=[(INIT, tok_small)], inc=[V])
        tok_gw2 = V.n
        with ExitStack() as pctx:
            PE = pctx.enter_context
            wo_sb = PE(nc.sbuf_tensor("wo_sb", [128, NG * 1024], BF16))
            QT = PE(nc.sbuf_tensor("QT", [128, NG * SEQ], BF16))
            KT = PE(nc.sbuf_tensor("KT", [128, SEQ], BF16))
            vaug = PE(nc.sbuf_tensor("vaug", [128, NM * 65], BF16))

            S.do("gpsimd", lambda g: g.memset(vaug[:, :], 1.0), inc=[G])
            tok_vinit = G.n

            with ExitStack() as p1c:
                P1 = p1c.enter_context
                wq_sb = P1(nc.sbuf_tensor("wq_sb", [128, NG * 1024], BF16))
                wk_sb = P1(nc.sbuf_tensor("wk_sb", [128, NG * KD], BF16))
                wv_sb = P1(nc.sbuf_tensor("wv_sb", [128, NG * KD], BF16))
                S.do("gpsimd", lambda g: g.dma_start(out=wq_sb[:, :], in_=wq_in),
                     inc=[WAT])
                S.do("gpsimd", lambda g: g.dma_start(out=wk_sb[:, :], in_=wk_in),
                     inc=[WAT])
                S.do("gpsimd", lambda g: g.dma_start(out=wv_sb[:, :], in_=wv_in),
                     inc=[WAT])
                tok_attnw = WAT.n
                S.do("gpsimd", lambda g: g.dma_start(out=wo_sb[:, :], in_=wo_in),
                     inc=[WOT])
                tok_wo = WOT.n
                xbuf = [P1(nc.sbuf_tensor(f"xbuf{i}", [128, 4 * 1024], F32))
                        for i in range(2)]
                ln1 = [P1(nc.sbuf_tensor(f"ln1_{i}", [128, 1024], BF16))
                       for i in range(2)]
                ln1T = [P1(nc.sbuf_tensor(f"ln1T{i}", [128, NG * 512], BF16))
                        for i in range(2)]
                mv = P1(nc.sbuf_tensor("mv", [128, 8], F32))
                sqx = P1(nc.sbuf_tensor("sqx", [128, 1024], F32))
                xtbuf = P1(nc.sbuf_tensor("xtbuf", [128, 512], F32))
                sqb = P1(nc.sbuf_tensor("sqb", [128, 512], F32))
                ksq = P1(nc.sbuf_tensor("ksq", [64, 512], F32))

                pq = [P1(nc.psum_tensor(f"pq{i}", [128, 512], F32))
                      for i in range(2)]
                pk = P1(nc.psum_tensor("pk", [128, 512], F32))
                pv = P1(nc.psum_tensor("pv", [128, 64], F32))
                ptr = [P1(nc.psum_tensor(f"ptr{i}", [128, 128], BF16))
                      for i in range(2)]
                pst = P1(nc.psum_tensor("pst", [1, 512], F32))
                pst2 = P1(nc.psum_tensor("pst2", [1, 512], F32))

                b_ln1 = [Buf(), Buf()]
                b_xbuf = [Buf(), Buf()]
                b_ln1T = [Buf(), Buf()]
                b_srow = Buf()
                b_ptr = [Buf() for _ in range(2)]
                b_pq = [Buf(), Buf()]
                b_sq = Buf()
                b_pst = Buf()
                b_pst2 = Buf()
                b_pk = Buf()
                b_ksq = Buf()
                b_pv = Buf()

                for c in range(NCH if phases >= 1 else 0):
                    cb = c % 2
                    pre_ln1T = b_ln1T[cb].wr()
                    S.do("sync", lambda s, c=c, cb=cb: s.dma_start(
                        out=xbuf[cb].ap().rearrange("p (t d) -> p t d", t=4),
                        in_=x_in[c * CH:(c + 1) * CH, :]
                        .rearrange("(t p) d -> p t d", p=128)),
                        wait=[(INIT, tok_small)] + b_xbuf[cb].wr(),
                        inc=[XB[cb]])
                    b_xbuf[cb].note_w(XB[cb])
                    tok_x = XB[cb].n
                    for t in range(4):
                        lb = t % 2
                        S.do("vector", lambda v, t=t, cb=cb: v.reduce_sum(
                            mv[:, 6:7], xbuf[cb][:, t * 1024:(t + 1) * 1024],
                            axis=AX.X),
                            wait=[(XB[cb], tok_x), (V, V.n)], inc=[V])
                        S.do("vector", lambda v, t=t, cb=cb: v.scalar_tensor_tensor(
                            sqx[:, :], xbuf[cb][:, t * 1024:(t + 1) * 1024], 1.0,
                            xbuf[cb][:, t * 1024:(t + 1) * 1024], ALU.mult,
                            ALU.mult, accum_out=mv[:, 7:8]),
                            wait=[(V, V.n)], inc=[V])
                        S.do("vector", lambda v: v.tensor_scalar_mul(
                            mv[:, 0:1], mv[:, 6:7], 1.0 / D),
                            wait=[(V, V.n)], inc=[V])
                        S.do("vector", lambda v: v.tensor_scalar_mul(
                            mv[:, 5:6], mv[:, 7:8], 1.0 / D),
                            wait=[(V, V.n)], inc=[V])
                        S.do("vector", lambda v: v.scalar_tensor_tensor(
                            mv[:, 4:5], mv[:, 0:1], 1.0, mv[:, 0:1],
                            ALU.mult, ALU.mult),
                            wait=[(V, V.n)], inc=[V])
                        S.do("vector", lambda v: v.tensor_sub(
                            mv[:, 1:2], mv[:, 5:6], mv[:, 4:5]),
                            wait=[(V, V.n)], inc=[V])
                        S.do("scalar", lambda s: s.activation(
                            mv[:, 4:5], mv[:, 1:2], AF.Ln, bias=cst[:, 0:1]),
                            wait=[(V, V.n), (A, A.n), (G, tok_ones)], inc=[A])
                        S.do("scalar", lambda s: s.activation(
                            mv[:, 2:3], mv[:, 4:5], AF.Exp, scale=-0.5),
                            wait=[(A, A.n)], inc=[A])
                        S.do("vector", lambda v: v.scalar_tensor_tensor(
                            mv[:, 3:4], mv[:, 0:1], -1.0, mv[:, 2:3],
                            ALU.mult, ALU.mult),
                            wait=[(A, A.n), (V, V.n)], inc=[V])
                        S.do("vector", lambda v, t=t, lb=lb, cb=cb: v.tensor_scalar(
                            ln1[lb][:, :], xbuf[cb][:, t * 1024:(t + 1) * 1024],
                            mv[:, 2:3], mv[:, 3:4], ALU.mult, ALU.add),
                            wait=[(V, V.n)] + b_ln1[lb].wr(), inc=[V])
                        b_ln1[lb].note_w(V)
                        b_xbuf[cb].note_r(V)
                        for gg in range(NG if 'tr' in p1s else 0):
                            pb = gg % 2
                            S.do("tensor", lambda t_, t=t, gg=gg, lb=lb, pb=pb:
                                 t_.transpose(
                                     ptr[pb][:, :],
                                     ln1[lb][:, gg * 128:(gg + 1) * 128],
                                     identB[:, :]),
                                 wait=b_ln1[lb].rd() + b_ptr[pb].wr(), inc=[T])
                            b_ptr[pb].note_w(T)
                            b_ln1[lb].note_r(T)
                            S.do("vector", lambda v, t=t, gg=gg, pb=pb: v.tensor_copy(
                                out=ln1T[cb][:, gg * 512 + t * 128:
                                             gg * 512 + (t + 1) * 128],
                                in_=ptr[pb][:, :]),
                                wait=b_ptr[pb].rd() + pre_ln1T, inc=[V])
                            b_ptr[pb].note_r(V)
                            b_ln1T[cb].note_w(V)
                    tok_ln1T = V.n

                    for hp in range(NG if 'q' in p1s else 0):
                        pqb = hp % 2
                        for gg in range(NG):
                            S.do("tensor", lambda t_, hp=hp, gg=gg, pqb=pqb: t_.matmul(
                                pq[pqb][:, :],
                                wq_sb[:, gg * 1024 + hp * 128:
                                      gg * 1024 + (hp + 1) * 128],
                                ln1T[cb][:, gg * 512:(gg + 1) * 512],
                                start=(gg == 0), stop=(gg == NG - 1)),
                                wait=[(V, tok_ln1T), (WAT, tok_attnw)]
                                + (b_pq[pqb].wr() if gg == 0 else []),
                                inc=[T] if gg == NG - 1 else [])
                        b_pq[pqb].note_w(T)
                        b_ln1T[cb].note_r(T)
                        S.do("scalar", lambda s, hp=hp, c=c, pqb=pqb: s.copy(
                            out=QT[:, hp * SEQ + c * CH: hp * SEQ + (c + 1) * CH],
                            in_=pq[pqb][:, :]),
                            wait=b_pq[pqb].rd(), inc=[A])
                        S.do("scalar", lambda s, pqb=pqb: s.square(
                            sqb[:, :], pq[pqb][:, :]),
                            wait=b_sq.wr(), inc=[A])
                        b_pq[pqb].note_r(A)
                        b_sq.note_w(A)
                        S.do("tensor", lambda t_, hp=hp: t_.matmul(
                            pst[:, :], ones_p1[:, :], sqb[:, :],
                            start=(hp == 0), stop=(hp == NG - 1)),
                            wait=b_sq.rd() + (b_pst.wr() if hp == 0 else []),
                            inc=[T])
                        b_sq.note_r(T)
                        b_pst.note_w(T)
                    if 'q' not in p1s:
                        continue
                    S.do("vector", lambda v: v.reduce_max(
                        srow[0:1, 6:7], pst[:, :], axis=AX.X),
                        wait=b_pst.rd() + [(V, V.n)], inc=[V])
                    b_pst.note_r(V)
                    if c == 0:
                        S.do("vector", lambda v: v.tensor_copy(
                            out=srow[0:1, 0:1], in_=srow[0:1, 6:7]),
                            wait=[(V, V.n)], inc=[V])
                    else:
                        S.do("vector", lambda v: v.tensor_max(
                            srow[0:1, 0:1], srow[0:1, 0:1], srow[0:1, 6:7]),
                            wait=[(V, V.n)], inc=[V])

                    for gg in range(NG if 'k' in p1s else 0):
                        S.do("tensor", lambda t_, gg=gg: t_.matmul(
                            pk[0:64, :], wk_sb[:, gg * KD:(gg + 1) * KD],
                            ln1T[cb][:, gg * 512:(gg + 1) * 512],
                            start=(gg == 0), stop=(gg == NG - 1),
                            tile_position=(0, 0)),
                            wait=[(V, tok_ln1T)] + (b_pk.wr() if gg == 0 else []),
                            inc=[T] if gg == NG - 1 else [])
                    for gg in range(NG if 'k' in p1s else 0):
                        S.do("tensor", lambda t_, gg=gg: t_.matmul(
                            pk[64:128, :], wk_sb[:, gg * KD:(gg + 1) * KD],
                            ln1T[cb][:, gg * 512:(gg + 1) * 512],
                            start=(gg == 0), stop=(gg == NG - 1),
                            tile_position=(0, 64)),
                            inc=[T] if gg == NG - 1 else [])
                    b_pk.note_w(T)
                    b_ln1T[cb].note_r(T)
                    if 'k' not in p1s:
                        continue
                    S.do("scalar", lambda s, c=c: s.copy(
                        out=KT[:, c * CH:(c + 1) * CH], in_=pk[:, :]),
                        wait=b_pk.rd(), inc=[A])
                    S.do("scalar", lambda s: s.square(ksq[:, :], pk[0:64, :]),
                         wait=b_ksq.wr() + [(A, A.n)], inc=[A])
                    b_pk.note_r(A)
                    b_ksq.note_w(A)
                    S.do("tensor", lambda t_: t_.matmul(
                        pst2[:, :], ones_p1[0:64, :], ksq[:, :],
                        start=True, stop=True),
                        wait=b_ksq.rd() + b_pst2.wr(), inc=[T])
                    b_ksq.note_r(T)
                    b_pst2.note_w(T)
                    S.do("vector", lambda v: v.reduce_max(
                        srow[0:1, 7:8], pst2[:, :], axis=AX.X),
                        wait=b_pst2.rd() + [(V, V.n)], inc=[V])
                    b_pst2.note_r(V)
                    if c == 0:
                        S.do("vector", lambda v: v.tensor_copy(
                            out=srow[0:1, 1:2], in_=srow[0:1, 7:8]),
                            wait=[(V, V.n)], inc=[V])
                    else:
                        S.do("vector", lambda v: v.tensor_max(
                            srow[0:1, 1:2], srow[0:1, 1:2], srow[0:1, 7:8]),
                            wait=[(V, V.n)], inc=[V])

                    b_xtb = Buf()
                    for gg in range(NG):
                        pb = gg % 2
                        for t in range(4):
                            S.do("tensor", lambda t_, gg=gg, t=t, pb=pb, cb=cb:
                                 t_.transpose(
                                     pq[pb][:, t * 128:(t + 1) * 128],
                                     xbuf[cb][:, t * 1024 + gg * 128:
                                              t * 1024 + (gg + 1) * 128],
                                     identF[:, :]),
                                 wait=[(XB[cb], tok_x)]
                                 + (b_pq[pb].wr() if t == 0 else []),
                                 inc=[T] if t == 3 else [])
                        b_pq[pb].note_w(T)
                        b_xbuf[cb].note_r(T)
                        S.do("vector", lambda v, gg=gg, pb=pb: v.tensor_copy(
                            out=xtbuf[:, :], in_=pq[pb][:, :]),
                            wait=b_pq[pb].rd() + b_xtb.wr(), inc=[V])
                        b_pq[pb].note_r(V)
                        b_xtb.note_w(V)
                        S.do("sync", lambda s, gg=gg, c=c: s.dma_start(
                            out=xts[gg][:, c * CH:(c + 1) * CH],
                            in_=xtbuf[:, :]),
                            wait=b_xtb.rd() + [(V, V.n)], inc=[XTS])
                        b_xtb.note_r(XTS)

                    for mt in range(4 if 'v' in p1s else 0):
                        m_abs = c * 4 + mt
                        for gg in range(NG):
                            S.do("tensor", lambda t_, gg=gg, mt=mt: t_.matmul(
                                pv[:, :],
                                ln1T[cb][:, gg * 512 + mt * 128:
                                         gg * 512 + (mt + 1) * 128],
                                wv_sb[:, gg * KD:(gg + 1) * KD],
                                start=(gg == 0), stop=(gg == NG - 1)),
                                wait=[(V, tok_ln1T), (G, tok_vinit)]
                                + (b_pv.wr() if gg == 0 else []),
                                inc=[T] if gg == NG - 1 else [])
                        b_pv.note_w(T)
                        b_ln1T[cb].note_r(T)
                        S.do("scalar", lambda s, m_abs=m_abs: s.copy(
                            out=vaug[:, m_abs * 65: m_abs * 65 + 64], in_=pv[:, :]),
                            wait=b_pv.rd(), inc=[A])
                        b_pv.note_r(A)

                tok_cbias = A.n
                if phases >= 2:
                    barrier()
                    S.do("vector", lambda v: v.tensor_mul(
                        srow[0:1, 2:3], srow[0:1, 0:1], srow[0:1, 1:2]),
                        wait=[(V, V.n)], inc=[V])
                    S.do("scalar", lambda s: s.activation(
                        srow[0:1, 3:4], srow[0:1, 2:3], AF.Ln),
                        wait=[(V, V.n)], inc=[A])
                    S.do("scalar", lambda s: s.activation(
                        srow[0:1, 4:5], srow[0:1, 3:4], AF.Exp, scale=0.5),
                        wait=[(A, A.n)], inc=[A])
                    S.do("vector", lambda v: v.tensor_scalar_mul(
                        srow[0:1, 5:6], srow[0:1, 4:5], -0.125),
                        wait=[(A, A.n)], inc=[V])
                    S.do("tensor", lambda t_: t_.matmul(
                        pv[:, 0:1], ones_1p[:, :], srow[0:1, 5:6],
                        start=True, stop=True),
                        wait=[(V, V.n)] + b_pv.wr(), inc=[T])
                    b_pv.note_w(T)
                    S.do("scalar", lambda s: s.copy(out=cbias[:, :], in_=pv[:, 0:1]),
                         wait=b_pv.rd(), inc=[A])
                    b_pv.note_r(A)
                    tok_cbias = A.n

            if "qt" in dbg:
                ap = dbg_tensor("qt", [128, NG * SEQ], BF16)
                S.do("sync", lambda s, ap=ap: s.dma_start(out=ap, in_=QT[:, :]),
                     wait=[(A, A.n), (T, T.n), (V, V.n)], inc=[STO])
            if "kt" in dbg:
                ap = dbg_tensor("kt", [128, SEQ], BF16)
                S.do("sync", lambda s, ap=ap: s.dma_start(out=ap, in_=KT[:, :]),
                     wait=[(A, A.n), (T, T.n), (V, V.n)], inc=[STO])
            if "vaug" in dbg:
                ap = dbg_tensor("vaug", [128, NM * 65], BF16)
                S.do("sync", lambda s, ap=ap: s.dma_start(out=ap, in_=vaug[:, :]),
                     wait=[(A, A.n), (T, T.n), (V, V.n)], inc=[STO])
            if "cbias" in dbg:
                ap = dbg_tensor("cbias", [128, 1])
                S.do("sync", lambda s, ap=ap: s.dma_start(out=ap, in_=cbias[:, :]),
                     wait=[(A, A.n)], inc=[STO])

            with ExitStack() as p3c:
                P3 = p3c.enter_context
                Epair = [P3(nc.sbuf_tensor(f"Ep{i}", [128, 1024], BF16))
                         for i in range(3)]
                o_all = [P3(nc.sbuf_tensor(f"o_all{i}", [128, NG * 512], BF16))
                         for i in range(2)]
                tmpo = P3(nc.sbuf_tensor("tmpo", [64, 512], F32))
                tmpoB = P3(nc.sbuf_tensor("tmpoB", [64, 512], F32))
                otmpB = P3(nc.sbuf_tensor("otmpB", [64, 512], BF16))
                recr = P3(nc.sbuf_tensor("recr", [128, 512], F32))
                xtb = [P3(nc.sbuf_tensor(f"xtb{i}", [128, 512], F32))
                       for i in range(2)]
                attnT = P3(nc.sbuf_tensor("attnT", [128, NG * 512], F32))
                x2T = P3(nc.sbuf_tensor("x2T", [128, NG * 512], F32))
                zT = P3(nc.sbuf_tensor("zT", [128, NG * 512], F32))
                yTt = [P3(nc.sbuf_tensor(f"yTt{i}", [128, 512], BF16))
                       for i in range(2)]
                sAB = P3(nc.sbuf_tensor("sAB", [128, 1024], F32))
                rowA = P3(nc.sbuf_tensor("rowA", [1, 512], F32))
                rowB = P3(nc.sbuf_tensor("rowB", [1, 512], F32))
                rowC = P3(nc.sbuf_tensor("rowC", [1, 512], F32))
                sq3 = [P3(nc.sbuf_tensor(f"sq3{i}", [128, 512], F32))
                       for i in range(2)]
                tv1 = P3(nc.sbuf_tensor("tv1", [128, 512], F32))
                lg8 = P3(nc.sbuf_tensor("lg8", [8, 512], F32))
                lgt = P3(nc.sbuf_tensor("lgt", [128, 4 * 8], F32))
                Ct = P3(nc.sbuf_tensor("Ct", [128, 4], F32))
                gt = P3(nc.sbuf_tensor("gt", [128, 10 * 8], F32))

                sp = [P3(nc.psum_tensor(f"sp{i}", [128, 1024], F32))
                      for i in range(2)]
                poA = P3(nc.psum_tensor("poA", [65, 512], F32))
                poB = P3(nc.psum_tensor("poB", [65, 512], F32))
                pr = P3(nc.psum_tensor("pr", [64, 512], F32))
                ptl = P3(nc.psum_tensor("ptl", [128, 512], F32))

                b_sp = [Buf(), Buf()]
                b_Ep = [Buf(), Buf(), Buf()]
                b_po = Buf()
                b_pr = Buf()
                b_ptl = Buf()
                b_tmpo = Buf()
                b_tmpoB = Buf()
                b_otmpB = Buf()
                b_oall = [Buf(), Buf()]
                b_oall_d = [0, 0]
                b_attnT = Buf()
                b_x2T = Buf()
                b_zT = Buf()
                b_sq3 = [Buf(), Buf()]
                b_rowA = Buf()
                b_rowB = Buf()
                b_rowC = Buf()
                b_sAB = Buf()
                b_tv1 = Buf()
                b_xtb3 = [Buf(), Buf()]
                b_recr = Buf()
                b_lg8 = Buf()
                b_lgt = Buf()
                b_Ct = Buf()
                b_yTt = [Buf(), Buf()]
                b_gt = Buf()
                b_stage_dma = [0]

                def sched_sim(c, hp, m):
                    par = m % 2
                    ep = (hp * NM + m) % 3
                    S.do("tensor", lambda t_, m=m, hp=hp, c=c, par=par:
                         t_.matmul(
                             sp[par][:, 0:512],
                             KT[0:64, m * 128:(m + 1) * 128],
                             QT[0:64, hp * SEQ + c * CH:
                                hp * SEQ + (c + 1) * CH],
                             start=True, stop=True, tile_position=(0, 0)),
                         wait=b_sp[par].wr())
                    S.do("tensor", lambda t_, m=m, hp=hp, c=c, par=par:
                         t_.matmul(
                             sp[par][:, 512:1024],
                             KT[64:128, m * 128:(m + 1) * 128],
                             QT[64:128, hp * SEQ + c * CH:
                                hp * SEQ + (c + 1) * CH],
                             start=True, stop=True, tile_position=(64, 0)),
                         inc=[T])
                    b_sp[par].note_w(T)
                    S.do("scalar", lambda s, par=par, ep=ep: s.activation(
                        Epair[ep][:, 0:1024], sp[par][:, 0:1024], AF.Exp,
                        bias=cbias[:, 0:1], scale=0.125),
                        wait=b_sp[par].rd() + b_Ep[ep].wr(), inc=[A])
                    b_sp[par].note_r(A)
                    b_Ep[ep].note_w(A)

                def sched_o(c, hp, m):
                    ep = (hp * NM + m) % 3
                    wa = b_Ep[ep].rd()
                    S.do("tensor", lambda t_, m=m, ep=ep: t_.matmul(
                        poA[:, :], vaug[:, m * 65:(m + 1) * 65],
                        Epair[ep][:, 0:512],
                        start=(m == 0), stop=(m == NM - 1)),
                        wait=wa + (b_po.wr() if m == 0 else []))
                    S.do("tensor", lambda t_: t_.nop(),
                         wait=b_Ep[ep].rd())
                    S.do("tensor", lambda t_, m=m, ep=ep: t_.matmul(
                        poB[:, :], vaug[:, m * 65:(m + 1) * 65],
                        Epair[ep][:, 512:1024],
                        start=(m == 0), stop=(m == NM - 1)),
                        inc=[T])
                    b_Ep[ep].note_r(T)
                    b_po.note_w(T)

                def sched_norm(c, hp):
                    ob = c % 2
                    S.do("vector", lambda v: v.reciprocal(
                        recr[64:65, :], poA[64:65, :]),
                        wait=b_po.rd() + b_recr.wr(), inc=[V])
                    b_recr.note_w(V)
                    S.do("scalar", lambda s: s.copy(
                        out=tmpo[:, :], in_=poA[0:64, :]),
                        wait=b_po.rd() + b_tmpo.wr() + [(V, V.n)], inc=[A])
                    S.do("scalar", lambda s: s.copy(
                        out=tmpoB[:, :], in_=poB[0:64, :]),
                        wait=b_tmpoB.wr(), inc=[A])
                    b_po.note_w({}) if False else None
                    b_tmpo.note_w(A)
                    b_tmpoB.note_w(A)
                    S.do("tensor", lambda t_: t_.matmul(
                        pr[:, :], onesB[64:65, :], recr[64:65, :],
                        start=True, stop=True, tile_position=(64, 0)),
                        wait=b_recr.rd() + b_pr.wr() + [(G, tok_ones)],
                        inc=[T])
                    b_pr.note_w(T)
                    b_recr.note_r(T)
                    S.do("vector", lambda v, hp=hp, ob=ob: v.tensor_mul(
                        o_all[ob][0:64, hp * 512:(hp + 1) * 512],
                        tmpo[:, :], pr[:, :]),
                        wait=b_pr.rd() + b_tmpo.rd() + pre_oall[ob],
                        inc=[V])
                    b_pr.note_r(V)
                    b_tmpo.note_r(V)
                    b_oall[ob].note_w(V)
                    S.do("vector", lambda v: v.reciprocal(
                        recr[64:65, :], poB[64:65, :]),
                        wait=b_recr.wr() + [(V, V.n)], inc=[V])
                    b_recr.note_w(V)
                    b_po.note_r(A, V)
                    S.do("tensor", lambda t_: t_.matmul(
                        pr[:, :], onesB[64:65, :], recr[64:65, :],
                        start=True, stop=True, tile_position=(64, 0)),
                        wait=b_recr.rd() + b_pr.wr(), inc=[T])
                    b_recr.note_r(T)
                    b_pr.note_w(T)
                    S.do("vector", lambda v: v.tensor_mul(
                        otmpB[:, :], tmpoB[:, :], pr[:, :]),
                        wait=b_pr.rd() + b_tmpoB.rd() + b_otmpB.wr(),
                        inc=[V])
                    b_pr.note_r(V)
                    b_tmpoB.note_r(V)
                    b_otmpB.note_w(V)
                    S.do("sync", lambda s, hp=hp, ob=ob: s.dma_start(
                        out=o_all[ob][64:128, hp * 512:(hp + 1) * 512],
                        in_=otmpB[:, :]),
                        wait=b_otmpB.rd() + [(V, V.n)] + pre_oall[ob],
                        inc=[OALLB[ob]])
                    b_otmpB.note_r(OALLB[ob])
                    b_oall[ob].note_w(V, OALLB[ob])
                    b_oall_d[ob] = OALLB[ob].n

                def ln_rows(eps):
                    S.do("vector", lambda v: v.tensor_scalar_mul(
                        rowA[0:1, :], rowA[0:1, :], 1.0 / D),
                        wait=b_rowA.rd() + [(V, V.n)], inc=[V])
                    b_rowA.note_w(V)
                    S.do("vector", lambda v: v.tensor_scalar_mul(
                        rowB[0:1, :], rowB[0:1, :], 1.0 / D),
                        wait=b_rowB.rd() + [(V, V.n)], inc=[V])
                    b_rowB.note_w(V)
                    S.do("vector", lambda v: v.scalar_tensor_tensor(
                        rowC[0:1, :], rowA[0:1, :], 1.0, rowA[0:1, :],
                        ALU.mult, ALU.mult),
                        wait=b_rowA.rd() + b_rowC.wr(), inc=[V])
                    b_rowC.note_w(V)
                    S.do("vector", lambda v: v.tensor_sub(
                        rowB[0:1, :], rowB[0:1, :], rowC[0:1, :]),
                        wait=b_rowC.rd() + [(V, V.n)], inc=[V])
                    b_rowC.note_r(V)
                    b_rowB.note_w(V)
                    S.do("scalar", lambda s, eps=eps: s.activation(
                        rowC[0:1, :], rowB[0:1, :], AF.Ln, bias=eps),
                        wait=b_rowB.rd() + b_rowC.wr(), inc=[A])
                    b_rowB.note_r(A)
                    b_rowC.note_w(A)
                    S.do("scalar", lambda s: s.activation(
                        rowB[0:1, :], rowC[0:1, :], AF.Exp, scale=-0.5),
                        wait=b_rowC.rd() + b_rowB.wr() + [(A, A.n)], inc=[A])
                    b_rowC.note_r(A)
                    b_rowB.note_w(A)
                    S.do("vector", lambda v: v.scalar_tensor_tensor(
                        rowC[0:1, :], rowA[0:1, :], -1.0, rowB[0:1, :],
                        ALU.mult, ALU.mult),
                        wait=b_rowB.rd() + b_rowA.rd() + b_rowC.wr(), inc=[V])
                    b_rowA.note_r(V)
                    b_rowB.note_r(V)
                    b_rowC.note_w(V)

                def bcast_ab():
                    S.do("tensor", lambda t_: t_.matmul(
                        ptl[:, :], ones_1p[:, :], rowB[0:1, :],
                        start=True, stop=True),
                        wait=b_rowB.rd() + b_ptl.wr(), inc=[T])
                    b_rowB.note_r(T)
                    b_ptl.note_w(T)
                    S.do("scalar", lambda s: s.copy(out=sAB[:, 0:512],
                                                    in_=ptl[:, :]),
                         wait=b_ptl.rd() + b_sAB.wr(), inc=[A])
                    b_ptl.note_r(A)
                    b_sAB.note_w(A)
                    S.do("tensor", lambda t_: t_.matmul(
                        ptl[:, :], ones_1p[:, :], rowC[0:1, :],
                        start=True, stop=True),
                        wait=b_rowC.rd() + b_ptl.wr(), inc=[T])
                    b_rowC.note_r(T)
                    b_ptl.note_w(T)
                    S.do("scalar", lambda s: s.copy(out=sAB[:, 512:1024],
                                                    in_=ptl[:, :]),
                         wait=b_ptl.rd() + [(A, A.n)], inc=[A])
                    b_ptl.note_r(A)
                    b_sAB.note_w(A)

                def stats_pass(src_buf, b_src, dst_row, b_dst):
                    for i in range(NG):
                        S.do("tensor", lambda t_, i=i: t_.matmul(
                            ptl[0:1, :], ones_p1[:, :],
                            src_buf[:, i * 512:(i + 1) * 512],
                            start=(i == 0), stop=(i == NG - 1)),
                            wait=b_src.rd()
                            + (b_ptl.wr() if i == 0 else []),
                            inc=[T] if i == NG - 1 else [])
                    b_ptl.note_w(T)
                    b_src.note_r(T)
                    S.do("vector", lambda v: v.tensor_copy(
                        out=dst_row[0:1, :], in_=ptl[0:1, :]),
                        wait=b_ptl.rd() + b_dst.wr(), inc=[V])
                    b_ptl.note_r(V)
                    b_dst.note_w(V)

                def sq_stats_pass(src_buf, b_src, dst_row, b_dst):
                    for i in range(NG):
                        qb = i % 2
                        S.do("scalar", lambda s, i=i, qb=qb: s.square(
                            sq3[qb][:, :], src_buf[:, i * 512:(i + 1) * 512]),
                            wait=b_src.rd() + b_sq3[qb].wr(), inc=[A])
                        b_sq3[qb].note_w(A)
                        b_src.note_r(A)
                        S.do("tensor", lambda t_, i=i, qb=qb: t_.matmul(
                            ptl[0:1, :], ones_p1[:, :], sq3[qb][:, :],
                            start=(i == 0), stop=(i == NG - 1)),
                            wait=b_sq3[qb].rd()
                            + (b_ptl.wr() if i == 0 else []),
                            inc=[T])
                        b_sq3[qb].note_r(T)
                    b_ptl.note_w(T)
                    S.do("vector", lambda v: v.tensor_copy(
                        out=dst_row[0:1, :], in_=ptl[0:1, :]),
                        wait=b_ptl.rd() + b_dst.wr(), inc=[V])
                    b_ptl.note_r(V)
                    b_dst.note_w(V)

                def tail_gen(c):
                    ob = c % 2

                    tok_xts = (c + 1) * NG * 16

                    def xt_load(i, c=c, tok_xts=0):
                        ib = i % 2
                        S.do("sync", lambda s, i=i, ib=ib: s.dma_start(
                            out=xtb[ib][:, :],
                            in_=xts[i][:, c * CH:(c + 1) * CH]),
                            wait=b_xtb3[ib].wr() + [(XTS, min(tok_xts, XTS.n))],
                            inc=[XTB[ib]])
                        b_xtb3[ib].note_w(XTB[ib])
                    tok_oall = b_oall[ob].w.get(V, 0)
                    tok_oall_d = b_oall_d[ob]
                    pre_attnT = b_attnT.wr()
                    for i in range(NG):
                        for g2 in range(NG):
                            S.do("tensor", lambda t_, i=i, g2=g2, ob=ob:
                                 t_.matmul(
                                     ptl[:, :],
                                     wo_sb[:, g2 * 1024 + i * 128:
                                           g2 * 1024 + (i + 1) * 128],
                                     o_all[ob][:, g2 * 512:(g2 + 1) * 512],
                                     start=(g2 == 0), stop=(g2 == NG - 1)),
                                 wait=[(V, tok_oall), (OALLB[ob], tok_oall_d),
                                       (WOT, tok_wo)]
                                 + (b_ptl.wr() if g2 == 0 else []),
                                 inc=[T] if g2 == NG - 1 else [])
                        b_ptl.note_w(T)
                        b_oall[ob].note_r(T)
                        S.do("vector", lambda v, i=i: v.tensor_copy(
                            out=attnT[:, i * 512:(i + 1) * 512],
                            in_=ptl[:, :]),
                            wait=b_ptl.rd() + pre_attnT, inc=[V])
                        b_ptl.note_r(V)
                        b_attnT.note_w(V)
                        yield
                    stats_pass(attnT, b_attnT, rowA, b_rowA)
                    yield
                    sq_stats_pass(attnT, b_attnT, rowB, b_rowB)
                    yield
                    ln_rows(cst[0:1, 0:1])
                    yield
                    bcast_ab()
                    yield
                    pre_x2T = b_x2T.wr()
                    xt_load(0, tok_xts=tok_xts)
                    xt_load(1, tok_xts=tok_xts)
                    for i in range(NG):
                        ib = i % 2
                        S.do("vector", lambda v, i=i: v.tensor_mul(
                            tv1[:, :], attnT[:, i * 512:(i + 1) * 512],
                            sAB[:, 0:512]),
                            wait=b_sAB.rd() + b_attnT.rd() + b_tv1.wr(),
                            inc=[V])
                        b_attnT.note_r(V)
                        b_sAB.note_r(V)
                        b_tv1.note_w(V)
                        S.do("vector", lambda v: v.tensor_add(
                            tv1[:, :], tv1[:, :], sAB[:, 512:1024]),
                            wait=b_tv1.rd() + [(V, V.n)], inc=[V])
                        b_sAB.note_r(V)
                        b_tv1.note_w(V)
                        S.do("vector", lambda v, i=i, ib=ib: v.tensor_add(
                            x2T[:, i * 512:(i + 1) * 512], tv1[:, :],
                            xtb[ib][:, :]),
                            wait=b_tv1.rd() + b_xtb3[ib].rd() + pre_x2T,
                            inc=[V])
                        b_tv1.note_r(V)
                        b_xtb3[ib].note_r(V)
                        b_x2T.note_w(V)
                        if i + 2 < NG:
                            xt_load(i + 2, tok_xts=tok_xts)
                        S.do("sync", lambda s, i=i, c=c: s.dma_start(
                            out=x2s[i][:, c * CH:(c + 1) * CH],
                            in_=x2T[:, i * 512:(i + 1) * 512]),
                            wait=[(V, V.n)], inc=[DS])
                        b_x2T.note_r(DS)
                        yield
                    stats_pass(x2T, b_x2T, rowA, b_rowA)
                    yield
                    sq_stats_pass(x2T, b_x2T, rowB, b_rowB)
                    yield
                    ln_rows(cst[0:1, 0:1])
                    yield
                    bcast_ab()
                    yield
                    pre_zT = b_zT.wr()
                    for i in range(NG):
                        S.do("vector", lambda v, i=i: v.tensor_mul(
                            tv1[:, :], x2T[:, i * 512:(i + 1) * 512],
                            sAB[:, 0:512]),
                            wait=b_sAB.rd() + b_x2T.rd() + b_tv1.wr(),
                            inc=[V])
                        b_x2T.note_r(V)
                        b_sAB.note_r(V)
                        b_tv1.note_w(V)
                        S.do("vector", lambda v, i=i: v.tensor_add(
                            zT[:, i * 512:(i + 1) * 512], tv1[:, :],
                            sAB[:, 512:1024]),
                            wait=b_tv1.rd() + pre_zT, inc=[V])
                        b_tv1.note_r(V)
                        b_zT.note_w(V)
                        S.do("sync", lambda s, i=i, c=c: s.dma_start(
                            out=zs[i][:, c * CH:(c + 1) * CH],
                            in_=zT[:, i * 512:(i + 1) * 512]),
                            wait=[(V, V.n)], inc=[DS])
                        b_zT.note_r(DS)
                        yield
                    sq_stats_pass(zT, b_zT, rowB, b_rowB)
                    yield
                    S.do("vector", lambda v: v.tensor_scalar_mul(
                        rowB[0:1, :], rowB[0:1, :], 1.0 / D),
                        wait=b_rowB.rd() + [(V, V.n)], inc=[V])
                    b_rowB.note_w(V)
                    S.do("scalar", lambda s: s.activation(
                        rowC[0:1, :], rowB[0:1, :], AF.Ln, bias=cst[0:1, 1:2]),
                        wait=b_rowB.rd() + b_rowC.wr(), inc=[A])
                    b_rowB.note_r(A)
                    b_rowC.note_w(A)
                    S.do("scalar", lambda s: s.activation(
                        rowB[0:1, :], rowC[0:1, :], AF.Exp, scale=-0.5),
                        wait=b_rowC.rd() + b_rowB.wr() + [(A, A.n)], inc=[A])
                    b_rowC.note_r(A)
                    b_rowB.note_w(A)
                    yield
                    S.do("tensor", lambda t_: t_.matmul(
                        ptl[:, :], ones_1p[:, :], rowB[0:1, :],
                        start=True, stop=True),
                        wait=b_rowB.rd() + b_ptl.wr(), inc=[T])
                    b_ptl.note_w(T)
                    S.do("scalar", lambda s: s.copy(
                        out=sAB[:, 0:512], in_=ptl[:, :]),
                        wait=b_ptl.rd() + b_sAB.wr(), inc=[A])
                    b_ptl.note_r(A)
                    b_sAB.note_w(A)
                    yield
                    for i in range(NG):
                        ib = i % 2
                        S.do("vector", lambda v, i=i: v.tensor_mul(
                            tv1[:, :], zT[:, i * 512:(i + 1) * 512],
                            sAB[:, 0:512]),
                            wait=b_sAB.rd() + b_zT.rd() + b_tv1.wr(), inc=[V])
                        b_zT.note_r(V)
                        b_sAB.note_r(V)
                        b_tv1.note_w(V)
                        S.do("vector", lambda v, i=i, ib=ib: v.tensor_scalar_mul(
                            yTt[ib][:, :], tv1[:, :], rms_sb[:, i:i + 1]),
                            wait=b_tv1.rd() + b_yTt[ib].wr(), inc=[V])
                        b_tv1.note_r(V)
                        b_yTt[ib].note_w(V)
                        S.do("gpsimd", lambda g, i=i, c=c, ib=ib: g.dma_start(
                            out=ys[i][:, c * CH:(c + 1) * CH],
                            in_=yTt[ib][:, :]),
                            wait=b_yTt[ib].rd(), inc=[YSS])
                        b_yTt[ib].note_r(YSS)
                        yield
                    for gg in range(NG):
                        S.do("tensor", lambda t_, gg=gg: t_.matmul(
                            ptl[0:8, :], gw2[:, gg * E:(gg + 1) * E],
                            zT[:, gg * 512:(gg + 1) * 512],
                            start=(gg == 0), stop=(gg == NG - 1)),
                            wait=b_zT.rd() + [(V, tok_gw2)]
                            + (b_ptl.wr() if gg == 0 else []),
                            inc=[T] if gg == NG - 1 else [])
                        b_zT.note_r(T)
                        b_ptl.note_w(T)
                    S.do("vector", lambda v: v.tensor_copy(
                        out=lg8[:, :], in_=ptl[0:8, :]),
                        wait=b_ptl.rd() + b_lg8.wr(), inc=[V])
                    b_ptl.note_r(V)
                    b_lg8.note_w(V)
                    yield
                    for tb in range(4):
                        S.do("tensor", lambda t_, tb=tb: t_.transpose(
                            ptl[:, tb:tb + 1],
                            rowB[0:1, tb * 128:(tb + 1) * 128],
                            identF[0:1, 0:1]),
                            wait=b_rowB.rd() + (b_ptl.wr() if tb == 0 else []),
                            inc=[T] if tb == 3 else [])
                    b_rowB.note_r(T)
                    b_ptl.note_w(T)
                    S.do("vector", lambda v: v.tensor_copy(
                        out=Ct[:, 0:4], in_=ptl[:, 0:4]),
                        wait=b_ptl.rd() + b_Ct.wr(), inc=[V])
                    b_ptl.note_r(V)
                    b_Ct.note_w(V)
                    yield
                    for tb in range(4):
                        S.do("tensor", lambda t_, tb=tb: t_.transpose(
                            ptl[:, 256 + tb * 8: 256 + (tb + 1) * 8],
                            lg8[:, tb * 128:(tb + 1) * 128],
                            identF[0:8, 0:8]),
                            wait=b_lg8.rd() + (b_ptl.wr() if tb == 0 else []),
                            inc=[T] if tb == 3 else [])
                    b_lg8.note_r(T)
                    b_ptl.note_w(T)
                    S.do("vector", lambda v: v.tensor_copy(
                        out=lgt[:, :], in_=ptl[:, 256:288]),
                        wait=b_ptl.rd() + b_lgt.wr(), inc=[V])
                    b_ptl.note_r(V)
                    b_lgt.note_w(V)
                    yield
                    for tb in range(4):
                        Lg = lgt[:, tb * 8:(tb + 1) * 8]
                        g0 = gt[:, 0:8]
                        top2 = gt[:, 8:16]
                        eq1 = gt[:, 16:24]
                        p1_ = gt[:, 24:32]
                        p2_ = gt[:, 32:40]
                        p3_ = gt[:, 40:48]
                        ffo1 = gt[:, 48:56]
                        eq2 = gt[:, 56:64]
                        ffo2 = gt[:, 64:72]
                        w12 = gt[:, 72:80]

                        def VD(fn, extra=()):
                            S.do("vector", fn,
                                 wait=[(V, V.n), (G, G.n)] + list(extra),
                                 inc=[V])

                        def GM(ap):
                            S.do("gpsimd", lambda g, ap=ap: g.memset(ap, 0.0),
                                 wait=[(V, V.n)], inc=[G])

                        VD(lambda v, tb=tb: v.tensor_scalar_mul(
                            g0, lgt[:, tb * 8:(tb + 1) * 8], Ct[:, tb:tb + 1]),
                           extra=b_Ct.rd() + b_lgt.rd() + b_gt.wr())
                        b_gt.note_w(V)
                        b_Ct.note_r(V)
                        VD(lambda v: v.max(top2, g0))
                        VD(lambda v: v.tensor_single_scalar(
                            eq1, g0, top2[:, 0:1], ALU.is_equal))
                        GM(p1_)
                        VD(lambda v: v.tensor_copy(out=p1_[:, 1:8],
                                                   in_=eq1[:, 0:7]))
                        GM(p2_)
                        VD(lambda v: v.tensor_copy(out=p2_[:, 0:2],
                                                   in_=p1_[:, 0:2]))
                        VD(lambda v: v.tensor_add(p2_[:, 2:8], p1_[:, 2:8],
                                                  p1_[:, 0:6]))
                        GM(p3_)
                        VD(lambda v: v.tensor_copy(out=p3_[:, 0:4],
                                                   in_=p2_[:, 0:4]))
                        VD(lambda v: v.tensor_add(p3_[:, 4:8], p2_[:, 4:8],
                                                  p2_[:, 0:4]))
                        VD(lambda v: v.tensor_single_scalar(
                            ffo1, p3_, 0.0, ALU.is_equal))
                        VD(lambda v: v.tensor_mul(ffo1, eq1, ffo1))
                        VD(lambda v: v.tensor_single_scalar(
                            eq2, g0, top2[:, 1:2], ALU.is_equal))
                        VD(lambda v: v.scalar_tensor_tensor(
                            eq2, ffo1, -1.0, eq2, ALU.mult, ALU.add))
                        VD(lambda v: v.tensor_scalar_max(eq2, eq2, 0.0))
                        GM(p1_)
                        VD(lambda v: v.tensor_copy(out=p1_[:, 1:8],
                                                   in_=eq2[:, 0:7]))
                        GM(p2_)
                        VD(lambda v: v.tensor_copy(out=p2_[:, 0:2],
                                                   in_=p1_[:, 0:2]))
                        VD(lambda v: v.tensor_add(p2_[:, 2:8], p1_[:, 2:8],
                                                  p1_[:, 0:6]))
                        GM(p3_)
                        VD(lambda v: v.tensor_copy(out=p3_[:, 0:4],
                                                   in_=p2_[:, 0:4]))
                        VD(lambda v: v.tensor_add(p3_[:, 4:8], p2_[:, 4:8],
                                                  p2_[:, 0:4]))
                        VD(lambda v: v.tensor_single_scalar(
                            ffo2, p3_, 0.0, ALU.is_equal))
                        VD(lambda v: v.tensor_mul(ffo2, eq2, ffo2))
                        VD(lambda v: v.tensor_sub(w12[:, 2:3], top2[:, 1:2],
                                                  top2[:, 0:1]))
                        S.do("scalar", lambda s: s.activation(
                            w12[:, 3:4], w12[:, 2:3], AF.Exp),
                            wait=[(V, V.n), (A, A.n)], inc=[A])
                        S.do("vector", lambda v: v.tensor_scalar_add(
                            w12[:, 4:5], w12[:, 3:4], 1.0),
                            wait=[(A, A.n)], inc=[V])
                        VD(lambda v: v.reciprocal(w12[:, 5:6], w12[:, 4:5]))
                        VD(lambda v: v.tensor_mul(w12[:, 6:7], w12[:, 3:4],
                                                  w12[:, 5:6]))
                        VD(lambda v: v.tensor_scalar_mul(ffo1, ffo1,
                                                         w12[:, 5:6]))
                        VD(lambda v: v.tensor_scalar_mul(ffo2, ffo2,
                                                         w12[:, 6:7]))
                        VD(lambda v: v.tensor_add(g0, ffo1, ffo2))
                        b_gt.note_w(V)
                        yield
                        for e in range(E):
                            S.do("tensor", lambda t_, e=e: t_.transpose(
                                ptl[0:1, 0:128], g0[:, e:e + 1],
                                identF[:, :]),
                                wait=b_gt.rd() + b_ptl.wr(), inc=[T])
                            b_gt.note_r(T)
                            b_ptl.note_w(T)
                            S.do("vector", lambda v, e=e: v.tensor_copy(
                                out=stage8[0:1, e * 128:(e + 1) * 128],
                                in_=ptl[0:1, 0:128]),
                                wait=b_ptl.rd() + [(STG, b_stage_dma[0])],
                                inc=[V])
                            b_ptl.note_r(V)
                        S.do("gpsimd", lambda g, tb=tb, c=c: g.dma_start(
                            out=wTe_s[:, c * CH + tb * 128:
                                      c * CH + (tb + 1) * 128],
                            in_=stage8[0:1, :].rearrange(
                                "p (e k) -> p e k", e=E)),
                            wait=[(V, V.n)], inc=[STG])
                        b_stage_dma[0] = STG.n
                        yield

                tail = None
                pre_oall = [[], []]
                for c in range(NCH if phases >= 3 else 0):
                    if c == 0:
                        barrier(extra=[(A, tok_cbias)])
                    pre_oall[c % 2] = b_oall[c % 2].wr()
                    for hp in range(NG if 'heads' in p3s else 0):
                        for m in range(NM):
                            sched_sim(c, hp, m)
                            if tail is not None and (hp * NM + m) % 4 == 0:
                                next(tail, None)
                            k = hp * NM + m - SKEW
                            if k >= 0:
                                pm = (k // NM, k % NM)
                                sched_o(c, *pm)
                                if pm[1] == NM - 1:
                                    sched_norm(c, pm[0])
                    for k in range(NG * NM - SKEW, NG * NM):
                        pm = (k // NM, k % NM)
                        sched_o(c, *pm)
                        if pm[1] == NM - 1:
                            sched_norm(c, pm[0])
                    if tail is not None:
                        for _ in tail:
                            pass
                    tail = tail_gen(c)
                if tail is not None:
                    for _ in tail:
                        pass

                if "x2t" in dbg:
                    ap = dbg_tensor("x2t", [NG, 128, SEQ])
                    S.do("sync", lambda s, ap=ap: s.dma_start(out=ap, in_=x2s),
                         wait=[(DS, DS.n), (V, V.n)], inc=[STO])
                if "zt" in dbg:
                    ap = dbg_tensor("zt", [NG, 128, SEQ])
                    S.do("sync", lambda s, ap=ap: s.dma_start(out=ap, in_=zs),
                         wait=[(DS, DS.n), (V, V.n)], inc=[STO])
                if "yt" in dbg:
                    ap = dbg_tensor("yt", [NG, 128, SEQ], BF16)
                    S.do("sync", lambda s, ap=ap: s.dma_start(out=ap, in_=ys),
                         wait=[(DG, DG.n), (V, V.n)], inc=[STO])
                if "wte" in dbg:
                    ap = dbg_tensor("wte", [E, SEQ])
                    S.do("sync", lambda s, ap=ap: s.dma_start(out=ap, in_=wTe_s),
                         wait=[(DG, DG.n)], inc=[STO])

        barrier()
        with ExitStack() as p4c:
            P4 = p4c.enter_context
            ybuf = P4(nc.sbuf_tensor("ybuf", [128, NG * FCH], BF16))
            acc = P4(nc.sbuf_tensor("acc", [128, NG * FCH], F32))
            hbuf = P4(nc.sbuf_tensor("hbuf", [128, 11 * FCH], BF16))
            w1q = [P4(nc.sbuf_tensor(f"w1q{i}", [128, NG * 768], BF16))
                   for i in range(2)]
            w2q = [P4(nc.sbuf_tensor(f"w2q{i}", [128, NG * 768], BF16))
                   for i in range(2)]
            w3h = P4(nc.sbuf_tensor("w3h", [128, 11 * 1024], BF16))
            sil = [P4(nc.sbuf_tensor(f"sil{i}", [128, FCH], F32))
                   for i in range(2)]
            wbb = P4(nc.sbuf_tensor("wbb", [128, FCH], F32))
            tmpc = [P4(nc.sbuf_tensor(f"tmpc{i}", [128, 512], F32))
                    for i in range(2)]
            x2f = [P4(nc.sbuf_tensor(f"x2f{i}", [128, 512], F32))
                   for i in range(2)]
            outb = [P4(nc.sbuf_tensor(f"outb{i}", [128, 1024], F32))
                    for i in range(2)]
            sAB4 = P4(nc.sbuf_tensor("sAB4", [128, 1024], F32))
            r4a = P4(nc.sbuf_tensor("r4a", [1, FCH], F32))
            r4b = P4(nc.sbuf_tensor("r4b", [1, FCH], F32))
            r4c = P4(nc.sbuf_tensor("r4c", [1, FCH], F32))
            sq4 = P4(nc.sbuf_tensor("sq4", [128, FCH], F32))
            wrow = P4(nc.sbuf_tensor("wrow", [1, FCH], F32))

            pup = P4(nc.psum_tensor("pup", [128, 1024], F32))
            pgt = P4(nc.psum_tensor("pgt", [128, 1024], F32))
            pdn = [P4(nc.psum_tensor(f"pdn{i}", [128, 512], F32))
                   for i in range(2)]
            pms = P4(nc.psum_tensor("pms", [128, 1024], F32))

            b_y = Buf()
            b_acc = Buf()
            b_h = Buf()
            b_w1q = [Buf(), Buf()]
            b_w2q = [Buf(), Buf()]
            b_w3h = Buf()
            b_sil = [Buf(), Buf()]
            b_wb = Buf()
            b_tmpc = [Buf(), Buf()]
            b_x2f = [Buf(), Buf()]
            b_outb = [Buf(), Buf()]
            b_sab4 = Buf()
            b_r4a = Buf()
            b_r4b = Buf()
            b_r4c = Buf()
            b_sq4 = Buf()
            b_pup = Buf()
            b_pgt = Buf()
            b_pdn = [Buf(), Buf()]
            b_pms = Buf()
            b_wrow = Buf()

            def q_load(eng, dstbuf, dst_ap, src_ap, waits, bufobj, stream):
                S.do(eng, lambda s, d=dst_ap, i_=src_ap: s.dma_start(out=d, in_=i_),
                     wait=waits + bufobj.wr(), inc=[stream])
                bufobj.note_w(stream)

            for F in range(NF if phases >= 4 else 0):
                S.do("gpsimd", lambda g, F=F: g.dma_start(
                    out=ybuf.ap().rearrange("p (g c) -> p g c", g=NG),
                    in_=ys.rearrange("g p c -> p g c")
                    [:, :, F * FCH:(F + 1) * FCH]),
                    wait=b_y.wr(), inc=[YB])
                b_y.note_w(YB)
                S.do("gpsimd", lambda g, F=F: g.dma_start(
                    out=acc.ap().rearrange("p (g c) -> p g c", g=NG),
                    in_=zs.rearrange("g p c -> p g c")
                    [:, :, F * FCH:(F + 1) * FCH]),
                    wait=b_acc.wr(), inc=[ACC])
                b_acc.note_w(ACC)

                for e in range(E):
                    S.do("sync", lambda s, e=e, F=F: s.dma_start(
                        out=wrow[0:1, :], in_=wTe_s[e, F * FCH:(F + 1) * FCH]),
                        wait=b_wrow.wr(), inc=[WR])
                    b_wrow.note_w(WR)
                    for nsub in range(2):
                        S.do("tensor", lambda t_, nsub=nsub: t_.matmul(
                            pms[:, nsub * 512:(nsub + 1) * 512], ones_1p[:, :],
                            wrow[0:1, nsub * 512:(nsub + 1) * 512],
                            start=True, stop=True),
                            wait=b_wrow.rd()
                            + (b_pms.wr() if nsub == 0 else []) + [(G, tok_ones)],
                            inc=[T] if nsub == 1 else [])
                    b_wrow.note_r(T)
                    b_pms.note_w(T)
                    S.do("scalar", lambda s: s.copy(out=wbb[:, :], in_=pms[:, :]),
                         wait=b_pms.rd() + b_wb.wr(), inc=[A])
                    b_pms.note_r(A)
                    b_wb.note_w(A)

                    for half in range(2):
                        for si, (j0, nj) in enumerate(HSLABS):
                            qb = si
                            qi = half * 2 + si
                            col0 = sum(NG * n * 128 for _, n in
                                       (HSLABS * 2)[:qi])
                            w = NG * nj * 128
                            S.do("gpsimd", lambda g, e=e, col0=col0, w=w,
                                 qb=qb: g.dma_start(
                                     out=w1q[qb][:, 0:w],
                                     in_=w1_in[e][:, col0:col0 + w]),
                                 wait=b_w1q[qb].wr(), inc=[WQ1[qb]])
                            b_w1q[qb].note_w(WQ1[qb])
                            S.do("gpsimd", lambda g, e=e, col0=col0, w=w,
                                 qb=qb: g.dma_start(
                                     out=w2q[qb][:, 0:w],
                                     in_=w2_in[e][:, col0:col0 + w]),
                                 wait=b_w2q[qb].wr(), inc=[WQ2[qb]])
                            b_w2q[qb].note_w(WQ2[qb])
                        S.do("gpsimd", lambda g, e=e, half=half: g.dma_start(
                            out=w3h[:, :],
                            in_=w3_in[e][:, half * 11264:(half + 1) * 11264]),
                            wait=b_w3h.wr(), inc=[W3H])
                        b_w3h.note_w(W3H)

                        pre_h = b_h.wr()
                        for si, (j0, nj) in enumerate(HSLABS):
                            qb = si
                            for jl in range(nj):
                                j_loc = j0 + jl
                                sb = (half * 11 + j_loc) % 2
                                for nsub in range(2):
                                    for gg in range(NG):
                                        S.do("tensor", lambda t_, gg=gg, jl=jl,
                                             nsub=nsub, qb=qb, nj=nj: t_.matmul(
                                                 pup[:, nsub * 512:
                                                     (nsub + 1) * 512],
                                                 w1q[qb][:, (gg * nj + jl) * 128:
                                                         (gg * nj + jl + 1) * 128],
                                                 ybuf[:, gg * FCH + nsub * 512:
                                                      gg * FCH + (nsub + 1) * 512],
                                                 start=(gg == 0),
                                                 stop=(gg == NG - 1)),
                                             wait=b_w1q[qb].rd() + b_y.rd()
                                             + (b_pup.wr()
                                                if (gg == 0 and nsub == 0) else []),
                                             inc=[T] if (gg == NG - 1 and nsub == 1)
                                             else [])
                                b_pup.note_w(T)
                                b_y.note_r(T)
                                b_w1q[qb].note_r(T)
                                for nsub in range(2):
                                    for gg in range(NG):
                                        S.do("tensor", lambda t_, gg=gg, jl=jl,
                                             nsub=nsub, qb=qb, nj=nj: t_.matmul(
                                                 pgt[:, nsub * 512:
                                                     (nsub + 1) * 512],
                                                 w2q[qb][:, (gg * nj + jl) * 128:
                                                         (gg * nj + jl + 1) * 128],
                                                 ybuf[:, gg * FCH + nsub * 512:
                                                      gg * FCH + (nsub + 1) * 512],
                                                 start=(gg == 0),
                                                 stop=(gg == NG - 1)),
                                             wait=b_w2q[qb].rd()
                                             + (b_pgt.wr()
                                                if (gg == 0 and nsub == 0) else []),
                                             inc=[T] if (gg == NG - 1 and nsub == 1)
                                             else [])
                                b_pgt.note_w(T)
                                b_w2q[qb].note_r(T)
                                S.do("scalar", lambda s, sb=sb: s.activation(
                                    sil[sb][:, :], pup[:, :], AF.Silu),
                                    wait=b_pup.rd() + b_sil[sb].wr(), inc=[A])
                                b_pup.note_r(A)
                                b_sil[sb].note_w(A)
                                S.do("vector", lambda v, j_loc=j_loc, sb=sb:
                                     v.tensor_mul(
                                         hbuf[:, j_loc * FCH:(j_loc + 1) * FCH],
                                         sil[sb][:, :], pgt[:, :]),
                                     wait=b_sil[sb].rd() + b_pgt.rd() + pre_h,
                                     inc=[V])
                                b_sil[sb].note_r(V)
                                b_pgt.note_r(V)
                                b_h.note_w(V)
                        tok_h = V.n
                        for i in range(NG):
                            for nsub in range(2):
                                db = (i * 2 + nsub) % 2
                                for jl in range(11):
                                    S.do("tensor", lambda t_, i=i, jl=jl,
                                         nsub=nsub, db=db: t_.matmul(
                                             pdn[db][:, :],
                                             w3h[:, jl * 1024 + i * 128:
                                                 jl * 1024 + (i + 1) * 128],
                                             hbuf[:, jl * FCH + nsub * 512:
                                                  jl * FCH + (nsub + 1) * 512],
                                             start=(jl == 0), stop=(jl == 10)),
                                         wait=b_w3h.rd() + [(V, tok_h)]
                                         + (b_pdn[db].wr() if jl == 0 else []),
                                         inc=[T] if jl == 10 else [])
                                b_pdn[db].note_w(T)
                                b_h.note_r(T)
                                b_w3h.note_r(T)
                                tb_ = db
                                S.do("vector", lambda v, db=db, tb_=tb_, nsub=nsub:
                                     v.tensor_mul(
                                         tmpc[tb_][:, :], pdn[db][:, :],
                                         wbb[:, nsub * 512:(nsub + 1) * 512]),
                                     wait=b_pdn[db].rd() + b_wb.rd()
                                     + b_tmpc[tb_].wr(), inc=[V])
                                b_pdn[db].note_r(V)
                                b_wb.note_r(V)
                                b_tmpc[tb_].note_w(V)
                                S.do("vector", lambda v, i=i, nsub=nsub, tb_=tb_:
                                     v.tensor_add(
                                         acc[:, i * FCH + nsub * 512:
                                             i * FCH + (nsub + 1) * 512],
                                         acc[:, i * FCH + nsub * 512:
                                             i * FCH + (nsub + 1) * 512],
                                         tmpc[tb_][:, :]),
                                     wait=b_tmpc[tb_].rd() + b_acc.rd()
                                     + [(V, V.n)], inc=[V])
                                b_tmpc[tb_].note_r(V)
                                b_acc.note_w(V)

                for i in range(NG):
                    S.do("scalar", lambda s, i=i: s.square(
                        sq4[:, :], acc[:, i * FCH:(i + 1) * FCH]),
                        wait=b_acc.rd() + b_sq4.wr(), inc=[A])
                    b_sq4.note_w(A)
                    for nsub in range(2):
                        S.do("tensor", lambda t_, i=i, nsub=nsub: t_.matmul(
                            pms[0:1, nsub * 512:(nsub + 1) * 512], ones_p1[:, :],
                            acc[:, i * FCH + nsub * 512:
                                i * FCH + (nsub + 1) * 512],
                            start=(i == 0), stop=(i == NG - 1)),
                            wait=b_acc.rd()
                            + (b_pms.wr() if (i == 0 and nsub == 0) else []),
                            inc=[T] if nsub == 1 else [])
                    b_pms.note_w(T)
                    b_acc.note_r(T, A)
                    for nsub in range(2):
                        S.do("tensor", lambda t_, i=i, nsub=nsub: t_.matmul(
                            pdn[nsub][0:1, :], ones_p1[:, :],
                            sq4[:, nsub * 512:(nsub + 1) * 512],
                            start=(i == 0), stop=(i == NG - 1)),
                            wait=b_sq4.rd()
                            + (b_pdn[nsub].wr() if i == 0 else []),
                            inc=[T] if nsub == 1 else [])
                        b_pdn[nsub].note_w(T)
                    b_sq4.note_r(T)
                S.do("vector", lambda v: v.tensor_scalar_mul(
                    r4a[0:1, :], pms[0:1, :], 1.0 / D),
                    wait=b_pms.rd() + b_r4a.wr(), inc=[V])
                b_pms.note_r(V)
                b_r4a.note_w(V)
                S.do("vector", lambda v: v.tensor_scalar_mul(
                    r4b[0:1, 0:512], pdn[0][0:1, :], 1.0 / D),
                    wait=b_pdn[0].rd() + b_r4b.wr(), inc=[V])
                S.do("vector", lambda v: v.tensor_scalar_mul(
                    r4b[0:1, 512:1024], pdn[1][0:1, :], 1.0 / D),
                    wait=b_pdn[1].rd() + [(V, V.n)], inc=[V])
                b_pdn[0].note_r(V)
                b_pdn[1].note_r(V)
                b_r4b.note_w(V)
                S.do("vector", lambda v: v.scalar_tensor_tensor(
                    r4c[0:1, :], r4a[0:1, :], 1.0, r4a[0:1, :],
                    ALU.mult, ALU.mult),
                    wait=b_r4a.rd() + b_r4c.wr() + [(V, V.n)], inc=[V])
                b_r4c.note_w(V)
                S.do("vector", lambda v: v.tensor_sub(
                    r4b[0:1, :], r4b[0:1, :], r4c[0:1, :]),
                    wait=b_r4c.rd() + [(V, V.n)], inc=[V])
                b_r4c.note_r(V)
                b_r4b.note_w(V)
                S.do("scalar", lambda s: s.activation(
                    r4c[0:1, :], r4b[0:1, :], AF.Ln, bias=cst[0:1, 0:1]),
                    wait=b_r4b.rd() + b_r4c.wr(), inc=[A])
                b_r4b.note_r(A)
                b_r4c.note_w(A)
                S.do("scalar", lambda s: s.activation(
                    r4b[0:1, :], r4c[0:1, :], AF.Exp, scale=-0.5),
                    wait=b_r4c.rd() + b_r4b.wr() + [(A, A.n)], inc=[A])
                b_r4c.note_r(A)
                b_r4b.note_w(A)
                S.do("vector", lambda v: v.scalar_tensor_tensor(
                    r4c[0:1, :], r4a[0:1, :], -1.0, r4b[0:1, :],
                    ALU.mult, ALU.mult),
                    wait=b_r4b.rd() + b_r4a.rd() + b_r4c.wr(), inc=[V])
                b_r4a.note_r(V)
                b_r4b.note_r(V)
                b_r4c.note_w(V)

                for nsub in range(2):
                    S.do("tensor", lambda t_, nsub=nsub: t_.matmul(
                        pms[:, 0:512], ones_1p[:, :],
                        r4b[0:1, nsub * 512:(nsub + 1) * 512],
                        start=True, stop=True),
                        wait=b_r4b.rd() + b_r4c.rd() + b_pms.wr())
                    S.do("tensor", lambda t_, nsub=nsub: t_.matmul(
                        pms[:, 512:1024], ones_1p[:, :],
                        r4c[0:1, nsub * 512:(nsub + 1) * 512],
                        start=True, stop=True), inc=[T])
                    b_r4b.note_r(T)
                    b_r4c.note_r(T)
                    b_pms.note_w(T)
                    S.do("scalar", lambda s: s.copy(out=sAB4[:, :], in_=pms[:, :]),
                         wait=b_pms.rd() + b_sab4.wr(), inc=[A])
                    b_pms.note_r(A)
                    b_sab4.note_w(A)
                    x2tok = [0] * NG

                    def x2_load(i, F=F, nsub=nsub):
                        xb = i % 2
                        S.do("gpsimd", lambda g, i=i, xb=xb: g.dma_start(
                            out=x2f[xb][:, :],
                            in_=x2s[i][:, F * FCH + nsub * 512:
                                       F * FCH + (nsub + 1) * 512]),
                            wait=b_x2f[xb].wr(), inc=[XF[xb]])
                        b_x2f[xb].note_w(XF[xb])
                        x2tok[i] = (XF[xb], XF[xb].n)

                    x2_load(0)
                    x2_load(1)
                    for i in range(NG):
                        xb = i % 2
                        S.do("vector", lambda v, i=i, nsub=nsub: v.tensor_mul(
                            tmpc[0][:, :],
                            acc[:, i * FCH + nsub * 512:
                                i * FCH + (nsub + 1) * 512],
                            sAB4[:, 0:512]),
                            wait=b_sab4.rd() + b_tmpc[0].wr() + b_acc.rd(),
                            inc=[V])
                        b_sab4.note_r(V)
                        b_tmpc[0].note_w(V)
                        S.do("vector", lambda v: v.tensor_add(
                            tmpc[1][:, :], tmpc[0][:, :], sAB4[:, 512:1024]),
                            wait=b_tmpc[0].rd() + b_tmpc[1].wr(), inc=[V])
                        b_tmpc[0].note_r(V)
                        b_tmpc[1].note_w(V)
                        S.do("vector", lambda v, i=i, nsub=nsub, xb=xb: v.tensor_add(
                            acc[:, i * FCH + nsub * 512:
                                i * FCH + (nsub + 1) * 512],
                            tmpc[1][:, :], x2f[xb][:, :]),
                            wait=b_tmpc[1].rd() + [x2tok[i], (V, V.n)],
                            inc=[V])
                        b_tmpc[1].note_r(V)
                        b_x2f[xb].note_r(V)
                        b_acc.note_w(V)
                        if i + 2 < NG:
                            x2_load(i + 2)
                    for tt in range(4):
                        ob = tt % 2
                        t_abs = F * FCH + nsub * 512 + tt * 128
                        for i in range(NG):
                            S.do("tensor", lambda t_, i=i, tt=tt, nsub=nsub:
                                 t_.transpose(
                                     pms[:, i * 128:(i + 1) * 128],
                                     acc[:, i * FCH + nsub * 512 + tt * 128:
                                         i * FCH + nsub * 512 + (tt + 1) * 128],
                                     identF[:, :]),
                                 wait=b_acc.rd()
                                 + (b_pms.wr() if i == 0 else []),
                                 inc=[T] if i == NG - 1 else [])
                        b_pms.note_w(T)
                        b_acc.note_r(T)
                        S.do("scalar", lambda s, ob=ob: s.copy(
                            out=outb[ob][:, :], in_=pms[:, :]),
                            wait=b_pms.rd() + b_outb[ob].wr(), inc=[A])
                        b_pms.note_r(A)
                        b_outb[ob].note_w(A)
                        S.do("gpsimd", lambda g, t_abs=t_abs, ob=ob: g.dma_start(
                            out=out[t_abs:t_abs + 128, :], in_=outb[ob][:, :]),
                            wait=b_outb[ob].rd(), inc=[OB[ob]])
                        b_outb[ob].note_r(OB[ob])

        S.do("sync", wait=[(s, s.n) for s in ALLS] + [(STO, STO.n)])

        with nc.Block() as blk:
            S.emit_into(blk)

    return nc, dbg_aps


_NC_CACHE = {}


def _get_nc(dbg=()):
    key = tuple(sorted(dbg))
    if key not in _NC_CACHE:
        _NC_CACHE[key] = build_nc(dbg)
    return _NC_CACHE[key]


def prep_inputs(L0_attn_x, L0_attn_query_proj, L0_attn_key_proj,
                L0_attn_value_proj, L0_attn_output_proj, L0_ffn_gate_w,
                L0_ffn_rms_w, L0_ffn_W1, L0_ffn_W2, L0_ffn_W3):
    x = np.ascontiguousarray(np.asarray(L0_attn_x, dtype=np.float32))
    def sbuf_layout(a):
        c = a.shape[1]
        return np.ascontiguousarray(
            a.reshape(NG, 128, c).transpose(1, 0, 2).reshape(128, NG * c))

    wq = sbuf_layout(np.asarray(L0_attn_query_proj, np.float32)
                     .transpose(2, 0, 1).reshape(D, H * KD))
    wk = sbuf_layout(np.asarray(L0_attn_key_proj, np.float32))
    wv = sbuf_layout(np.asarray(L0_attn_value_proj, np.float32))
    wo = sbuf_layout(np.asarray(L0_attn_output_proj, np.float32)
                     .transpose(1, 2, 0).reshape(H * KD, D))
    gw = np.ascontiguousarray(
        np.asarray(L0_ffn_gate_w, np.float32).T.reshape(NG, 128, E)
        .transpose(1, 0, 2).reshape(128, NG * E))
    rms = np.ascontiguousarray(
        np.asarray(L0_ffn_rms_w, np.float32).reshape(NG, 128).T)
    qrs = []
    for half in range(2):
        for j0, nj in HSLABS:
            qrs.append((half * 1408 + j0 * 128, half * 1408 + (j0 + nj) * 128))

    def slab12(W):
        out = np.empty((E, 128, NG * DE), np.float32)
        for e in range(E):
            v = np.asarray(W[e], np.float32).T.reshape(NG, 128, DE) \
                .transpose(1, 0, 2)
            out[e] = np.concatenate(
                [v[:, :, a:b].reshape(128, NG * (b - a)) for a, b in qrs],
                axis=1)
        return out

    def slab3(W):
        out = np.empty((E, 128, 22 * 1024), np.float32)
        for e in range(E):
            out[e] = np.asarray(W[e], np.float32).T.reshape(22, 128, 1024) \
                .transpose(1, 0, 2).reshape(128, 22 * 1024)
        return out

    w1t = slab12(L0_ffn_W1)
    w2t = slab12(L0_ffn_W2)
    w3t = slab3(L0_ffn_W3)
    in_maps = []
    for c in range(N_CORES):
        in_maps.append(dict(x=np.ascontiguousarray(x[c]), wq=wq, wk=wk, wv=wv,
                            wo=wo, gw=gw, rms=rms, w1t=w1t, w2t=w2t, w3t=w3t))
    return in_maps


def kernel(**inputs):
    in_maps = prep_inputs(**inputs)
    nc, _ = _get_nc()
    res = run_bass_kernel_spmd(nc, in_maps, core_ids=list(range(N_CORES)))
    return np.stack([res.results[c]["out"] for c in range(N_CORES)], axis=0)
